# version 07
# baseline (speedup 1.0000x reference)
import functools
import numpy as np

import concourse.bass as bass
import concourse.mybir as mybir
from concourse.bass_utils import run_bass_kernel_spmd

F32 = mybir.dt.float32
BF16 = mybir.dt.bfloat16
AF = mybir.ActivationFunctionType
ALU = mybir.AluOpType
AX = mybir.AxisListType.X

SEQ, D, HID, NB = 8192, 1024, 4096, 512
JS = 2048
NSUP, SUP = 8, 1024
B_SHIFT = 30.0
DC = D // 128
TC = JS // 128
XC = SEQ // 128
MK = dict(skip_group_check=True)


class StopPhase(Exception):
    pass


def _ts(i, n=128):
    return slice(i * n, (i + 1) * n)


def build_nc(debug=False, upto=5, p4stage=9):
    nc = bass.Bass("TRN2", target_bir_lowering=False)

    x_d = nc.dram_tensor("x", [SEQ, D], F32, kind="ExternalInput").ap()
    xj_d = nc.dram_tensor("xj", [JS, D], F32, kind="ExternalInput").ap()
    wq_d = nc.dram_tensor("wq", [D, D], F32, kind="ExternalInput").ap()
    wk_d = nc.dram_tensor("wk", [D, D], F32, kind="ExternalInput").ap()
    wv_d = nc.dram_tensor("wv", [D, D], F32, kind="ExternalInput").ap()
    w1t_d = nc.dram_tensor("w1t", [D, HID], F32, kind="ExternalInput").ap()
    w2t_d = nc.dram_tensor("w2t", [HID, D], F32, kind="ExternalInput").ap()
    rwt_d = nc.dram_tensor("rwt", [D, NB], F32, kind="ExternalInput").ap()
    ut_d = nc.dram_tensor("ut", [D, NB], F32, kind="ExternalInput").ap()
    u_d = nc.dram_tensor("u", [NB, D], F32, kind="ExternalInput").ap()
    v_d = nc.dram_tensor("v", [NB, D], F32, kind="ExternalInput").ap()
    lng_d = nc.dram_tensor("lng", [DC, 128, 1], F32, kind="ExternalInput").ap()
    lnb_d = nc.dram_tensor("lnb", [DC, 128, 1], F32, kind="ExternalInput").ap()
    rb_d = nc.dram_tensor("rb", [1, NB], F32, kind="ExternalInput").ap()
    gam_d = nc.dram_tensor("gamma", [1, D], F32, kind="ExternalInput").ap()
    id_d = nc.dram_tensor("ident", [128, 128], F32, kind="ExternalInput").ap()
    y_d = nc.dram_tensor("y", [JS, D], F32, kind="ExternalOutput").ap()

    ikind = "ExternalOutput" if debug else "Internal"
    qbuf_d = nc.dram_tensor("qbuf", [NSUP, DC, 128, SUP], BF16, kind=ikind).ap()
    vbuf_d = nc.dram_tensor("vbuf", [XC, 128, D], BF16, kind=ikind).ap()
    x2buf_d = nc.dram_tensor("x2buf", [TC, 128, D], F32, kind=ikind).ap()
    if debug:
        odbg_d = nc.dram_tensor("odbg", [TC, 128, D], F32, kind="ExternalOutput").ap()
        ldbg_d = nc.dram_tensor("ldbg", [128, TC], F32, kind="ExternalOutput").ap()
        kdbg_d = nc.dram_tensor("kdbg", [DC, 128, JS], BF16, kind="ExternalOutput").ap()
        adbg_d = nc.dram_tensor("adbg", [TC, 128, NB], F32, kind="ExternalOutput").ap()
        wdbg_d = nc.dram_tensor("wdbg", [4, 128, JS], BF16, kind="ExternalOutput").ap()
        sdbg_d = nc.dram_tensor("sdbg", [TC, 128, D], F32, kind="ExternalOutput").ap()
        xsdbg_d = nc.dram_tensor("xsdbg", [128, 8192], BF16, kind="ExternalOutput").ap()
        stdbg_d = nc.dram_tensor("stdbg", [128, 8], F32, kind="ExternalOutput").ap()
        xfdbg_d = nc.dram_tensor("xfdbg", [128, 16384], BF16, kind="ExternalOutput").ap()
        zfdbg_d = nc.dram_tensor("zfdbg", [128, 16384], BF16, kind="ExternalOutput").ap()
        wqdbg_d = nc.dram_tensor("wqdbg", [128, 8192], BF16, kind="ExternalOutput").ap()

    BB = nc.alloc_sbuf_tensor("BB", [128, 49152], BF16).ap()
    FF = nc.alloc_sbuf_tensor("FF", [128, 16384], F32).ap()
    stg = nc.alloc_sbuf_tensor("stg", [128, 6144], F32).ap()
    junk = nc.alloc_sbuf_tensor("junkb", [128, 1024], BF16).ap()

    A = nc.alloc_sbuf_tensor
    ident_b = A("ident_b", [128, 128], BF16).ap()
    ones_b = A("ones_b", [128, 1], BF16).ap()
    ones1f = A("ones1f", [1, 128], F32).ap()
    negB = A("negB", [128, 1], F32).ap()
    zerob = A("zerob", [128, 1], F32).ap()
    eps5 = A("eps5", [128, 1], F32).ap()
    onesf = A("onesf", [128, 1], F32).ap()
    eps12 = A("eps12", [128, 1], F32).ap()
    g_t = A("g_t", [128, DC], F32).ap()
    b_t = A("b_t", [128, DC], F32).ap()
    su2 = A("su2", [128, 4], F32).ap()
    su_t = A("su_t", [128, 4], F32).ap()
    sv2 = A("sv2", [128, 4], F32).ap()
    sv_t = A("sv_t", [128, 4], F32).ap()
    SSt = A("SSt", [128, 16], F32).ap()
    SS2 = A("SS2", [128, 16], F32).ap()
    fac = A("fac", [128, 16], F32).ap()
    l_sb = A("l_sb", [128, 16], F32).ap()
    top8 = A("top8", [128, 8], F32).ap()
    rb_row = A("rb_row", [1, NB], F32).ap()
    brow = A("brow", [1, NB], F32).ap()
    grow = A("grow", [1, D], F32).ap()
    sumx = A("sumx", [128, 1], F32).ap()
    sumsq = A("sumsq", [128, 1], F32).ap()
    mean = A("mean", [128, 1], F32).ap()
    var = A("var", [128, 1], F32).ap()
    rstd = A("rstd", [128, 1], F32).ap()
    msq = A("msq", [128, 1], F32).ap()
    lt = A("lt", [128, 1], F32).ap()
    lt1 = A("lt1", [128, 1], F32).ap()
    vtmp = A("vtmp", [128, 1], F32).ap()
    linv = A("linv", [128, 16], F32).ap()
    stdbg = A("stdbg_s", [128, 8], F32).ap()

    wqb = BB[:, 0:8192]
    wkb = BB[:, 8192:16384]
    wvb = BB[:, 16384:24576]
    xsup = [BB[:, 24576:32768], BB[:, 32768:40960]]
    xln = [BB[:, 40960:41984], BB[:, 41984:43008]]
    qsb = [BB[:, 43008:44032], BB[:, 44032:45056]]
    xjf = BB[:, 24576:40960]
    kq = [BB[:, 16384:24576], BB[:, 40960:49152]]

    def kfm(e):
        return kq[e // 4][:, (e % 4) * 2048:(e % 4) * 2048 + 2048]

    qstream = BB[:, 0:8192]
    vstream = BB[:, 8192:16384]
    ppair = [BB[:, 24576:25088], BB[:, 25088:25600]]
    xf_fm = BB[:, 0:16384]
    z_fm = BB[:, 16384:32768]
    xf_tm = BB[:, 40960:41984]
    z_tm = BB[:, 41984:43008]
    rw_b = BB[:, 43008:47104]
    w_tm = BB[:, 32768:40960]
    w_fmr = BB[:, 40960:49152]
    wh_r = BB[:, 16384:24576]
    ut_b = [BB[:, 24576:25600], BB[:, 25600:26624]]
    w1sb2 = [BB[:, 24576:28672], BB[:, 28672:32768]]
    w2sb2 = [BB[:, 32768:36864], BB[:, 36864:40960]]
    h1sb = BB[:, 43008:47104]
    vpp = BB[:, 0:4096]

    O_sb = FF[:, 0:16384]
    wmask = FF[:, 0:8192]
    mge = FF[:, 8192:8704]
    static = FF[:, 0:16384]

    xstg = [stg[:, 0:1024], stg[:, 1024:2048]]
    xres = [stg[:, 0:1024], stg[:, 1024:2048]]
    x2sb = [stg[:, 2048:3072], stg[:, 3072:4096]]
    xf32 = stg[:, 4096:5120]
    gbc = stg[:, 0:1024]
    wstage = [stg[:, 1024:2048], stg[:, 2048:3072]]
    bbc = stg[:, 3072:3584]
    clipb = [stg[:, 3584:4096], stg[:, 4096:4608]]
    alphab = [stg[:, 4608:5120], stg[:, 5120:5632]]
    utstg = [stg[:, 3584:3712], stg[:, 3712:3840]]
    fsb = stg[:, 4608:5632]
    ysb = [stg[:, 2048:3072], stg[:, 3072:4096]]

    P = nc.alloc_psum_tensor
    p0 = P("p0", [128, 512], F32).ap()
    p1 = P("p1", [128, 512], F32).ap()
    p2 = P("p2", [128, 512], F32).ap()
    p3 = P("p3", [128, 512], F32).ap()
    p4 = P("p4", [128, 512], F32).ap()
    p5 = P("p5", [128, 512], F32).ap()
    p6 = P("p6", [128, 512], F32).ap()
    p_t = P("p_t", [128, 1024], BF16).ap()
    p_t2 = p4.bitcast(BF16)

    def ptb(gi):
        return p_t if gi % 2 == 0 else p_t2

    def stats_seq(vector):
        vector.tensor_scalar(mean[:, :], sumx[:, :], 1.0 / D, None, op0=ALU.mult)
        vector.tensor_scalar(vtmp[:, :], sumsq[:, :], 1.0 / D, None, op0=ALU.mult)
        vector.tensor_tensor(msq[:, :], mean[:, :], mean[:, :], op=ALU.mult)
        vector.drain()
        return vector.tensor_tensor(var[:, :], vtmp[:, :], msq[:, :], op=ALU.subtract)

    sem_names = [
        "s0a", "s0b", "s0i", "x_a", "x_b", "kx_a", "kx_b", "sp_a", "sp_b",
        "qv_a", "qv_b", "x3_a", "x3_b", "sp3_a", "sp3_b", "ra_a", "ra_b",
        "du_a", "du_b", "dut_a", "dut_b", "ee_a", "ee_b", "fv_a", "fv_b",
        "fx_a", "fx_b", "fs_a", "fs_b",
        "s0_ld", "s0_v",
        "s1_x", "s1_sq", "s1_var", "s1_rstd", "s1_app", "s1_tr", "s1_ev",
        "s1_pmm", "s1_pev", "s1_sp",
        "k_x", "k_sq", "k_var", "k_rstd", "k_app", "k_tr", "k_ev", "k_mm", "k_kev",
        "s2_qv", "s2_smm", "s2_sp", "s2_pv", "s2_fold",
        "s3_x", "s3_s", "s3_v", "s3_tr", "s3_sp",
        "a_ld", "a_te", "a_ve",
        "r_mm", "r_ev", "r_sp", "r_tk",
        "c_w", "c_wt", "c_wf",
        "d_ld", "d_sq", "d_utc", "d_h", "d_wh",
        "e_ld", "e_cast", "e_m1", "e_g", "e_m2", "e_f",
        "f_ld", "f_sq", "f_vpp", "f_dyn", "f_ve", "f_sc", "f_x2", "f_st",
        "dbg",
    ]
    SEM = {n: nc.alloc_semaphore(name=n) for n in sem_names}
    lo = min(s.num for s in SEM.values())
    hi = max(s.num for s in SEM.values())
    nc.gpsimd.sem_clear(range(lo, hi + 1))
    nc.all_engine_barrier()

    with nc.Block("p0") as block:
        @block.sync
        def _(sync):
            sync.dma_start(out=stg[:, 5632:5760], in_=id_d[:, :]).then_inc(SEM["s0i"], 16)
            for wi, wd in enumerate((wq_d, wk_d, wv_d)):
                for c in range(DC):
                    gi = wi * DC + c
                    if gi >= 2:
                        sync.wait_ge(SEM["s0_v"], gi - 1)
                    sync.dma_start(out=xstg[gi % 2], in_=wd[_ts(c), :]
                                   ).then_inc(SEM["s0a" if gi % 2 == 0 else "s0b"], 16)

        @block.scalar
        def _(scalar):
            for fn in (AF.Exp, AF.Ln, AF.Gelu, AF.Tanh, AF.Square):
                scalar.activation(out=junk[0:1, 0:1], in_=junk[0:1, 0:1], func=fn,
                                  bias=zerob[0:1, :], scale=1.0)
                scalar.drain()

        @block.vector
        def _(vector):
            vector.memset(ones_b[:, :], 1.0)
            vector.memset(ones1f[:, :], 1.0)
            vector.memset(negB[:, :], -B_SHIFT)
            vector.memset(zerob[:, :], 0.0)
            vector.memset(eps5[:, :], 1e-5)
            vector.memset(onesf[:, :], 1.0)
            vector.memset(eps12[:, :], 1e-12)
            vector.wait_ge(SEM["s0i"], 16)
            vector.tensor_copy(ident_b[:, :], stg[:, 5632:5760])
            for wi in range(3):
                dst = (wqb, wkb, wvb)[wi]
                for c in range(DC):
                    gi = wi * DC + c
                    vector.wait_ge(SEM["s0a" if gi % 2 == 0 else "s0b"], 16 * (gi // 2 + 1))
                    vector.tensor_copy(dst[:, c * 1024:(c + 1) * 1024],
                                       xstg[gi % 2]).then_inc(SEM["s0_v"], 1)

    def emit_ln_sync(sync, n, src_d, sx, sapp):
        sxa, sxb = sx
        for gi in range(n):
            if gi >= 2:
                sync.wait_ge(SEM[sapp], gi - 1)
            sync.dma_start(out=xstg[gi % 2], in_=src_d[_ts(gi), :]
                           ).then_inc(SEM[sxa if gi % 2 == 0 else sxb], 16)

    def emit_ln_vector(vector, n, sx, ssq, svar, srstd, sapp, str_, sev, dst_of):
        sxa, sxb = sx
        for gi in range(n):
            xc = xstg[gi % 2]
            vector.wait_ge(SEM[sxa if gi % 2 == 0 else sxb], 16 * (gi // 2 + 1))
            vector.reduce_sum(sumx[:, :], xc, axis=AX)
            vector.drain()
            vector.wait_ge(SEM[ssq], gi + 1)
            stats_seq(vector).then_inc(SEM[svar], 1)
            vector.wait_ge(SEM[srstd], gi + 1)
            vector.tensor_scalar(xln[gi % 2], xc, mean[:, :], rstd[:, :],
                                 op0=ALU.subtract, op1=ALU.mult).then_inc(SEM[sapp], 1)
            vector.wait_ge(SEM[str_], gi + 1)
            vector.tensor_copy(dst_of(gi), ptb(gi).rearrange("p (d t) -> p d t", d=DC)
                               ).then_inc(SEM[sev], 1)

    def emit_ln_scalar(scalar, n, sx, ssq, svar, srstd):
        sxa, sxb = sx
        for gi in range(n):
            xc = xstg[gi % 2]
            scalar.wait_ge(SEM[sxa if gi % 2 == 0 else sxb], 16 * (gi // 2 + 1))
            scalar.activation(out=junk[:, :], in_=xc, func=AF.Square,
                              accum_out=sumsq[:, :]).then_inc(SEM[ssq], 1)
            scalar.wait_ge(SEM[svar], gi + 1)
            scalar.activation(out=rstd[:, :], in_=var[:, :], func=AF.Ln, bias=eps5[:, :], scale=1.0)
            scalar.drain()
            scalar.activation(out=rstd[:, :], in_=rstd[:, :], func=AF.Exp, bias=zerob[:, :],
                              scale=-0.5).then_inc(SEM[srstd], 1)

    def emit_tr_chunk(tensor, gi, sapp, sev, str_, extra_wait=None):
        tensor.wait_ge(SEM[sapp], gi + 1)
        if gi >= 2:
            tensor.wait_ge(SEM[sev], gi - 1)
        if extra_wait is not None:
            extra_wait(tensor, gi)
        for dcx in range(DC):
            ins = tensor.transpose(ptb(gi)[:, dcx * 128:(dcx + 1) * 128],
                                   xln[gi % 2][:, dcx * 128:(dcx + 1) * 128],
                                   ident_b[:, :])
            if dcx == DC - 1:
                ins.then_inc(SEM[str_], 1)

    if upto < 1:
        return nc
    with nc.Block("p1") as block:
        @block.sync
        def _(sync):
            emit_ln_sync(sync, XC, x_d, ("x_a", "x_b"), "s1_app")

        @block.scalar
        def _(scalar):
            emit_ln_scalar(scalar, XC, ("x_a", "x_b"), "s1_sq", "s1_var", "s1_rstd")

        @block.tensor
        def _(tensor):
            pg = 0
            for c in range(8):
                emit_tr_chunk(tensor, c, "s1_app", "s1_ev", "s1_tr")
            for s in range(NSUP):
                def maybe_trans(k, s_=s):
                    if s_ + 1 < NSUP and k % 2 == 1:
                        c2 = k // 2
                        emit_tr_chunk(tensor, (s_ + 1) * 8 + c2, "s1_app", "s1_ev", "s1_tr")
                tensor.wait_ge(SEM["s1_ev"], (s + 1) * 8)
                for e in range(DC):
                    pp = (p0, p1) if pg % 2 == 0 else (p2, p3)
                    if pg >= 2:
                        tensor.wait_ge(SEM["s1_pev"], pg - 1)
                    for dcx in range(DC):
                        for th in range(2):
                            ins = tensor.matmul(
                                pp[th][:, :],
                                wqb[:, dcx * 1024 + e * 128: dcx * 1024 + (e + 1) * 128],
                                xsup[s % 2][:, dcx * 1024 + th * 512: dcx * 1024 + th * 512 + 512],
                                start=(dcx == 0), stop=(dcx == DC - 1), **MK)
                            if dcx == DC - 1 and th == 1:
                                ins.then_inc(SEM["s1_pmm"], 1)
                    maybe_trans(pg % 16)
                    pg += 1
                for c in range(8):
                    pp = (p0, p1) if pg % 2 == 0 else (p2, p3)
                    if pg >= 2:
                        tensor.wait_ge(SEM["s1_pev"], pg - 1)
                    for dcx in range(DC):
                        for eh in range(2):
                            ins = tensor.matmul(
                                pp[eh][:, :],
                                xsup[s % 2][:, dcx * 1024 + c * 128: dcx * 1024 + (c + 1) * 128],
                                wvb[:, dcx * 1024 + eh * 512: dcx * 1024 + eh * 512 + 512],
                                start=(dcx == 0), stop=(dcx == DC - 1), **MK)
                            if dcx == DC - 1 and eh == 1:
                                ins.then_inc(SEM["s1_pmm"], 1)
                    maybe_trans(pg % 16)
                    pg += 1

        @block.vector
        def _(vector):
            def ln_chunk(s, c):
                gi = s * 8 + c
                xc = xstg[gi % 2]
                vector.wait_ge(SEM["x_a" if gi % 2 == 0 else "x_b"], 16 * (gi // 2 + 1))
                vector.reduce_sum(sumx[:, :], xc, axis=AX)
                vector.drain()
                vector.wait_ge(SEM["s1_sq"], gi + 1)
                stats_seq(vector).then_inc(SEM["s1_var"], 1)
                vector.wait_ge(SEM["s1_rstd"], gi + 1)
                vector.tensor_scalar(xln[gi % 2], xc, mean[:, :], rstd[:, :],
                                     op0=ALU.subtract, op1=ALU.mult).then_inc(SEM["s1_app"], 1)
                vector.wait_ge(SEM["s1_tr"], gi + 1)
                if s >= 2 and c == 0:
                    vector.wait_ge(SEM["s1_pmm"], 16 * (s - 1))
                dst = xsup[s % 2].rearrange("p (d t) -> p d t", d=DC)[:, :, c * 128:(c + 1) * 128]
                vector.tensor_copy(dst, ptb(gi).rearrange("p (d t) -> p d t", d=DC)
                                   ).then_inc(SEM["s1_ev"], 1)

            for c in range(8):
                ln_chunk(0, c)
            for s in range(NSUP):
                for k in range(16):
                    pg = s * 16 + k
                    pp = (p0, p1) if pg % 2 == 0 else (p2, p3)
                    vector.wait_ge(SEM["s1_pmm"], pg + 1)
                    if pg >= 2:
                        vector.wait_ge(SEM["sp_a" if pg % 2 == 0 else "sp_b"], 16 * (pg // 2))
                    vector.tensor_copy(qsb[pg % 2][:, 0:512], pp[0][:, :])
                    vector.tensor_copy(qsb[pg % 2][:, 512:1024], pp[1][:, :]
                                       ).then_inc(SEM["s1_pev"], 1)
                    if s + 1 < NSUP and k % 2 == 1:
                        ln_chunk(s + 1, k // 2)

        @block.gpsimd
        def _(gpsimd):
            if debug:
                gpsimd.wait_ge(SEM["s1_ev"], 8)
                gpsimd.dma_start(out=xsdbg_d[:, :], in_=xsup[0]).then_inc(SEM["dbg"], 16)
                gpsimd.dma_start(out=stdbg_d[:, :], in_=stdbg[:, :]).then_inc(SEM["dbg"], 16)
                gpsimd.dma_start(out=wqdbg_d[:, :], in_=wqb).then_inc(SEM["dbg"], 16)
            for s in range(NSUP):
                for e in range(DC):
                    pg = s * 16 + e
                    gpsimd.wait_ge(SEM["s1_pev"], pg + 1)
                    gpsimd.dma_start(out=qbuf_d[s, e, :, :], in_=qsb[pg % 2]
                                     ).then_inc(SEM["sp_a" if pg % 2 == 0 else "sp_b"], 16)
                for c in range(8):
                    pg = s * 16 + 8 + c
                    gpsimd.wait_ge(SEM["s1_pev"], pg + 1)
                    gpsimd.dma_start(out=vbuf_d[s * 8 + c, :, :], in_=qsb[pg % 2]
                                     ).then_inc(SEM["sp_a" if pg % 2 == 0 else "sp_b"], 16)
            gpsimd.wait_ge(SEM["sp_a"], 16 * NSUP * 8)
            gpsimd.wait_ge(SEM["sp_b"], 16 * NSUP * 8)

    if upto < 2:
        return nc
    with nc.Block("p1k") as block:
        @block.sync
        def _(sync):
            emit_ln_sync(sync, TC, xj_d, ("kx_a", "kx_b"), "k_app")

        @block.scalar
        def _(scalar):
            emit_ln_scalar(scalar, TC, ("kx_a", "kx_b"), "k_sq", "k_var", "k_rstd")

        @block.tensor
        def _(tensor):
            for gi in range(TC):
                emit_tr_chunk(tensor, gi, "k_app", "k_ev", "k_tr")
            tensor.wait_ge(SEM["k_ev"], TC)
            kg = 0
            for e in range(DC):
                for tb in range(4):
                    pp = p0 if kg % 2 == 0 else p1
                    if kg >= 2:
                        tensor.wait_ge(SEM["k_kev"], kg - 1)
                    for dcx in range(DC):
                        ins = tensor.matmul(
                            pp[:, :],
                            wkb[:, dcx * 1024 + e * 128: dcx * 1024 + (e + 1) * 128],
                            xjf[:, dcx * 2048 + tb * 512: dcx * 2048 + tb * 512 + 512],
                            start=(dcx == 0), stop=(dcx == DC - 1), **MK)
                        if dcx == DC - 1:
                            ins.then_inc(SEM["k_mm"], 1)
                    kg += 1

        @block.vector
        def _(vector):
            emit_ln_vector(vector, TC, ("kx_a", "kx_b"), "k_sq", "k_var", "k_rstd", "k_app", "k_tr", "k_ev",
                           lambda gi: xjf.rearrange("p (d t) -> p d t", d=DC)[:, :, gi * 128:(gi + 1) * 128])
            for kg in range(DC * 4):
                e, tb = kg // 4, kg % 4
                pp = p0 if kg % 2 == 0 else p1
                vector.wait_ge(SEM["k_mm"], kg + 1)
                vector.tensor_copy(kfm(e)[:, tb * 512:tb * 512 + 512], pp[:, :]
                                   ).then_inc(SEM["k_kev"], 1)

        if debug:
            @block.gpsimd
            def _(gpsimd):
                gpsimd.wait_ge(SEM["k_kev"], DC * 4)
                for e in range(DC):
                    gpsimd.dma_start(out=kdbg_d[e, :, :], in_=kfm(e)).then_inc(SEM["dbg"], 16)
                gpsimd.wait_ge(SEM["dbg"], 16 * DC)

    if upto < 3:
        return nc
    OA = (p0, p1)
    OB = (p2, p3)
    SBK = (p4, p5)

    with nc.Block("p2") as block:
        @block.sync
        def _(sync):
            for s in range(16):
                if s >= 2:
                    sync.wait_ge(SEM["s2_pv"], 16 * (s - 1))
                qdst = qstream[:, (s % 2) * 4096:(s % 2) * 4096 + 4096]
                vdst = vstream[:, (s % 2) * 4096:(s % 2) * 4096 + 4096]
                qs = "qv_a" if s % 2 == 0 else "qv_b"
                sync.dma_start(out=qdst.rearrange("p (e t) -> p e t", e=DC),
                               in_=qbuf_d[s // 2, :, :, (s % 2) * 512:(s % 2) * 512 + 512]
                               .transpose([1, 0, 2])).then_inc(SEM[qs], 16)
                sync.dma_start(out=vdst.rearrange("p (c e) -> p c e", c=4),
                               in_=vbuf_d[s * 4:(s + 1) * 4].transpose([1, 0, 2])
                               ).then_inc(SEM[qs], 16)

        @block.tensor
        def _(tensor):
            def emit_S(g):
                s = g // 16
                jb = (g // 2) % 8
                t = g % 2
                sb = (s % 2) * 4096
                if t == 0 and jb == 0:
                    tensor.wait_ge(SEM["qv_a" if s % 2 == 0 else "qv_b"], 32 * (s // 2 + 1))
                if g >= 2:
                    tensor.wait_ge(SEM["s2_sp"], g - 1)
                for icl in range(2):
                    ic = t * 2 + icl
                    for e in range(DC):
                        ins = tensor.matmul(
                            SBK[g % 2][:, icl * 256:icl * 256 + 256],
                            qstream[:, sb + e * 512 + ic * 128: sb + e * 512 + (ic + 1) * 128],
                            kfm(e)[:, jb * 256: jb * 256 + 256],
                            start=(e == 0), stop=(e == DC - 1), **MK)
                        if icl == 1 and e == DC - 1:
                            ins.then_inc(SEM["s2_smm"], 1)

            def emit_PV(g):
                G = g // 2
                s = g // 16
                t = g % 2
                sb = (s % 2) * 4096
                tensor.wait_ge(SEM["s2_sp"], g + 1)
                if t == 0:
                    tensor.wait_ge(SEM["s2_fold"], G)
                par = G % 2
                for icl in range(2):
                    ic = t * 2 + icl
                    for jsub in range(2):
                        lhsT = ppair[g % 2][:, icl * 256 + jsub * 128: icl * 256 + (jsub + 1) * 128]
                        ob = OA if jsub == 0 else OB
                        for eh in range(2):
                            tensor.matmul(ob[eh][:, :], lhsT,
                                          vstream[:, sb + ic * 1024 + eh * 512: sb + ic * 1024 + eh * 512 + 512],
                                          start=(ic == 0), stop=(ic == 3), **MK)
                        slot = par * 16 + jsub * 8 + ic
                        ins = tensor.matmul(p6[:, slot:slot + 1], lhsT, ones_b[:, :],
                                            start=True, stop=True, **MK)
                        if icl == 1 and jsub == 1:
                            ins.then_inc(SEM["s2_pv"], 1)

            for s in range(16):
                base = s * 16
                emit_S(base)
                for t in range(15):
                    emit_S(base + t + 1)
                    emit_PV(base + t)
                emit_PV(base + 15)

        @block.scalar
        def _(scalar):
            for g in range(256):
                scalar.wait_ge(SEM["s2_smm"], g + 1)
                if g >= 2:
                    scalar.wait_ge(SEM["s2_pv"], g - 1)
                scalar.activation(out=ppair[g % 2][:, :], in_=SBK[g % 2][:, :], func=AF.Exp,
                                  bias=negB[:, :], scale=SCALE_EXP).then_inc(SEM["s2_sp"], 1)

        @block.vector
        def _(vector):
            for G in range(16 * 8):
                s, jb = G // 8, G % 8
                par = G % 2
                vector.wait_ge(SEM["s2_pv"], 2 * (G + 1))
                for jsub in range(2):
                    jc = jb * 2 + jsub
                    ob = OA if jsub == 0 else OB
                    ltx = lt if jsub == 0 else lt1
                    sl = par * 16 + jsub * 8
                    vector.reduce_sum(ltx[:, :], p6[:, sl:sl + 4], axis=AX)
                    for eh in range(2):
                        dst = O_sb[:, jc * 1024 + eh * 512: jc * 1024 + eh * 512 + 512]
                        if s == 0:
                            vector.tensor_copy(dst, ob[eh][:, :])
                        else:
                            vector.tensor_tensor(dst, dst, ob[eh][:, :], op=ALU.add)
                    if s == 0:
                        ins = vector.tensor_copy(l_sb[:, jc:jc + 1], ltx[:, :])
                    else:
                        ins = vector.tensor_tensor(l_sb[:, jc:jc + 1], l_sb[:, jc:jc + 1],
                                                   ltx[:, :], op=ALU.add)
                    if jsub == 1:
                        ins.then_inc(SEM["s2_fold"], 1)

    if upto < 4:
        return nc
    with nc.Block("p3") as block:
        @block.sync
        def _(sync):
            for tc in range(TC):
                if tc >= 2:
                    sync.wait_ge(SEM["s3_v"], 9 * (tc - 2) + 3)
                sync.dma_start(out=xres[tc % 2], in_=xj_d[_ts(tc), :]
                               ).then_inc(SEM["x3_a" if tc % 2 == 0 else "x3_b"], 16)

        @block.vector
        def _(vector):
            vector.reciprocal(linv[:, :], l_sb[:, :])
            vector.drain()
            for tc in range(TC):
                oc = O_sb[:, tc * 1024:(tc + 1) * 1024]
                vector.tensor_scalar(oc, oc, linv[:, tc:tc + 1], None, op0=ALU.mult
                                     ).then_inc(SEM["s3_v"], 1)
                vector.drain()
                vector.reduce_sum(sumx[:, :], oc, axis=AX)
                vector.drain()
                vector.wait_ge(SEM["s3_s"], 6 * tc + 1)
                stats_seq(vector).then_inc(SEM["s3_v"], 1)
                vector.wait_ge(SEM["s3_s"], 6 * tc + 2)
                if debug:
                    vector.wait_ge(SEM["dbg"], 16 * (DC + 1 + tc + 1))
                vector.tensor_scalar(oc, oc, mean[:, :], rstd[:, :],
                                     op0=ALU.subtract, op1=ALU.mult)
                vector.drain()
                vector.wait_ge(SEM["x3_a" if tc % 2 == 0 else "x3_b"], 16 * (tc // 2 + 1))
                if tc >= 2:
                    vector.wait_ge(SEM["sp3_a" if tc % 2 == 0 else "sp3_b"], 16 * (tc // 2))
                vector.tensor_tensor(x2sb[tc % 2], oc, xres[tc % 2], op=ALU.add
                                     ).then_inc(SEM["s3_v"], 1)
                vector.drain()
                vector.reduce_sum(sumx[:, :], x2sb[tc % 2], axis=AX)
                vector.drain()
                vector.wait_ge(SEM["s3_s"], 6 * tc + 3)
                stats_seq(vector).then_inc(SEM["s3_v"], 1)
                vector.wait_ge(SEM["s3_s"], 6 * tc + 4)
                vector.tensor_scalar(xf32, x2sb[tc % 2], mean[:, :], rstd[:, :],
                                     op0=ALU.subtract, op1=ALU.mult)
                vector.drain()
                vector.tensor_copy(xf_tm, xf32).then_inc(SEM["s3_v"], 1)
                vector.reduce_sum(sumx[:, :], xf32, axis=AX)
                vector.drain()
                vector.wait_ge(SEM["s3_s"], 6 * tc + 5)
                stats_seq(vector).then_inc(SEM["s3_v"], 1)
                vector.wait_ge(SEM["s3_s"], 6 * tc + 6)
                vector.tensor_scalar(z_tm, xf32, mean[:, :], rstd[:, :],
                                     op0=ALU.subtract, op1=ALU.mult).then_inc(SEM["s3_v"], 1)
                vector.wait_ge(SEM["s3_tr"], 2 * tc + 1)
                vector.tensor_copy(
                    xf_fm.rearrange("p (d t) -> p d t", d=DC)[:, :, tc * 128:(tc + 1) * 128],
                    p_t.rearrange("p (d t) -> p d t", d=DC)).then_inc(SEM["s3_v"], 1)
                vector.wait_ge(SEM["s3_tr"], 2 * tc + 2)
                vector.tensor_copy(
                    z_fm.rearrange("p (d t) -> p d t", d=DC)[:, :, tc * 128:(tc + 1) * 128],
                    p_t2.rearrange("p (d t) -> p d t", d=DC)).then_inc(SEM["s3_v"], 1)

        @block.scalar
        def _(scalar):
            for tc in range(TC):
                for j, (srcv, wv_) in enumerate(((O_sb[:, tc * 1024:(tc + 1) * 1024], 9 * tc + 1),
                                                 (x2sb[tc % 2], 9 * tc + 3),
                                                 (xf32, 9 * tc + 5))):
                    scalar.wait_ge(SEM["s3_v"], wv_)
                    scalar.activation(out=junk[:, :], in_=srcv, func=AF.Square,
                                      accum_out=sumsq[:, :]).then_inc(SEM["s3_s"], 1)
                    scalar.wait_ge(SEM["s3_v"], 9 * tc + 2 + 2 * j)
                    scalar.activation(out=rstd[:, :], in_=var[:, :], func=AF.Ln,
                                      bias=eps5[:, :], scale=1.0)
                    scalar.drain()
                    scalar.activation(out=rstd[:, :], in_=rstd[:, :], func=AF.Exp,
                                      bias=zerob[:, :], scale=-0.5).then_inc(SEM["s3_s"], 1)

        @block.tensor
        def _(tensor):
            for tc in range(TC):
                tensor.wait_ge(SEM["s3_v"], 9 * tc + 5)
                if tc > 0:
                    tensor.wait_ge(SEM["s3_v"], 9 * (tc - 1) + 8)
                for dcx in range(DC):
                    ins = tensor.transpose(p_t[:, dcx * 128:(dcx + 1) * 128],
                                           xf_tm[:, dcx * 128:(dcx + 1) * 128], ident_b[:, :])
                    if dcx == DC - 1:
                        ins.then_inc(SEM["s3_tr"], 1)
                tensor.wait_ge(SEM["s3_v"], 9 * tc + 7)
                if tc > 0:
                    tensor.wait_ge(SEM["s3_v"], 9 * (tc - 1) + 9)
                for dcx in range(DC):
                    ins = tensor.transpose(p_t2[:, dcx * 128:(dcx + 1) * 128],
                                           z_tm[:, dcx * 128:(dcx + 1) * 128], ident_b[:, :])
                    if dcx == DC - 1:
                        ins.then_inc(SEM["s3_tr"], 1)

        @block.gpsimd
        def _(gpsimd):
            if debug:
                gpsimd.wait_ge(SEM["s2_fold"], 64)
                gpsimd.dma_start(out=ldbg_d[:, :], in_=l_sb[:, :]).then_inc(SEM["dbg"], 16)
            for tc in range(TC):
                if debug:
                    gpsimd.wait_ge(SEM["s3_v"], 9 * tc + 1)
                    gpsimd.dma_start(out=odbg_d[tc, :, :], in_=O_sb[:, tc * 1024:(tc + 1) * 1024]
                                     ).then_inc(SEM["dbg"], 16)
                gpsimd.wait_ge(SEM["s3_v"], 9 * tc + 3)
                gpsimd.dma_start(out=x2buf_d[tc, :, :], in_=x2sb[tc % 2]
                                 ).then_inc(SEM["sp3_a" if tc % 2 == 0 else "sp3_b"], 16)
            gpsimd.wait_ge(SEM["sp3_a"], 16 * TC // 2)
            gpsimd.wait_ge(SEM["sp3_b"], 16 * TC // 2)

    if upto < 5:
        return nc
    try:
        p4block = nc.Block("p4")
        p4block.__enter__()
        block = p4block
        @block.sync
        def _(sync):
            for c in range(DC):
                sync.dma_start(out=g_t[:, c:c + 1], in_=lng_d[c, :, :]).then_inc(SEM["a_ld"], 16)
                sync.dma_start(out=b_t[:, c:c + 1], in_=lnb_d[c, :, :]).then_inc(SEM["a_ld"], 16)
            sync.dma_start(out=rb_row[:, :], in_=rb_d[:, :]).then_inc(SEM["a_ld"], 16)
            sync.dma_start(out=grow[:, :], in_=gam_d[:, :]).then_inc(SEM["a_ld"], 16)
            for c in range(DC):
                if c >= 2:
                    sync.wait_ge(SEM["a_ve"], c - 1)
                    sync.wait_ge(SEM["a_te"], c)
                sync.dma_start(out=wstage[c % 2][:, 0:512], in_=rwt_d[_ts(c), :]
                               ).then_inc(SEM["ra_a" if c % 2 == 0 else "ra_b"], 16)

        @block.tensor
        def _(tensor):
            tensor.wait_ge(SEM["a_ld"], 16 * 18)
            tensor.matmul(p4[:, :], ones1f[:, :], grow[:, 0:512], start=True, stop=True, **MK)
            tensor.matmul(p5[:, :], ones1f[:, :], grow[:, 512:1024], start=True, stop=True, **MK
                          ).then_inc(SEM["a_te"], 1)
            for c in range(DC):
                tensor.wait_ge(SEM["ra_a" if c % 2 == 0 else "ra_b"], 16 * (c // 2 + 1))
                tensor.matmul(p6[0:1, :], b_t[:, c:c + 1], wstage[c % 2][:, 0:512],
                              start=(c == 0), stop=(c == DC - 1), **MK).then_inc(SEM["a_te"], 1)
            tensor.wait_ge(SEM["a_ve"], DC + 2)
            tensor.matmul(p4[:, :], ones1f[:, :], brow[:, :], start=True, stop=True, **MK
                          ).then_inc(SEM["a_te"], 1)

        @block.vector
        def _(vector):
            vector.wait_ge(SEM["a_te"], 1)
            vector.tensor_copy(gbc[:, 0:512], p4[:, :])
            vector.tensor_copy(gbc[:, 512:1024], p5[:, :])
            for c in range(DC):
                vector.wait_ge(SEM["ra_a" if c % 2 == 0 else "ra_b"], 16 * (c // 2 + 1))
                vector.wait_ge(SEM["a_te"], c + 2)
                vector.tensor_scalar(rw_b[:, c * 512:(c + 1) * 512], wstage[c % 2][:, 0:512],
                                     g_t[:, c:c + 1], None, op0=ALU.mult).then_inc(SEM["a_ve"], 1)
            vector.wait_ge(SEM["a_te"], DC + 1)
            vector.tensor_copy(brow[:, :], p6[0:1, :]).then_inc(SEM["a_ve"], 1)
            vector.drain()
            vector.tensor_tensor(brow[:, :], brow[:, :], rb_row[:, :], op=ALU.add
                                 ).then_inc(SEM["a_ve"], 1)
            vector.wait_ge(SEM["a_te"], DC + 2)
            vector.tensor_copy(bbc[:, :], p4[:, :]).then_inc(SEM["a_ve"], 1)

        if p4stage < 1:
            raise StopPhase()
        if debug:
            @block.gpsimd
            def _(gpsimd):
                gpsimd.dma_start(out=xfdbg_d[:, :], in_=xf_fm).then_inc(SEM["dbg"], 16)
                gpsimd.dma_start(out=zfdbg_d[:, :], in_=z_fm).then_inc(SEM["dbg"], 16)

        @block.tensor
        def _(tensor):
            for tc in range(TC):
                pp = p4 if tc % 2 == 0 else p5
                if tc == 0:
                    tensor.wait_ge(SEM["a_ve"], DC + 3)
                if tc >= 2:
                    tensor.wait_ge(SEM["r_ev"], 2 * (tc - 2) + 1)
                for dcx in range(DC):
                    ins = tensor.matmul(
                        pp[:, :],
                        z_fm[:, dcx * 2048 + tc * 128: dcx * 2048 + (tc + 1) * 128],
                        rw_b[:, dcx * 512:(dcx + 1) * 512],
                        start=(dcx == 0), stop=(dcx == DC - 1), **MK)
                    if dcx == DC - 1:
                        ins.then_inc(SEM["r_mm"], 1)

        @block.vector
        def _(vector):
            for tc in range(TC):
                pp = p4 if tc % 2 == 0 else p5
                vector.wait_ge(SEM["r_mm"], tc + 1)
                if tc >= 2:
                    vector.wait_ge(SEM["r_sp"], tc - 1)
                vector.tensor_tensor(clipb[tc % 2], pp[:, :], bbc[:, :], op=ALU.add
                                     ).then_inc(SEM["r_ev"], 1)
                vector.drain()
                vector.tensor_scalar(clipb[tc % 2], clipb[tc % 2], -10.0, 10.0,
                                     op0=ALU.max, op1=ALU.min).then_inc(SEM["r_ev"], 1)
                vector.wait_ge(SEM["r_sp"], tc + 1)
                vector.max(out=top8[:, :], in_=alphab[tc % 2])
                vector.drain()
                vector.reduce_sum(SSt[:, tc:tc + 1], top8[:, :], axis=AX)
                vector.tensor_scalar(mge[:, :], alphab[tc % 2], top8[:, 7:8], None, op0=ALU.is_ge)
                vector.drain()
                vector.tensor_tensor(wmask[:, tc * 512:(tc + 1) * 512], alphab[tc % 2],
                                     mge[:, :], op=ALU.mult).then_inc(SEM["r_tk"], 1)

        @block.scalar
        def _(scalar):
            for tc in range(TC):
                scalar.wait_ge(SEM["r_ev"], 2 * tc + 2)
                if tc >= 2:
                    scalar.wait_ge(SEM["r_tk"], tc - 1)
                    if debug:
                        scalar.wait_ge(SEM["dbg"], 16 * (DC + 1 + TC) + 16 * (tc - 1))
                scalar.activation(out=alphab[tc % 2], in_=clipb[tc % 2], func=AF.Exp,
                                  bias=zerob[:, :], scale=1.0)
                scalar.drain()
                scalar.activation(out=alphab[tc % 2], in_=alphab[tc % 2], func=AF.Ln,
                                  bias=onesf[:, :], scale=1.0).then_inc(SEM["r_sp"], 1)

        if debug:
            @block.gpsimd
            def _(gpsimd):
                for tc in range(TC):
                    gpsimd.wait_ge(SEM["r_sp"], tc + 1)
                    gpsimd.dma_start(out=adbg_d[tc, :, :], in_=alphab[tc % 2]).then_inc(SEM["dbg"], 16)

        if p4stage < 2:
            raise StopPhase()
        @block.scalar
        def _(scalar):
            scalar.wait_ge(SEM["r_tk"], TC)
            scalar.activation(out=fac[:, :], in_=SSt[:, :], func=AF.Tanh, bias=zerob[:, :],
                              scale=1.0).then_inc(SEM["r_sp"], 1)

        @block.vector
        def _(vector):
            vector.wait_ge(SEM["r_sp"], TC + 1)
            vector.tensor_scalar(SS2[:, :], SSt[:, :], 1e-6, None, op0=ALU.add)
            vector.drain()
            vector.reciprocal(SS2[:, :], SS2[:, :])
            vector.drain()
            vector.tensor_tensor(fac[:, :], fac[:, :], SS2[:, :], op=ALU.mult)
            vector.drain()
            for tc in range(TC):
                vector.tensor_scalar(w_tm[:, tc * 512:(tc + 1) * 512],
                                     wmask[:, tc * 512:(tc + 1) * 512],
                                     fac[:, tc:tc + 1], None, op0=ALU.mult).then_inc(SEM["c_w"], 1)

        @block.tensor
        def _(tensor):
            for tc in range(TC):
                tensor.wait_ge(SEM["c_w"], tc + 1)
                if tc >= 2:
                    tensor.wait_ge(SEM["c_wf"], tc - 1)
                bank = ptb(tc)
                for us in range(4):
                    ins = tensor.transpose(bank[:, us * 128:(us + 1) * 128],
                                           w_tm[:, tc * 512 + us * 128: tc * 512 + (us + 1) * 128],
                                           ident_b[:, :])
                    if us == 3:
                        ins.then_inc(SEM["c_wt"], 1)

        @block.vector
        def _(vector):
            for tc in range(TC):
                vector.wait_ge(SEM["c_wt"], tc + 1)
                vector.tensor_copy(
                    w_fmr.rearrange("p (u t) -> p u t", u=4)[:, :, tc * 128:(tc + 1) * 128],
                    ptb(tc)[:, 0:512].rearrange("p (u t) -> p u t", u=4)
                    ).then_inc(SEM["c_wf"], 1)

        if debug:
            @block.gpsimd
            def _(gpsimd):
                gpsimd.wait_ge(SEM["c_wf"], TC)
                for us in range(4):
                    gpsimd.dma_start(out=wdbg_d[us, :, :],
                                     in_=w_fmr[:, us * 2048:(us + 1) * 2048]).then_inc(SEM["dbg"], 16)

        if p4stage < 3:
            raise StopPhase()
        @block.sync
        def _(sync):
            sync.wait_ge(SEM["a_ve"], DC)
            sync.wait_ge(SEM["a_te"], DC + 1)
            for uc in range(4):
                if uc >= 2:
                    sync.wait_ge(SEM["d_sq"], uc - 1)
                sync.dma_start(out=wstage[uc % 2], in_=u_d[_ts(uc), :]
                               ).then_inc(SEM["du_a" if uc % 2 == 0 else "du_b"], 16)
            sync.wait_ge(SEM["r_sp"], TC)
            for uc in range(4):
                for dcx in range(DC):
                    gi = uc * DC + dcx
                    if gi >= 2:
                        sync.wait_ge(SEM["d_utc"], gi - 1)
                    sync.dma_start(out=utstg[gi % 2], in_=ut_d[_ts(dcx), _ts(uc)]
                                   ).then_inc(SEM["dut_a" if gi % 2 == 0 else "dut_b"], 16)

        @block.scalar
        def _(scalar):
            for uc in range(4):
                scalar.wait_ge(SEM["du_a" if uc % 2 == 0 else "du_b"], 16 * (uc // 2 + 1))
                scalar.activation(out=junk[:, :], in_=wstage[uc % 2], func=AF.Square,
                                  accum_out=su2[:, uc:uc + 1]).then_inc(SEM["d_sq"], 1)
            scalar.activation(out=su_t[:, :], in_=su2[:, :], func=AF.Ln, bias=eps12[:, :], scale=1.0)
            scalar.drain()
            scalar.activation(out=su_t[:, :], in_=su_t[:, :], func=AF.Exp, bias=zerob[:, :],
                              scale=-0.5).then_inc(SEM["d_sq"], 1)

        @block.vector
        def _(vector):
            def stt_group(uc_):
                for tb in range(4):
                    hg = uc_ * 4 + tb
                    pp = p4 if hg % 2 == 0 else p5
                    vector.wait_ge(SEM["d_h"], hg + 1)
                    vector.scalar_tensor_tensor(
                        wh_r[:, uc_ * 2048 + tb * 512: uc_ * 2048 + tb * 512 + 512],
                        pp[:, :], su_t[:, uc_:uc_ + 1],
                        w_fmr[:, uc_ * 2048 + tb * 512: uc_ * 2048 + tb * 512 + 512],
                        op0=ALU.mult, op1=ALU.mult).then_inc(SEM["d_wh"], 1)

            vector.wait_ge(SEM["d_sq"], 5)
            for uc in range(4):
                for dcx in range(DC):
                    gi = uc * DC + dcx
                    vector.wait_ge(SEM["dut_a" if gi % 2 == 0 else "dut_b"], 16 * (gi // 2 + 1))
                    if gi >= 4 and dcx == 0:
                        vector.wait_ge(SEM["d_h"], 4 * (uc - 1))
                    vector.tensor_copy(ut_b[uc % 2][:, dcx * 128:(dcx + 1) * 128],
                                       utstg[gi % 2]).then_inc(SEM["d_utc"], 1)
                if uc >= 1:
                    stt_group(uc - 1)
            stt_group(3)

        @block.tensor
        def _(tensor):
            tensor.wait_ge(SEM["d_sq"], 5)
            tensor.wait_ge(SEM["c_wf"], TC)
            hg = 0
            for uc in range(4):
                tensor.wait_ge(SEM["d_utc"], DC * (uc + 1))
                for tb in range(4):
                    pp = p4 if hg % 2 == 0 else p5
                    if hg >= 2:
                        tensor.wait_ge(SEM["d_wh"], hg - 1)
                    for dcx in range(DC):
                        ins = tensor.matmul(
                            pp[:, :], ut_b[uc % 2][:, dcx * 128:(dcx + 1) * 128],
                            xf_fm[:, dcx * 2048 + tb * 512: dcx * 2048 + tb * 512 + 512],
                            start=(dcx == 0), stop=(dcx == DC - 1), **MK)
                        if dcx == DC - 1:
                            ins.then_inc(SEM["d_h"], 1)
                    hg += 1

        if p4stage < 4:
            raise StopPhase()
        @block.sync
        def _(sync):
            sync.wait_ge(SEM["d_sq"], 5)
            for hs in range(8):
                for k in range(12):
                    gi = hs * 12 + k
                    if gi >= 2:
                        sync.wait_ge(SEM["e_cast"], gi - 1)
                    if k < 8:
                        src = w1t_d[_ts(k), hs * 512:(hs + 1) * 512]
                        dst = wstage[gi % 2][:, 0:512]
                    else:
                        src = w2t_d[_ts(hs * 4 + (k - 8)), :]
                        dst = wstage[gi % 2]
                    sync.dma_start(out=dst, in_=src
                                   ).then_inc(SEM["ee_a" if gi % 2 == 0 else "ee_b"], 16)

        @block.vector
        def _(vector):
            def fold_group(hs_):
                for k in range(16):
                    g2 = hs_ * 16 + k
                    tb, tc2 = k // 4, k % 4
                    tcg = tb * 4 + tc2
                    ob = OA if g2 % 2 == 0 else OB
                    vector.wait_ge(SEM["e_m2"], g2 + 1)
                    for dh in range(2):
                        dst = static[:, tcg * 1024 + dh * 512: tcg * 1024 + dh * 512 + 512]
                        if hs_ == 0:
                            ins = vector.tensor_copy(dst, ob[dh][:, :])
                        else:
                            ins = vector.tensor_tensor(dst, dst, ob[dh][:, :], op=ALU.add)
                        if dh == 1:
                            ins.then_inc(SEM["e_f"], 1)

            for hs in range(8):
                par = hs % 2
                for k in range(12):
                    gi = hs * 12 + k
                    vector.wait_ge(SEM["ee_a" if gi % 2 == 0 else "ee_b"], 16 * (gi // 2 + 1))
                    if hs >= 2 and k == 0:
                        vector.wait_ge(SEM["e_m2"], 16 * (hs - 1))
                    if k < 8:
                        vector.tensor_copy(w1sb2[par][:, k * 512:(k + 1) * 512],
                                           wstage[gi % 2][:, 0:512]).then_inc(SEM["e_cast"], 1)
                    else:
                        vector.tensor_copy(w2sb2[par][:, (k - 8) * 1024:(k - 7) * 1024],
                                           wstage[gi % 2]).then_inc(SEM["e_cast"], 1)
                if hs >= 1:
                    fold_group(hs - 1)
            fold_group(7)

        @block.tensor
        def _(tensor):
            tensor.wait_ge(SEM["d_wh"], 16)
            for hs in range(8):
                par = hs % 2
                tensor.wait_ge(SEM["e_cast"], 12 * (hs + 1))
                for tb in range(4):
                    gE = hs * 4 + tb
                    for hc in range(4):
                        pp = p4 if hc % 2 == 0 else p5
                        if gE * 4 + hc - 1 > 0:
                            tensor.wait_ge(SEM["e_g"], gE * 4 + hc - 1)
                        for dcx in range(DC):
                            ins = tensor.matmul(
                                pp[:, :], w1sb2[par][:, dcx * 512 + hc * 128: dcx * 512 + (hc + 1) * 128],
                                xf_fm[:, dcx * 2048 + tb * 512: dcx * 2048 + tb * 512 + 512],
                                start=(dcx == 0), stop=(dcx == DC - 1), **MK)
                            if dcx == DC - 1:
                                ins.then_inc(SEM["e_m1"], 1)
                    tensor.wait_ge(SEM["e_g"], 4 * (gE + 1))
                    for tc2 in range(4):
                        g2 = gE * 4 + tc2
                        ob = OA if g2 % 2 == 0 else OB
                        if g2 >= 1:
                            tensor.wait_ge(SEM["e_f"], g2)
                        for hc in range(4):
                            for dh in range(2):
                                ins = tensor.matmul(
                                    ob[dh][:, :],
                                    h1sb[:, hc * 512 + tc2 * 128: hc * 512 + (tc2 + 1) * 128],
                                    w2sb2[par][:, hc * 1024 + dh * 512: hc * 1024 + dh * 512 + 512],
                                    start=(hc == 0), stop=(hc == 3), **MK)
                                if hc == 3 and dh == 1:
                                    ins.then_inc(SEM["e_m2"], 1)

        @block.scalar
        def _(scalar):
            for gE in range(32):
                for hc in range(4):
                    pp = p4 if hc % 2 == 0 else p5
                    scalar.wait_ge(SEM["e_m1"], gE * 4 + hc + 1)
                    scalar.activation(out=h1sb[:, hc * 512:(hc + 1) * 512], in_=pp[:, :],
                                      func=AF.Gelu, bias=zerob[:, :], scale=1.0
                                      ).then_inc(SEM["e_g"], 1)

        if debug:
            @block.gpsimd
            def _(gpsimd):
                gpsimd.wait_ge(SEM["e_f"], 128)
                for tc in range(TC):
                    gpsimd.dma_start(out=sdbg_d[tc, :, :],
                                     in_=static[:, tc * 1024:(tc + 1) * 1024]).then_inc(SEM["dbg"], 16)

        if p4stage < 5:
            raise StopPhase()
        @block.sync
        def _(sync):
            sync.wait_ge(SEM["e_cast"], 96)
            for uc in range(4):
                if uc >= 2:
                    sync.wait_ge(SEM["f_sq"], uc - 1)
                sync.dma_start(out=wstage[uc % 2], in_=v_d[_ts(uc), :]
                               ).then_inc(SEM["fv_a" if uc % 2 == 0 else "fv_b"], 16)
            for uc in range(4):
                sync.wait_ge(SEM["f_sq"], min(uc + 2, 5))
                if uc >= 2:
                    sync.wait_ge(SEM["f_vpp"], uc - 1)
                sync.dma_start(out=wstage[uc % 2], in_=v_d[_ts(uc), :]
                               ).then_inc(SEM["fv_a" if uc % 2 == 0 else "fv_b"], 16)
            sync.wait_ge(SEM["f_vpp"], 4)
            for tc in range(TC):
                if tc >= 2:
                    sync.wait_ge(SEM["f_ve"], 3 * (tc - 2) + 3)
                sync.dma_start(out=xres[tc % 2], in_=x2buf_d[tc, :, :]
                               ).then_inc(SEM["fx_a" if tc % 2 == 0 else "fx_b"], 16)

        @block.scalar
        def _(scalar):
            for uc in range(4):
                scalar.wait_ge(SEM["fv_a" if uc % 2 == 0 else "fv_b"], 16 * (uc // 2 + 1))
                scalar.activation(out=junk[:, :], in_=wstage[uc % 2], func=AF.Square,
                                  accum_out=sv2[:, uc:uc + 1]).then_inc(SEM["f_sq"], 1)
            scalar.activation(out=sv_t[:, :], in_=sv2[:, :], func=AF.Ln, bias=eps12[:, :], scale=1.0)
            scalar.drain()
            scalar.activation(out=sv_t[:, :], in_=sv_t[:, :], func=AF.Exp, bias=zerob[:, :],
                              scale=-0.5).then_inc(SEM["f_sq"], 1)
            for tc in range(TC):
                scalar.wait_ge(SEM["f_ve"], 3 * tc + 1)
                scalar.activation(out=junk[:, :], in_=fsb, func=AF.Square,
                                  accum_out=sumsq[:, :]).then_inc(SEM["f_sc"], 1)
                scalar.wait_ge(SEM["f_ve"], 3 * tc + 2)
                scalar.activation(out=rstd[:, :], in_=var[:, :], func=AF.Ln, bias=eps5[:, :], scale=1.0)
                scalar.drain()
                scalar.activation(out=rstd[:, :], in_=rstd[:, :], func=AF.Exp, bias=zerob[:, :],
                                  scale=-0.5).then_inc(SEM["f_sc"], 1)

        @block.vector
        def _(vector):
            vector.wait_ge(SEM["f_sq"], 5)
            vector.wait_ge(SEM["e_f"], 128)
            for uc in range(4):
                vector.wait_ge(SEM["fv_a" if uc % 2 == 0 else "fv_b"], 16 * ((4 + uc) // 2 + 1))
                vector.scalar_tensor_tensor(vpp[:, uc * 1024:(uc + 1) * 1024],
                                            wstage[uc % 2], sv_t[:, uc:uc + 1], gbc[:, :],
                                            op0=ALU.mult, op1=ALU.mult).then_inc(SEM["f_vpp"], 1)
            for tc in range(TC):
                ob = OA if tc % 2 == 0 else OB
                vector.wait_ge(SEM["f_dyn"], tc + 1)
                for dh in range(2):
                    ins = vector.tensor_tensor(
                        fsb[:, dh * 512:dh * 512 + 512],
                        static[:, tc * 1024 + dh * 512: tc * 1024 + dh * 512 + 512],
                        ob[dh][:, :], op=ALU.add)
                    if dh == 1:
                        ins.then_inc(SEM["f_ve"], 1)
                vector.drain()
                vector.reduce_sum(sumx[:, :], fsb, axis=AX)
                vector.drain()
                vector.wait_ge(SEM["f_sc"], 2 * tc + 1)
                stats_seq(vector).then_inc(SEM["f_ve"], 1)
                vector.wait_ge(SEM["f_sc"], 2 * tc + 2)
                vector.tensor_scalar(fsb, fsb, mean[:, :], rstd[:, :],
                                     op0=ALU.subtract, op1=ALU.mult)
                vector.drain()
                vector.wait_ge(SEM["fx_a" if tc % 2 == 0 else "fx_b"], 16 * (tc // 2 + 1))
                if tc >= 2:
                    vector.wait_ge(SEM["fs_a" if tc % 2 == 0 else "fs_b"], 16 * (tc // 2))
                vector.tensor_tensor(ysb[tc % 2], fsb, xres[tc % 2], op=ALU.add
                                     ).then_inc(SEM["f_ve"], 1)

        @block.tensor
        def _(tensor):
            for tc in range(TC):
                ob = OA if tc % 2 == 0 else OB
                if tc == 0:
                    tensor.wait_ge(SEM["f_vpp"], 4)
                if tc >= 2:
                    tensor.wait_ge(SEM["f_ve"], 3 * (tc - 2) + 1)
                for uc in range(4):
                    for dh in range(2):
                        ins = tensor.matmul(
                            ob[dh][:, :], wh_r[:, uc * 2048 + tc * 128: uc * 2048 + (tc + 1) * 128],
                            vpp[:, uc * 1024 + dh * 512: uc * 1024 + dh * 512 + 512],
                            start=(uc == 0), stop=(uc == 3), **MK)
                        if uc == 3 and dh == 1:
                            ins.then_inc(SEM["f_dyn"], 1)

        @block.gpsimd
        def _(gpsimd):
            for tc in range(TC):
                gpsimd.wait_ge(SEM["f_ve"], 3 * tc + 3)
                gpsimd.dma_start(out=y_d[_ts(tc), :], in_=ysb[tc % 2]
                                 ).then_inc(SEM["fs_a" if tc % 2 == 0 else "fs_b"], 16)
            gpsimd.wait_ge(SEM["fs_a"], 16 * TC // 2)
            gpsimd.wait_ge(SEM["fs_b"], 16 * TC // 2)

        p4block.__exit__(None, None, None)
    except StopPhase:
        p4block.__exit__(None, None, None)
    return nc


SCALE_EXP = 1.0 / 32.0


@functools.lru_cache(maxsize=2)
def _get_nc(debug=False):
    return build_nc(debug=debug)


def build_upto(n, debug=True):
    return build_nc(debug=debug, upto=n)


def _prep_inputs(inputs, core):
    b = core // 4
    j0 = (core % 4) * JS
    x = np.asarray(inputs["L0_attn_x"], dtype=np.float32)
    c = np.ascontiguousarray
    return {
        "x": c(x[b]),
        "xj": c(x[b, j0:j0 + JS]),
        "wq": c(np.asarray(inputs["L0_attn_W_query"], np.float32)),
        "wk": c(np.asarray(inputs["L0_attn_W_key"], np.float32)),
        "wv": c(np.asarray(inputs["L0_attn_W_value"], np.float32)),
        "w1t": c(np.asarray(inputs["L0_ffn_W1"], np.float32).T),
        "w2t": c(np.asarray(inputs["L0_ffn_W2"], np.float32).T),
        "rwt": c(np.asarray(inputs["L0_ffn_router_W"], np.float32).T),
        "ut": c(np.asarray(inputs["L0_ffn_raw_U"], np.float32).T),
        "u": c(np.asarray(inputs["L0_ffn_raw_U"], np.float32)),
        "v": c(np.asarray(inputs["L0_ffn_raw_V"], np.float32)),
        "lng": c(np.asarray(inputs["L0_ffn_ln_g"], np.float32).reshape(DC, 128, 1)),
        "lnb": c(np.asarray(inputs["L0_ffn_ln_b"], np.float32).reshape(DC, 128, 1)),
        "rb": c(np.asarray(inputs["L0_ffn_router_b"], np.float32).reshape(1, NB)),
        "gamma": c(np.asarray(inputs["L0_ffn_gamma"], np.float32).reshape(1, D)),
        "ident": np.eye(128, dtype=np.float32),
    }


def kernel(**inputs):
    nc = _get_nc(False)
    in_maps = [_prep_inputs(inputs, c) for c in range(8)]
    res = run_bass_kernel_spmd(nc, in_maps, core_ids=list(range(8)))
    out = np.empty((2, SEQ, D), dtype=np.float32)
    for c in range(8):
        b = c // 4
        j0 = (c % 4) * JS
        out[b, j0:j0 + JS] = res.results[c]["y"]
    return out


# version 08
# speedup vs baseline: 1.0503x; 1.0503x over previous
import functools
import numpy as np

import concourse.bass as bass
import concourse.mybir as mybir
from concourse.bass_utils import run_bass_kernel_spmd

F32 = mybir.dt.float32
BF16 = mybir.dt.bfloat16
AF = mybir.ActivationFunctionType
ALU = mybir.AluOpType
AX = mybir.AxisListType.X

SEQ, D, HID, NB = 8192, 1024, 4096, 512
JS = 2048
NSUP, SUP = 8, 1024
B_SHIFT = 30.0
DC = D // 128
TC = JS // 128
XC = SEQ // 128
MK = dict(skip_group_check=True)


class StopPhase(Exception):
    pass


def _ts(i, n=128):
    return slice(i * n, (i + 1) * n)


def build_nc(debug=False, upto=5, p4stage=9):
    nc = bass.Bass("TRN2", target_bir_lowering=False)

    x_d = nc.dram_tensor("x", [SEQ, D], F32, kind="ExternalInput").ap()
    xj_d = nc.dram_tensor("xj", [JS, D], F32, kind="ExternalInput").ap()
    wq_d = nc.dram_tensor("wq", [D, D], F32, kind="ExternalInput").ap()
    wk_d = nc.dram_tensor("wk", [D, D], F32, kind="ExternalInput").ap()
    wv_d = nc.dram_tensor("wv", [D, D], F32, kind="ExternalInput").ap()
    w1t_d = nc.dram_tensor("w1t", [D, HID], F32, kind="ExternalInput").ap()
    w2t_d = nc.dram_tensor("w2t", [HID, D], F32, kind="ExternalInput").ap()
    rwt_d = nc.dram_tensor("rwt", [D, NB], F32, kind="ExternalInput").ap()
    ut_d = nc.dram_tensor("ut", [D, NB], F32, kind="ExternalInput").ap()
    u_d = nc.dram_tensor("u", [NB, D], F32, kind="ExternalInput").ap()
    v_d = nc.dram_tensor("v", [NB, D], F32, kind="ExternalInput").ap()
    lng_d = nc.dram_tensor("lng", [DC, 128, 1], F32, kind="ExternalInput").ap()
    lnb_d = nc.dram_tensor("lnb", [DC, 128, 1], F32, kind="ExternalInput").ap()
    rb_d = nc.dram_tensor("rb", [1, NB], F32, kind="ExternalInput").ap()
    gam_d = nc.dram_tensor("gamma", [1, D], F32, kind="ExternalInput").ap()
    id_d = nc.dram_tensor("ident", [128, 128], F32, kind="ExternalInput").ap()
    y_d = nc.dram_tensor("y", [JS, D], F32, kind="ExternalOutput").ap()

    ikind = "ExternalOutput" if debug else "Internal"
    qbuf_d = nc.dram_tensor("qbuf", [NSUP, DC, 128, SUP], BF16, kind=ikind).ap()
    vbuf_d = nc.dram_tensor("vbuf", [XC, 128, D], BF16, kind=ikind).ap()
    x2buf_d = nc.dram_tensor("x2buf", [TC, 128, D], F32, kind=ikind).ap()
    if debug:
        odbg_d = nc.dram_tensor("odbg", [TC, 128, D], F32, kind="ExternalOutput").ap()
        ldbg_d = nc.dram_tensor("ldbg", [128, TC], F32, kind="ExternalOutput").ap()
        kdbg_d = nc.dram_tensor("kdbg", [DC, 128, JS], BF16, kind="ExternalOutput").ap()
        adbg_d = nc.dram_tensor("adbg", [TC, 128, NB], F32, kind="ExternalOutput").ap()
        wdbg_d = nc.dram_tensor("wdbg", [4, 128, JS], BF16, kind="ExternalOutput").ap()
        sdbg_d = nc.dram_tensor("sdbg", [TC, 128, D], F32, kind="ExternalOutput").ap()
        xsdbg_d = nc.dram_tensor("xsdbg", [128, 8192], BF16, kind="ExternalOutput").ap()
        stdbg_d = nc.dram_tensor("stdbg", [128, 8], F32, kind="ExternalOutput").ap()
        xfdbg_d = nc.dram_tensor("xfdbg", [128, 16384], BF16, kind="ExternalOutput").ap()
        zfdbg_d = nc.dram_tensor("zfdbg", [128, 16384], BF16, kind="ExternalOutput").ap()
        wqdbg_d = nc.dram_tensor("wqdbg", [128, 8192], BF16, kind="ExternalOutput").ap()

    BB = nc.alloc_sbuf_tensor("BB", [128, 49152], BF16).ap()
    FF = nc.alloc_sbuf_tensor("FF", [128, 16384], F32).ap()
    stg = nc.alloc_sbuf_tensor("stg", [128, 6144], F32).ap()
    junk = nc.alloc_sbuf_tensor("junkb", [128, 1024], BF16).ap()

    A = nc.alloc_sbuf_tensor
    ident_b = A("ident_b", [128, 128], BF16).ap()
    ones_b = A("ones_b", [128, 1], BF16).ap()
    ones1f = A("ones1f", [1, 128], F32).ap()
    negB = A("negB", [128, 1], F32).ap()
    zerob = A("zerob", [128, 1], F32).ap()
    eps5 = A("eps5", [128, 1], F32).ap()
    onesf = A("onesf", [128, 1], F32).ap()
    eps12 = A("eps12", [128, 1], F32).ap()
    g_t = A("g_t", [128, DC], F32).ap()
    b_t = A("b_t", [128, DC], F32).ap()
    su2 = A("su2", [128, 4], F32).ap()
    su_t = A("su_t", [128, 4], F32).ap()
    sv2 = A("sv2", [128, 4], F32).ap()
    sv_t = A("sv_t", [128, 4], F32).ap()
    SSt = A("SSt", [128, 16], F32).ap()
    SS2 = A("SS2", [128, 16], F32).ap()
    fac = A("fac", [128, 16], F32).ap()
    l_sb = A("l_sb", [128, 16], F32).ap()
    top8 = A("top8", [128, 8], F32).ap()
    rb_row = A("rb_row", [1, NB], F32).ap()
    brow = A("brow", [1, NB], F32).ap()
    grow = A("grow", [1, D], F32).ap()
    sumx = A("sumx", [128, 1], F32).ap()
    sumsq = A("sumsq", [128, 1], F32).ap()
    mean = A("mean", [128, 1], F32).ap()
    var = A("var", [128, 1], F32).ap()
    rstd = A("rstd", [128, 1], F32).ap()
    msq = A("msq", [128, 1], F32).ap()
    lt = A("lt", [128, 1], F32).ap()
    lt1 = A("lt1", [128, 1], F32).ap()
    vtmp = A("vtmp", [128, 1], F32).ap()
    linv = A("linv", [128, 16], F32).ap()
    stdbg = A("stdbg_s", [128, 8], F32).ap()

    wqb = BB[:, 0:8192]
    wkb = BB[:, 8192:16384]
    wvb = BB[:, 16384:24576]
    xsup = [BB[:, 24576:32768], BB[:, 32768:40960]]
    xln = [BB[:, 40960:41984], BB[:, 41984:43008]]
    qsb = [BB[:, 43008:44032], BB[:, 44032:45056]]
    xjf = BB[:, 24576:40960]
    kq = [BB[:, 16384:24576], BB[:, 40960:49152]]

    def kfm(e):
        return kq[e // 4][:, (e % 4) * 2048:(e % 4) * 2048 + 2048]

    qstream = BB[:, 0:8192]
    vstream = BB[:, 8192:16384]
    ppair = [BB[:, 24576:25088], BB[:, 25088:25600]]
    xf_fm = BB[:, 0:16384]
    z_fm = BB[:, 16384:32768]
    xf_tm = BB[:, 40960:41984]
    z_tm = BB[:, 41984:43008]
    rw_b = BB[:, 43008:47104]
    w_tm = BB[:, 32768:40960]
    w_fmr = BB[:, 40960:49152]
    wh_r = BB[:, 16384:24576]
    ut_b = [BB[:, 24576:25600], BB[:, 25600:26624]]
    w1sb2 = [BB[:, 24576:28672], BB[:, 28672:32768]]
    w2sb2 = [BB[:, 32768:36864], BB[:, 36864:40960]]
    h1sb = BB[:, 43008:47104]
    vpp = BB[:, 0:4096]

    O_sb = FF[:, 0:16384]
    wmask = FF[:, 0:8192]
    mge = FF[:, 8192:8704]
    static = FF[:, 0:16384]

    xstg = [stg[:, 0:1024], stg[:, 1024:2048]]
    xstg4 = [stg[:, 0:1024], stg[:, 1024:2048], stg[:, 2048:3072], stg[:, 3072:4096]]
    XSEM = ("x_a", "x_b", "x_c", "x_d")
    xres = [stg[:, 0:1024], stg[:, 1024:2048]]
    x2sb = [stg[:, 2048:3072], stg[:, 3072:4096]]
    xf32 = stg[:, 4096:5120]
    gbc = stg[:, 0:1024]
    wstage = [stg[:, 1024:2048], stg[:, 2048:3072]]
    bbc = stg[:, 3072:3584]
    clipb = [stg[:, 3584:4096], stg[:, 4096:4608]]
    alphab = [stg[:, 4608:5120], stg[:, 5120:5632]]
    utstg = [stg[:, 3584:3712], stg[:, 3712:3840]]
    fsb = stg[:, 4608:5632]
    ysb = [stg[:, 2048:3072], stg[:, 3072:4096]]

    P = nc.alloc_psum_tensor
    p0 = P("p0", [128, 512], F32).ap()
    p1 = P("p1", [128, 512], F32).ap()
    p2 = P("p2", [128, 512], F32).ap()
    p3 = P("p3", [128, 512], F32).ap()
    p4 = P("p4", [128, 512], F32).ap()
    p5 = P("p5", [128, 512], F32).ap()
    p6 = P("p6", [128, 512], F32).ap()
    p_t = P("p_t", [128, 1024], BF16).ap()
    p_t2 = p4.bitcast(BF16)

    def ptb(gi):
        return p_t if gi % 2 == 0 else p_t2

    def stats_seq(vector):
        vector.tensor_scalar(mean[:, :], sumx[:, :], 1.0 / D, None, op0=ALU.mult)
        vector.tensor_scalar(vtmp[:, :], sumsq[:, :], 1.0 / D, None, op0=ALU.mult)
        vector.tensor_tensor(msq[:, :], mean[:, :], mean[:, :], op=ALU.mult)
        vector.drain()
        return vector.tensor_tensor(var[:, :], vtmp[:, :], msq[:, :], op=ALU.subtract)

    sem_names = [
        "s0a", "s0b", "s0i", "x_a", "x_b", "x_c", "x_d", "kx_a", "kx_b", "sp_a", "sp_b",
        "qv_a", "qv_b", "x3_a", "x3_b", "sp3_a", "sp3_b", "ra_a", "ra_b",
        "du_a", "du_b", "dut_a", "dut_b", "ee_a", "ee_b", "fv_a", "fv_b",
        "fx_a", "fx_b", "fs_a", "fs_b",
        "s0_ld", "s0_v",
        "s1_x", "s1_sq", "s1_var", "s1_rstd", "s1_app", "s1_tr", "s1_ev",
        "s1_pmm", "s1_pev", "s1_sp",
        "k_x", "k_sq", "k_var", "k_rstd", "k_app", "k_tr", "k_ev", "k_mm", "k_kev",
        "s2_qv", "s2_smm", "s2_sp", "s2_pv", "s2_fold",
        "s3_x", "s3_s", "s3_v", "s3_tr", "s3_sp",
        "a_ld", "a_te", "a_ve",
        "r_mm", "r_ev", "r_sp", "r_tk",
        "c_w", "c_wt", "c_wf",
        "d_ld", "d_sq", "d_utc", "d_h", "d_wh",
        "e_ld", "e_cast", "e_m1", "e_g", "e_m2", "e_f",
        "f_ld", "f_sq", "f_vpp", "f_dyn", "f_ve", "f_sc", "f_x2", "f_st",
        "dbg",
    ]
    SEM = {n: nc.alloc_semaphore(name=n) for n in sem_names}
    lo = min(s.num for s in SEM.values())
    hi = max(s.num for s in SEM.values())
    nc.gpsimd.sem_clear(range(lo, hi + 1))
    nc.all_engine_barrier()

    with nc.Block("p0") as block:
        @block.sync
        def _(sync):
            sync.dma_start(out=stg[:, 5632:5760], in_=id_d[:, :]).then_inc(SEM["s0i"], 16)
            for wi, wd in enumerate((wq_d, wk_d, wv_d)):
                for c in range(DC):
                    gi = wi * DC + c
                    if gi >= 2:
                        sync.wait_ge(SEM["s0_v"], gi - 1)
                    sync.dma_start(out=xstg[gi % 2], in_=wd[_ts(c), :]
                                   ).then_inc(SEM["s0a" if gi % 2 == 0 else "s0b"], 16)

        @block.scalar
        def _(scalar):
            for fn in (AF.Exp, AF.Ln, AF.Gelu, AF.Tanh, AF.Square):
                scalar.activation(out=junk[0:1, 0:1], in_=junk[0:1, 0:1], func=fn,
                                  bias=zerob[0:1, :], scale=1.0)
                scalar.drain()

        @block.vector
        def _(vector):
            vector.memset(ones_b[:, :], 1.0)
            vector.memset(ones1f[:, :], 1.0)
            vector.memset(negB[:, :], -B_SHIFT)
            vector.memset(zerob[:, :], 0.0)
            vector.memset(eps5[:, :], 1e-5)
            vector.memset(onesf[:, :], 1.0)
            vector.memset(eps12[:, :], 1e-12)
            vector.wait_ge(SEM["s0i"], 16)
            vector.tensor_copy(ident_b[:, :], stg[:, 5632:5760])
            for wi in range(3):
                dst = (wqb, wkb, wvb)[wi]
                for c in range(DC):
                    gi = wi * DC + c
                    vector.wait_ge(SEM["s0a" if gi % 2 == 0 else "s0b"], 16 * (gi // 2 + 1))
                    vector.tensor_copy(dst[:, c * 1024:(c + 1) * 1024],
                                       xstg[gi % 2]).then_inc(SEM["s0_v"], 1)

    def emit_ln_sync(sync, n, src_d, sx, sapp):
        sxa, sxb = sx
        for gi in range(n):
            if gi >= 2:
                sync.wait_ge(SEM[sapp], gi - 1)
            sync.dma_start(out=xstg[gi % 2], in_=src_d[_ts(gi), :]
                           ).then_inc(SEM[sxa if gi % 2 == 0 else sxb], 16)

    def emit_ln_vector(vector, n, sx, ssq, svar, srstd, sapp, str_, sev, dst_of):
        sxa, sxb = sx
        for gi in range(n):
            xc = xstg[gi % 2]
            vector.wait_ge(SEM[sxa if gi % 2 == 0 else sxb], 16 * (gi // 2 + 1))
            vector.reduce_sum(sumx[:, :], xc, axis=AX)
            vector.drain()
            vector.wait_ge(SEM[ssq], gi + 1)
            stats_seq(vector).then_inc(SEM[svar], 1)
            vector.wait_ge(SEM[srstd], gi + 1)
            vector.tensor_scalar(xln[gi % 2], xc, mean[:, :], rstd[:, :],
                                 op0=ALU.subtract, op1=ALU.mult).then_inc(SEM[sapp], 1)
            vector.wait_ge(SEM[str_], gi + 1)
            vector.tensor_copy(dst_of(gi), ptb(gi).rearrange("p (d t) -> p d t", d=DC)
                               ).then_inc(SEM[sev], 1)

    def emit_ln_scalar(scalar, n, sx, ssq, svar, srstd):
        sxa, sxb = sx
        for gi in range(n):
            xc = xstg[gi % 2]
            scalar.wait_ge(SEM[sxa if gi % 2 == 0 else sxb], 16 * (gi // 2 + 1))
            scalar.activation(out=junk[:, :], in_=xc, func=AF.Square,
                              accum_out=sumsq[:, :]).then_inc(SEM[ssq], 1)
            scalar.wait_ge(SEM[svar], gi + 1)
            scalar.activation(out=rstd[:, :], in_=var[:, :], func=AF.Ln, bias=eps5[:, :], scale=1.0)
            scalar.drain()
            scalar.activation(out=rstd[:, :], in_=rstd[:, :], func=AF.Exp, bias=zerob[:, :],
                              scale=-0.5).then_inc(SEM[srstd], 1)

    def emit_tr_chunk(tensor, gi, sapp, sev, str_, extra_wait=None):
        tensor.wait_ge(SEM[sapp], gi + 1)
        if gi >= 2:
            tensor.wait_ge(SEM[sev], gi - 1)
        if extra_wait is not None:
            extra_wait(tensor, gi)
        for dcx in range(DC):
            ins = tensor.transpose(ptb(gi)[:, dcx * 128:(dcx + 1) * 128],
                                   xln[gi % 2][:, dcx * 128:(dcx + 1) * 128],
                                   ident_b[:, :])
            if dcx == DC - 1:
                ins.then_inc(SEM[str_], 1)

    if upto < 1:
        return nc
    with nc.Block("p1") as block:
        @block.sync
        def _(sync):
            for gi in range(XC):
                if gi >= 4:
                    sync.wait_ge(SEM["s1_app"], gi - 3)
                sync.dma_start(out=xstg4[gi % 4], in_=x_d[_ts(gi), :]
                               ).then_inc(SEM[XSEM[gi % 4]], 16)

        @block.scalar
        def _(scalar):
            for gi in range(XC):
                xc = xstg4[gi % 4]
                scalar.wait_ge(SEM[XSEM[gi % 4]], 16 * (gi // 4 + 1))
                scalar.activation(out=junk[:, :], in_=xc, func=AF.Square,
                                  accum_out=sumsq[:, :]).then_inc(SEM["s1_sq"], 1)
                scalar.wait_ge(SEM["s1_var"], gi + 1)
                scalar.activation(out=rstd[:, :], in_=var[:, :], func=AF.Ln, bias=eps5[:, :], scale=1.0)
                scalar.drain()
                scalar.activation(out=rstd[:, :], in_=rstd[:, :], func=AF.Exp, bias=zerob[:, :],
                                  scale=-0.5).then_inc(SEM["s1_rstd"], 1)

        @block.tensor
        def _(tensor):
            pg = 0
            for c in range(8):
                emit_tr_chunk(tensor, c, "s1_app", "s1_ev", "s1_tr")
            for s in range(NSUP):
                def maybe_trans(k, s_=s):
                    if s_ + 1 < NSUP and k % 2 == 1:
                        c2 = k // 2
                        emit_tr_chunk(tensor, (s_ + 1) * 8 + c2, "s1_app", "s1_ev", "s1_tr")
                tensor.wait_ge(SEM["s1_ev"], (s + 1) * 8)
                for e in range(DC):
                    pp = (p0, p1) if pg % 2 == 0 else (p2, p3)
                    if pg >= 2:
                        tensor.wait_ge(SEM["s1_pev"], pg - 1)
                    for dcx in range(DC):
                        for th in range(2):
                            ins = tensor.matmul(
                                pp[th][:, :],
                                wqb[:, dcx * 1024 + e * 128: dcx * 1024 + (e + 1) * 128],
                                xsup[s % 2][:, dcx * 1024 + th * 512: dcx * 1024 + th * 512 + 512],
                                start=(dcx == 0), stop=(dcx == DC - 1), **MK)
                            if dcx == DC - 1 and th == 1:
                                ins.then_inc(SEM["s1_pmm"], 1)
                    maybe_trans(pg % 16)
                    pg += 1
                for c in range(8):
                    pp = (p0, p1) if pg % 2 == 0 else (p2, p3)
                    if pg >= 2:
                        tensor.wait_ge(SEM["s1_pev"], pg - 1)
                    for dcx in range(DC):
                        for eh in range(2):
                            ins = tensor.matmul(
                                pp[eh][:, :],
                                xsup[s % 2][:, dcx * 1024 + c * 128: dcx * 1024 + (c + 1) * 128],
                                wvb[:, dcx * 1024 + eh * 512: dcx * 1024 + eh * 512 + 512],
                                start=(dcx == 0), stop=(dcx == DC - 1), **MK)
                            if dcx == DC - 1 and eh == 1:
                                ins.then_inc(SEM["s1_pmm"], 1)
                    maybe_trans(pg % 16)
                    pg += 1

        @block.vector
        def _(vector):
            def ln_chunk(s, c):
                gi = s * 8 + c
                xc = xstg4[gi % 4]
                vector.wait_ge(SEM[XSEM[gi % 4]], 16 * (gi // 4 + 1))
                vector.reduce_sum(sumx[:, :], xc, axis=AX)
                vector.drain()
                vector.wait_ge(SEM["s1_sq"], gi + 1)
                stats_seq(vector).then_inc(SEM["s1_var"], 1)
                vector.wait_ge(SEM["s1_rstd"], gi + 1)
                vector.tensor_scalar(xln[gi % 2], xc, mean[:, :], rstd[:, :],
                                     op0=ALU.subtract, op1=ALU.mult).then_inc(SEM["s1_app"], 1)
                vector.wait_ge(SEM["s1_tr"], gi + 1)
                if s >= 2 and c == 0:
                    vector.wait_ge(SEM["s1_pmm"], 16 * (s - 1))
                dst = xsup[s % 2].rearrange("p (d t) -> p d t", d=DC)[:, :, c * 128:(c + 1) * 128]
                vector.tensor_copy(dst, ptb(gi).rearrange("p (d t) -> p d t", d=DC)
                                   ).then_inc(SEM["s1_ev"], 1)

            for c in range(8):
                ln_chunk(0, c)
            for s in range(NSUP):
                for k in range(16):
                    pg = s * 16 + k
                    pp = (p0, p1) if pg % 2 == 0 else (p2, p3)
                    vector.wait_ge(SEM["s1_pmm"], pg + 1)
                    if pg >= 2:
                        vector.wait_ge(SEM["sp_a" if pg % 2 == 0 else "sp_b"], 16 * (pg // 2))
                    vector.tensor_copy(qsb[pg % 2][:, 0:512], pp[0][:, :])
                    vector.tensor_copy(qsb[pg % 2][:, 512:1024], pp[1][:, :]
                                       ).then_inc(SEM["s1_pev"], 1)
                    if s + 1 < NSUP and k % 2 == 1:
                        ln_chunk(s + 1, k // 2)

        @block.gpsimd
        def _(gpsimd):
            if debug:
                gpsimd.wait_ge(SEM["s1_ev"], 8)
                gpsimd.dma_start(out=xsdbg_d[:, :], in_=xsup[0]).then_inc(SEM["dbg"], 16)
                gpsimd.dma_start(out=stdbg_d[:, :], in_=stdbg[:, :]).then_inc(SEM["dbg"], 16)
                gpsimd.dma_start(out=wqdbg_d[:, :], in_=wqb).then_inc(SEM["dbg"], 16)
            for s in range(NSUP):
                for e in range(DC):
                    pg = s * 16 + e
                    gpsimd.wait_ge(SEM["s1_pev"], pg + 1)
                    gpsimd.dma_start(out=qbuf_d[s, e, :, :], in_=qsb[pg % 2]
                                     ).then_inc(SEM["sp_a" if pg % 2 == 0 else "sp_b"], 16)
                for c in range(8):
                    pg = s * 16 + 8 + c
                    gpsimd.wait_ge(SEM["s1_pev"], pg + 1)
                    gpsimd.dma_start(out=vbuf_d[s * 8 + c, :, :], in_=qsb[pg % 2]
                                     ).then_inc(SEM["sp_a" if pg % 2 == 0 else "sp_b"], 16)
            gpsimd.wait_ge(SEM["sp_a"], 16 * NSUP * 8)
            gpsimd.wait_ge(SEM["sp_b"], 16 * NSUP * 8)

    if upto < 2:
        return nc
    with nc.Block("p1k") as block:
        @block.sync
        def _(sync):
            emit_ln_sync(sync, TC, xj_d, ("kx_a", "kx_b"), "k_app")

        @block.scalar
        def _(scalar):
            emit_ln_scalar(scalar, TC, ("kx_a", "kx_b"), "k_sq", "k_var", "k_rstd")

        @block.tensor
        def _(tensor):
            for gi in range(TC):
                emit_tr_chunk(tensor, gi, "k_app", "k_ev", "k_tr")
            tensor.wait_ge(SEM["k_ev"], TC)
            kg = 0
            for e in range(DC):
                for tb in range(4):
                    pp = p0 if kg % 2 == 0 else p1
                    if kg >= 2:
                        tensor.wait_ge(SEM["k_kev"], kg - 1)
                    for dcx in range(DC):
                        ins = tensor.matmul(
                            pp[:, :],
                            wkb[:, dcx * 1024 + e * 128: dcx * 1024 + (e + 1) * 128],
                            xjf[:, dcx * 2048 + tb * 512: dcx * 2048 + tb * 512 + 512],
                            start=(dcx == 0), stop=(dcx == DC - 1), **MK)
                        if dcx == DC - 1:
                            ins.then_inc(SEM["k_mm"], 1)
                    kg += 1

        @block.vector
        def _(vector):
            emit_ln_vector(vector, TC, ("kx_a", "kx_b"), "k_sq", "k_var", "k_rstd", "k_app", "k_tr", "k_ev",
                           lambda gi: xjf.rearrange("p (d t) -> p d t", d=DC)[:, :, gi * 128:(gi + 1) * 128])
            for kg in range(DC * 4):
                e, tb = kg // 4, kg % 4
                pp = p0 if kg % 2 == 0 else p1
                vector.wait_ge(SEM["k_mm"], kg + 1)
                vector.tensor_copy(kfm(e)[:, tb * 512:tb * 512 + 512], pp[:, :]
                                   ).then_inc(SEM["k_kev"], 1)

        if debug:
            @block.gpsimd
            def _(gpsimd):
                gpsimd.wait_ge(SEM["k_kev"], DC * 4)
                for e in range(DC):
                    gpsimd.dma_start(out=kdbg_d[e, :, :], in_=kfm(e)).then_inc(SEM["dbg"], 16)
                gpsimd.wait_ge(SEM["dbg"], 16 * DC)

    if upto < 3:
        return nc
    OA = (p0, p1)
    OB = (p2, p3)
    SBK = (p4, p5)

    with nc.Block("p2") as block:
        @block.sync
        def _(sync):
            for s in range(16):
                if s >= 2:
                    sync.wait_ge(SEM["s2_pv"], 16 * (s - 1))
                qdst = qstream[:, (s % 2) * 4096:(s % 2) * 4096 + 4096]
                vdst = vstream[:, (s % 2) * 4096:(s % 2) * 4096 + 4096]
                qs = "qv_a" if s % 2 == 0 else "qv_b"
                sync.dma_start(out=qdst.rearrange("p (e t) -> p e t", e=DC),
                               in_=qbuf_d[s // 2, :, :, (s % 2) * 512:(s % 2) * 512 + 512]
                               .transpose([1, 0, 2])).then_inc(SEM[qs], 16)
                sync.dma_start(out=vdst.rearrange("p (c e) -> p c e", c=4),
                               in_=vbuf_d[s * 4:(s + 1) * 4].transpose([1, 0, 2])
                               ).then_inc(SEM[qs], 16)

        @block.tensor
        def _(tensor):
            def emit_S(g):
                s = g // 16
                jb = (g // 2) % 8
                t = g % 2
                sb = (s % 2) * 4096
                if t == 0 and jb == 0:
                    tensor.wait_ge(SEM["qv_a" if s % 2 == 0 else "qv_b"], 32 * (s // 2 + 1))
                if g >= 2:
                    tensor.wait_ge(SEM["s2_sp"], g - 1)
                for icl in range(2):
                    ic = t * 2 + icl
                    for e in range(DC):
                        ins = tensor.matmul(
                            SBK[g % 2][:, icl * 256:icl * 256 + 256],
                            qstream[:, sb + e * 512 + ic * 128: sb + e * 512 + (ic + 1) * 128],
                            kfm(e)[:, jb * 256: jb * 256 + 256],
                            start=(e == 0), stop=(e == DC - 1), **MK)
                        if icl == 1 and e == DC - 1:
                            ins.then_inc(SEM["s2_smm"], 1)

            def emit_PV(g):
                G = g // 2
                s = g // 16
                t = g % 2
                sb = (s % 2) * 4096
                tensor.wait_ge(SEM["s2_sp"], g + 1)
                if t == 0:
                    tensor.wait_ge(SEM["s2_fold"], G)
                par = G % 2
                for icl in range(2):
                    ic = t * 2 + icl
                    for jsub in range(2):
                        lhsT = ppair[g % 2][:, icl * 256 + jsub * 128: icl * 256 + (jsub + 1) * 128]
                        ob = OA if jsub == 0 else OB
                        for eh in range(2):
                            tensor.matmul(ob[eh][:, :], lhsT,
                                          vstream[:, sb + ic * 1024 + eh * 512: sb + ic * 1024 + eh * 512 + 512],
                                          start=(ic == 0), stop=(ic == 3), **MK)
                        slot = par * 16 + jsub * 8 + ic
                        ins = tensor.matmul(p6[:, slot:slot + 1], lhsT, ones_b[:, :],
                                            start=True, stop=True, **MK)
                        if icl == 1 and jsub == 1:
                            ins.then_inc(SEM["s2_pv"], 1)

            for s in range(16):
                base = s * 16
                emit_S(base)
                for t in range(15):
                    emit_S(base + t + 1)
                    emit_PV(base + t)
                emit_PV(base + 15)

        @block.scalar
        def _(scalar):
            for g in range(256):
                scalar.wait_ge(SEM["s2_smm"], g + 1)
                if g >= 2:
                    scalar.wait_ge(SEM["s2_pv"], g - 1)
                scalar.activation(out=ppair[g % 2][:, :], in_=SBK[g % 2][:, :], func=AF.Exp,
                                  bias=negB[:, :], scale=SCALE_EXP).then_inc(SEM["s2_sp"], 1)

        @block.vector
        def _(vector):
            for G in range(16 * 8):
                s, jb = G // 8, G % 8
                par = G % 2
                vector.wait_ge(SEM["s2_pv"], 2 * (G + 1))
                for jsub in range(2):
                    jc = jb * 2 + jsub
                    ob = OA if jsub == 0 else OB
                    ltx = lt if jsub == 0 else lt1
                    sl = par * 16 + jsub * 8
                    vector.reduce_sum(ltx[:, :], p6[:, sl:sl + 4], axis=AX)
                    for eh in range(2):
                        dst = O_sb[:, jc * 1024 + eh * 512: jc * 1024 + eh * 512 + 512]
                        if s == 0:
                            vector.tensor_copy(dst, ob[eh][:, :])
                        else:
                            vector.tensor_tensor(dst, dst, ob[eh][:, :], op=ALU.add)
                    if s == 0:
                        ins = vector.tensor_copy(l_sb[:, jc:jc + 1], ltx[:, :])
                    else:
                        ins = vector.tensor_tensor(l_sb[:, jc:jc + 1], l_sb[:, jc:jc + 1],
                                                   ltx[:, :], op=ALU.add)
                    if jsub == 1:
                        ins.then_inc(SEM["s2_fold"], 1)

    if upto < 4:
        return nc
    with nc.Block("p3") as block:
        @block.sync
        def _(sync):
            for tc in range(TC):
                if tc >= 2:
                    sync.wait_ge(SEM["s3_v"], 9 * (tc - 2) + 3)
                sync.dma_start(out=xres[tc % 2], in_=xj_d[_ts(tc), :]
                               ).then_inc(SEM["x3_a" if tc % 2 == 0 else "x3_b"], 16)

        @block.vector
        def _(vector):
            vector.reciprocal(linv[:, :], l_sb[:, :])
            vector.drain()
            for tc in range(TC):
                oc = O_sb[:, tc * 1024:(tc + 1) * 1024]
                vector.tensor_scalar(oc, oc, linv[:, tc:tc + 1], None, op0=ALU.mult
                                     ).then_inc(SEM["s3_v"], 1)
                vector.drain()
                vector.reduce_sum(sumx[:, :], oc, axis=AX)
                vector.drain()
                vector.wait_ge(SEM["s3_s"], 6 * tc + 1)
                stats_seq(vector).then_inc(SEM["s3_v"], 1)
                vector.wait_ge(SEM["s3_s"], 6 * tc + 2)
                if debug:
                    vector.wait_ge(SEM["dbg"], 16 * (DC + 1 + tc + 1))
                vector.tensor_scalar(oc, oc, mean[:, :], rstd[:, :],
                                     op0=ALU.subtract, op1=ALU.mult)
                vector.drain()
                vector.wait_ge(SEM["x3_a" if tc % 2 == 0 else "x3_b"], 16 * (tc // 2 + 1))
                if tc >= 2:
                    vector.wait_ge(SEM["sp3_a" if tc % 2 == 0 else "sp3_b"], 16 * (tc // 2))
                vector.tensor_tensor(x2sb[tc % 2], oc, xres[tc % 2], op=ALU.add
                                     ).then_inc(SEM["s3_v"], 1)
                vector.drain()
                vector.reduce_sum(sumx[:, :], x2sb[tc % 2], axis=AX)
                vector.drain()
                vector.wait_ge(SEM["s3_s"], 6 * tc + 3)
                stats_seq(vector).then_inc(SEM["s3_v"], 1)
                vector.wait_ge(SEM["s3_s"], 6 * tc + 4)
                vector.tensor_scalar(xf32, x2sb[tc % 2], mean[:, :], rstd[:, :],
                                     op0=ALU.subtract, op1=ALU.mult)
                vector.drain()
                vector.tensor_copy(xf_tm, xf32).then_inc(SEM["s3_v"], 1)
                vector.reduce_sum(sumx[:, :], xf32, axis=AX)
                vector.drain()
                vector.wait_ge(SEM["s3_s"], 6 * tc + 5)
                stats_seq(vector).then_inc(SEM["s3_v"], 1)
                vector.wait_ge(SEM["s3_s"], 6 * tc + 6)
                vector.tensor_scalar(z_tm, xf32, mean[:, :], rstd[:, :],
                                     op0=ALU.subtract, op1=ALU.mult).then_inc(SEM["s3_v"], 1)
                vector.wait_ge(SEM["s3_tr"], 2 * tc + 1)
                vector.tensor_copy(
                    xf_fm.rearrange("p (d t) -> p d t", d=DC)[:, :, tc * 128:(tc + 1) * 128],
                    p_t.rearrange("p (d t) -> p d t", d=DC)).then_inc(SEM["s3_v"], 1)
                vector.wait_ge(SEM["s3_tr"], 2 * tc + 2)
                vector.tensor_copy(
                    z_fm.rearrange("p (d t) -> p d t", d=DC)[:, :, tc * 128:(tc + 1) * 128],
                    p_t2.rearrange("p (d t) -> p d t", d=DC)).then_inc(SEM["s3_v"], 1)

        @block.scalar
        def _(scalar):
            for tc in range(TC):
                for j, (srcv, wv_) in enumerate(((O_sb[:, tc * 1024:(tc + 1) * 1024], 9 * tc + 1),
                                                 (x2sb[tc % 2], 9 * tc + 3),
                                                 (xf32, 9 * tc + 5))):
                    scalar.wait_ge(SEM["s3_v"], wv_)
                    scalar.activation(out=junk[:, :], in_=srcv, func=AF.Square,
                                      accum_out=sumsq[:, :]).then_inc(SEM["s3_s"], 1)
                    scalar.wait_ge(SEM["s3_v"], 9 * tc + 2 + 2 * j)
                    scalar.activation(out=rstd[:, :], in_=var[:, :], func=AF.Ln,
                                      bias=eps5[:, :], scale=1.0)
                    scalar.drain()
                    scalar.activation(out=rstd[:, :], in_=rstd[:, :], func=AF.Exp,
                                      bias=zerob[:, :], scale=-0.5).then_inc(SEM["s3_s"], 1)

        @block.tensor
        def _(tensor):
            for tc in range(TC):
                tensor.wait_ge(SEM["s3_v"], 9 * tc + 5)
                if tc > 0:
                    tensor.wait_ge(SEM["s3_v"], 9 * (tc - 1) + 8)
                for dcx in range(DC):
                    ins = tensor.transpose(p_t[:, dcx * 128:(dcx + 1) * 128],
                                           xf_tm[:, dcx * 128:(dcx + 1) * 128], ident_b[:, :])
                    if dcx == DC - 1:
                        ins.then_inc(SEM["s3_tr"], 1)
                tensor.wait_ge(SEM["s3_v"], 9 * tc + 7)
                if tc > 0:
                    tensor.wait_ge(SEM["s3_v"], 9 * (tc - 1) + 9)
                for dcx in range(DC):
                    ins = tensor.transpose(p_t2[:, dcx * 128:(dcx + 1) * 128],
                                           z_tm[:, dcx * 128:(dcx + 1) * 128], ident_b[:, :])
                    if dcx == DC - 1:
                        ins.then_inc(SEM["s3_tr"], 1)

        @block.gpsimd
        def _(gpsimd):
            if debug:
                gpsimd.wait_ge(SEM["s2_fold"], 64)
                gpsimd.dma_start(out=ldbg_d[:, :], in_=l_sb[:, :]).then_inc(SEM["dbg"], 16)
            for tc in range(TC):
                if debug:
                    gpsimd.wait_ge(SEM["s3_v"], 9 * tc + 1)
                    gpsimd.dma_start(out=odbg_d[tc, :, :], in_=O_sb[:, tc * 1024:(tc + 1) * 1024]
                                     ).then_inc(SEM["dbg"], 16)
                gpsimd.wait_ge(SEM["s3_v"], 9 * tc + 3)
                gpsimd.dma_start(out=x2buf_d[tc, :, :], in_=x2sb[tc % 2]
                                 ).then_inc(SEM["sp3_a" if tc % 2 == 0 else "sp3_b"], 16)
            gpsimd.wait_ge(SEM["sp3_a"], 16 * TC // 2)
            gpsimd.wait_ge(SEM["sp3_b"], 16 * TC // 2)

    if upto < 5:
        return nc
    try:
        p4block = nc.Block("p4")
        p4block.__enter__()
        block = p4block
        @block.sync
        def _(sync):
            for c in range(DC):
                sync.dma_start(out=g_t[:, c:c + 1], in_=lng_d[c, :, :]).then_inc(SEM["a_ld"], 16)
                sync.dma_start(out=b_t[:, c:c + 1], in_=lnb_d[c, :, :]).then_inc(SEM["a_ld"], 16)
            sync.dma_start(out=rb_row[:, :], in_=rb_d[:, :]).then_inc(SEM["a_ld"], 16)
            sync.dma_start(out=grow[:, :], in_=gam_d[:, :]).then_inc(SEM["a_ld"], 16)
            for c in range(DC):
                if c >= 2:
                    sync.wait_ge(SEM["a_ve"], c - 1)
                    sync.wait_ge(SEM["a_te"], c)
                sync.dma_start(out=wstage[c % 2][:, 0:512], in_=rwt_d[_ts(c), :]
                               ).then_inc(SEM["ra_a" if c % 2 == 0 else "ra_b"], 16)

        @block.tensor
        def _(tensor):
            tensor.wait_ge(SEM["a_ld"], 16 * 18)
            tensor.matmul(p4[:, :], ones1f[:, :], grow[:, 0:512], start=True, stop=True, **MK)
            tensor.matmul(p5[:, :], ones1f[:, :], grow[:, 512:1024], start=True, stop=True, **MK
                          ).then_inc(SEM["a_te"], 1)
            for c in range(DC):
                tensor.wait_ge(SEM["ra_a" if c % 2 == 0 else "ra_b"], 16 * (c // 2 + 1))
                tensor.matmul(p6[0:1, :], b_t[:, c:c + 1], wstage[c % 2][:, 0:512],
                              start=(c == 0), stop=(c == DC - 1), **MK).then_inc(SEM["a_te"], 1)
            tensor.wait_ge(SEM["a_ve"], DC + 2)
            tensor.matmul(p4[:, :], ones1f[:, :], brow[:, :], start=True, stop=True, **MK
                          ).then_inc(SEM["a_te"], 1)

        @block.vector
        def _(vector):
            vector.wait_ge(SEM["a_te"], 1)
            vector.tensor_copy(gbc[:, 0:512], p4[:, :])
            vector.tensor_copy(gbc[:, 512:1024], p5[:, :])
            for c in range(DC):
                vector.wait_ge(SEM["ra_a" if c % 2 == 0 else "ra_b"], 16 * (c // 2 + 1))
                vector.wait_ge(SEM["a_te"], c + 2)
                vector.tensor_scalar(rw_b[:, c * 512:(c + 1) * 512], wstage[c % 2][:, 0:512],
                                     g_t[:, c:c + 1], None, op0=ALU.mult).then_inc(SEM["a_ve"], 1)
            vector.wait_ge(SEM["a_te"], DC + 1)
            vector.tensor_copy(brow[:, :], p6[0:1, :]).then_inc(SEM["a_ve"], 1)
            vector.drain()
            vector.tensor_tensor(brow[:, :], brow[:, :], rb_row[:, :], op=ALU.add
                                 ).then_inc(SEM["a_ve"], 1)
            vector.wait_ge(SEM["a_te"], DC + 2)
            vector.tensor_copy(bbc[:, :], p4[:, :]).then_inc(SEM["a_ve"], 1)

        if p4stage < 1:
            raise StopPhase()
        if debug:
            @block.gpsimd
            def _(gpsimd):
                gpsimd.dma_start(out=xfdbg_d[:, :], in_=xf_fm).then_inc(SEM["dbg"], 16)
                gpsimd.dma_start(out=zfdbg_d[:, :], in_=z_fm).then_inc(SEM["dbg"], 16)

        @block.tensor
        def _(tensor):
            for tc in range(TC):
                pp = p4 if tc % 2 == 0 else p5
                if tc == 0:
                    tensor.wait_ge(SEM["a_ve"], DC + 3)
                if tc >= 2:
                    tensor.wait_ge(SEM["r_ev"], 2 * (tc - 2) + 1)
                for dcx in range(DC):
                    ins = tensor.matmul(
                        pp[:, :],
                        z_fm[:, dcx * 2048 + tc * 128: dcx * 2048 + (tc + 1) * 128],
                        rw_b[:, dcx * 512:(dcx + 1) * 512],
                        start=(dcx == 0), stop=(dcx == DC - 1), **MK)
                    if dcx == DC - 1:
                        ins.then_inc(SEM["r_mm"], 1)

        @block.vector
        def _(vector):
            for tc in range(TC):
                pp = p4 if tc % 2 == 0 else p5
                vector.wait_ge(SEM["r_mm"], tc + 1)
                if tc >= 2:
                    vector.wait_ge(SEM["r_sp"], tc - 1)
                vector.tensor_tensor(clipb[tc % 2], pp[:, :], bbc[:, :], op=ALU.add
                                     ).then_inc(SEM["r_ev"], 1)
                vector.drain()
                vector.tensor_scalar(clipb[tc % 2], clipb[tc % 2], -10.0, 10.0,
                                     op0=ALU.max, op1=ALU.min).then_inc(SEM["r_ev"], 1)
                vector.wait_ge(SEM["r_sp"], tc + 1)
                vector.max(out=top8[:, :], in_=alphab[tc % 2])
                vector.drain()
                vector.reduce_sum(SSt[:, tc:tc + 1], top8[:, :], axis=AX)
                vector.tensor_scalar(mge[:, :], alphab[tc % 2], top8[:, 7:8], None, op0=ALU.is_ge)
                vector.drain()
                vector.tensor_tensor(wmask[:, tc * 512:(tc + 1) * 512], alphab[tc % 2],
                                     mge[:, :], op=ALU.mult).then_inc(SEM["r_tk"], 1)

        @block.scalar
        def _(scalar):
            for tc in range(TC):
                scalar.wait_ge(SEM["r_ev"], 2 * tc + 2)
                if tc >= 2:
                    scalar.wait_ge(SEM["r_tk"], tc - 1)
                    if debug:
                        scalar.wait_ge(SEM["dbg"], 16 * (DC + 1 + TC) + 16 * (tc - 1))
                scalar.activation(out=alphab[tc % 2], in_=clipb[tc % 2], func=AF.Exp,
                                  bias=zerob[:, :], scale=1.0)
                scalar.drain()
                scalar.activation(out=alphab[tc % 2], in_=alphab[tc % 2], func=AF.Ln,
                                  bias=onesf[:, :], scale=1.0).then_inc(SEM["r_sp"], 1)

        if debug:
            @block.gpsimd
            def _(gpsimd):
                for tc in range(TC):
                    gpsimd.wait_ge(SEM["r_sp"], tc + 1)
                    gpsimd.dma_start(out=adbg_d[tc, :, :], in_=alphab[tc % 2]).then_inc(SEM["dbg"], 16)

        if p4stage < 2:
            raise StopPhase()
        @block.scalar
        def _(scalar):
            scalar.wait_ge(SEM["r_tk"], TC)
            scalar.activation(out=fac[:, :], in_=SSt[:, :], func=AF.Tanh, bias=zerob[:, :],
                              scale=1.0).then_inc(SEM["r_sp"], 1)

        @block.vector
        def _(vector):
            vector.wait_ge(SEM["r_sp"], TC + 1)
            vector.tensor_scalar(SS2[:, :], SSt[:, :], 1e-6, None, op0=ALU.add)
            vector.drain()
            vector.reciprocal(SS2[:, :], SS2[:, :])
            vector.drain()
            vector.tensor_tensor(fac[:, :], fac[:, :], SS2[:, :], op=ALU.mult)
            vector.drain()
            for tc in range(TC):
                vector.tensor_scalar(w_tm[:, tc * 512:(tc + 1) * 512],
                                     wmask[:, tc * 512:(tc + 1) * 512],
                                     fac[:, tc:tc + 1], None, op0=ALU.mult).then_inc(SEM["c_w"], 1)

        @block.tensor
        def _(tensor):
            for tc in range(TC):
                tensor.wait_ge(SEM["c_w"], tc + 1)
                if tc >= 2:
                    tensor.wait_ge(SEM["c_wf"], tc - 1)
                bank = ptb(tc)
                for us in range(4):
                    ins = tensor.transpose(bank[:, us * 128:(us + 1) * 128],
                                           w_tm[:, tc * 512 + us * 128: tc * 512 + (us + 1) * 128],
                                           ident_b[:, :])
                    if us == 3:
                        ins.then_inc(SEM["c_wt"], 1)

        @block.vector
        def _(vector):
            for tc in range(TC):
                vector.wait_ge(SEM["c_wt"], tc + 1)
                vector.tensor_copy(
                    w_fmr.rearrange("p (u t) -> p u t", u=4)[:, :, tc * 128:(tc + 1) * 128],
                    ptb(tc)[:, 0:512].rearrange("p (u t) -> p u t", u=4)
                    ).then_inc(SEM["c_wf"], 1)

        if debug:
            @block.gpsimd
            def _(gpsimd):
                gpsimd.wait_ge(SEM["c_wf"], TC)
                for us in range(4):
                    gpsimd.dma_start(out=wdbg_d[us, :, :],
                                     in_=w_fmr[:, us * 2048:(us + 1) * 2048]).then_inc(SEM["dbg"], 16)

        if p4stage < 3:
            raise StopPhase()
        @block.sync
        def _(sync):
            sync.wait_ge(SEM["a_ve"], DC)
            sync.wait_ge(SEM["a_te"], DC + 1)
            for uc in range(4):
                if uc >= 2:
                    sync.wait_ge(SEM["d_sq"], uc - 1)
                sync.dma_start(out=wstage[uc % 2], in_=u_d[_ts(uc), :]
                               ).then_inc(SEM["du_a" if uc % 2 == 0 else "du_b"], 16)
            sync.wait_ge(SEM["r_sp"], TC)
            for uc in range(4):
                for dcx in range(DC):
                    gi = uc * DC + dcx
                    if gi >= 2:
                        sync.wait_ge(SEM["d_utc"], gi - 1)
                    sync.dma_start(out=utstg[gi % 2], in_=ut_d[_ts(dcx), _ts(uc)]
                                   ).then_inc(SEM["dut_a" if gi % 2 == 0 else "dut_b"], 16)

        @block.scalar
        def _(scalar):
            for uc in range(4):
                scalar.wait_ge(SEM["du_a" if uc % 2 == 0 else "du_b"], 16 * (uc // 2 + 1))
                scalar.activation(out=junk[:, :], in_=wstage[uc % 2], func=AF.Square,
                                  accum_out=su2[:, uc:uc + 1]).then_inc(SEM["d_sq"], 1)
            scalar.activation(out=su_t[:, :], in_=su2[:, :], func=AF.Ln, bias=eps12[:, :], scale=1.0)
            scalar.drain()
            scalar.activation(out=su_t[:, :], in_=su_t[:, :], func=AF.Exp, bias=zerob[:, :],
                              scale=-0.5).then_inc(SEM["d_sq"], 1)

        @block.vector
        def _(vector):
            def stt_group(uc_):
                for tb in range(4):
                    hg = uc_ * 4 + tb
                    pp = p4 if hg % 2 == 0 else p5
                    vector.wait_ge(SEM["d_h"], hg + 1)
                    vector.scalar_tensor_tensor(
                        wh_r[:, uc_ * 2048 + tb * 512: uc_ * 2048 + tb * 512 + 512],
                        pp[:, :], su_t[:, uc_:uc_ + 1],
                        w_fmr[:, uc_ * 2048 + tb * 512: uc_ * 2048 + tb * 512 + 512],
                        op0=ALU.mult, op1=ALU.mult).then_inc(SEM["d_wh"], 1)

            vector.wait_ge(SEM["d_sq"], 5)
            for uc in range(4):
                for dcx in range(DC):
                    gi = uc * DC + dcx
                    vector.wait_ge(SEM["dut_a" if gi % 2 == 0 else "dut_b"], 16 * (gi // 2 + 1))
                    if gi >= 4 and dcx == 0:
                        vector.wait_ge(SEM["d_h"], 4 * (uc - 1))
                    vector.tensor_copy(ut_b[uc % 2][:, dcx * 128:(dcx + 1) * 128],
                                       utstg[gi % 2]).then_inc(SEM["d_utc"], 1)
                if uc >= 1:
                    stt_group(uc - 1)
            stt_group(3)

        @block.tensor
        def _(tensor):
            tensor.wait_ge(SEM["d_sq"], 5)
            tensor.wait_ge(SEM["c_wf"], TC)
            hg = 0
            for uc in range(4):
                tensor.wait_ge(SEM["d_utc"], DC * (uc + 1))
                for tb in range(4):
                    pp = p4 if hg % 2 == 0 else p5
                    if hg >= 2:
                        tensor.wait_ge(SEM["d_wh"], hg - 1)
                    for dcx in range(DC):
                        ins = tensor.matmul(
                            pp[:, :], ut_b[uc % 2][:, dcx * 128:(dcx + 1) * 128],
                            xf_fm[:, dcx * 2048 + tb * 512: dcx * 2048 + tb * 512 + 512],
                            start=(dcx == 0), stop=(dcx == DC - 1), **MK)
                        if dcx == DC - 1:
                            ins.then_inc(SEM["d_h"], 1)
                    hg += 1

        if p4stage < 4:
            raise StopPhase()
        @block.sync
        def _(sync):
            sync.wait_ge(SEM["d_sq"], 5)
            for hs in range(8):
                for k in range(12):
                    gi = hs * 12 + k
                    if gi >= 2:
                        sync.wait_ge(SEM["e_cast"], gi - 1)
                    if k < 8:
                        src = w1t_d[_ts(k), hs * 512:(hs + 1) * 512]
                        dst = wstage[gi % 2][:, 0:512]
                    else:
                        src = w2t_d[_ts(hs * 4 + (k - 8)), :]
                        dst = wstage[gi % 2]
                    sync.dma_start(out=dst, in_=src
                                   ).then_inc(SEM["ee_a" if gi % 2 == 0 else "ee_b"], 16)

        @block.vector
        def _(vector):
            def fold_group(hs_):
                for k in range(16):
                    g2 = hs_ * 16 + k
                    tb, tc2 = k // 4, k % 4
                    tcg = tb * 4 + tc2
                    ob = OA if g2 % 2 == 0 else OB
                    vector.wait_ge(SEM["e_m2"], g2 + 1)
                    for dh in range(2):
                        dst = static[:, tcg * 1024 + dh * 512: tcg * 1024 + dh * 512 + 512]
                        if hs_ == 0:
                            ins = vector.tensor_copy(dst, ob[dh][:, :])
                        else:
                            ins = vector.tensor_tensor(dst, dst, ob[dh][:, :], op=ALU.add)
                        if dh == 1:
                            ins.then_inc(SEM["e_f"], 1)

            for hs in range(8):
                par = hs % 2
                for k in range(12):
                    gi = hs * 12 + k
                    vector.wait_ge(SEM["ee_a" if gi % 2 == 0 else "ee_b"], 16 * (gi // 2 + 1))
                    if hs >= 2 and k == 0:
                        vector.wait_ge(SEM["e_m2"], 16 * (hs - 1))
                    if k < 8:
                        vector.tensor_copy(w1sb2[par][:, k * 512:(k + 1) * 512],
                                           wstage[gi % 2][:, 0:512]).then_inc(SEM["e_cast"], 1)
                    else:
                        vector.tensor_copy(w2sb2[par][:, (k - 8) * 1024:(k - 7) * 1024],
                                           wstage[gi % 2]).then_inc(SEM["e_cast"], 1)
                if hs >= 1:
                    fold_group(hs - 1)
            fold_group(7)

        @block.tensor
        def _(tensor):
            tensor.wait_ge(SEM["d_wh"], 16)
            for hs in range(8):
                par = hs % 2
                tensor.wait_ge(SEM["e_cast"], 12 * (hs + 1))
                for tb in range(4):
                    gE = hs * 4 + tb
                    for hc in range(4):
                        pp = p4 if hc % 2 == 0 else p5
                        if gE * 4 + hc - 1 > 0:
                            tensor.wait_ge(SEM["e_g"], gE * 4 + hc - 1)
                        for dcx in range(DC):
                            ins = tensor.matmul(
                                pp[:, :], w1sb2[par][:, dcx * 512 + hc * 128: dcx * 512 + (hc + 1) * 128],
                                xf_fm[:, dcx * 2048 + tb * 512: dcx * 2048 + tb * 512 + 512],
                                start=(dcx == 0), stop=(dcx == DC - 1), **MK)
                            if dcx == DC - 1:
                                ins.then_inc(SEM["e_m1"], 1)
                    tensor.wait_ge(SEM["e_g"], 4 * (gE + 1))
                    for tc2 in range(4):
                        g2 = gE * 4 + tc2
                        ob = OA if g2 % 2 == 0 else OB
                        if g2 >= 2:
                            tensor.wait_ge(SEM["e_f"], g2 - 1)
                        for hc in range(4):
                            for dh in range(2):
                                ins = tensor.matmul(
                                    ob[dh][:, :],
                                    h1sb[:, hc * 512 + tc2 * 128: hc * 512 + (tc2 + 1) * 128],
                                    w2sb2[par][:, hc * 1024 + dh * 512: hc * 1024 + dh * 512 + 512],
                                    start=(hc == 0), stop=(hc == 3), **MK)
                                if hc == 3 and dh == 1:
                                    ins.then_inc(SEM["e_m2"], 1)

        @block.scalar
        def _(scalar):
            for gE in range(32):
                for hc in range(4):
                    pp = p4 if hc % 2 == 0 else p5
                    scalar.wait_ge(SEM["e_m1"], gE * 4 + hc + 1)
                    scalar.activation(out=h1sb[:, hc * 512:(hc + 1) * 512], in_=pp[:, :],
                                      func=AF.Gelu, bias=zerob[:, :], scale=1.0
                                      ).then_inc(SEM["e_g"], 1)

        if debug:
            @block.gpsimd
            def _(gpsimd):
                gpsimd.wait_ge(SEM["e_f"], 128)
                for tc in range(TC):
                    gpsimd.dma_start(out=sdbg_d[tc, :, :],
                                     in_=static[:, tc * 1024:(tc + 1) * 1024]).then_inc(SEM["dbg"], 16)

        if p4stage < 5:
            raise StopPhase()
        @block.sync
        def _(sync):
            sync.wait_ge(SEM["e_cast"], 96)
            for uc in range(4):
                if uc >= 2:
                    sync.wait_ge(SEM["f_sq"], uc - 1)
                sync.dma_start(out=wstage[uc % 2], in_=v_d[_ts(uc), :]
                               ).then_inc(SEM["fv_a" if uc % 2 == 0 else "fv_b"], 16)
            for uc in range(4):
                sync.wait_ge(SEM["f_sq"], min(uc + 2, 5))
                if uc >= 2:
                    sync.wait_ge(SEM["f_vpp"], uc - 1)
                sync.dma_start(out=wstage[uc % 2], in_=v_d[_ts(uc), :]
                               ).then_inc(SEM["fv_a" if uc % 2 == 0 else "fv_b"], 16)
            sync.wait_ge(SEM["f_vpp"], 4)
            for tc in range(TC):
                if tc >= 2:
                    sync.wait_ge(SEM["f_ve"], 3 * (tc - 2) + 3)
                sync.dma_start(out=xres[tc % 2], in_=x2buf_d[tc, :, :]
                               ).then_inc(SEM["fx_a" if tc % 2 == 0 else "fx_b"], 16)

        @block.scalar
        def _(scalar):
            for uc in range(4):
                scalar.wait_ge(SEM["fv_a" if uc % 2 == 0 else "fv_b"], 16 * (uc // 2 + 1))
                scalar.activation(out=junk[:, :], in_=wstage[uc % 2], func=AF.Square,
                                  accum_out=sv2[:, uc:uc + 1]).then_inc(SEM["f_sq"], 1)
            scalar.activation(out=sv_t[:, :], in_=sv2[:, :], func=AF.Ln, bias=eps12[:, :], scale=1.0)
            scalar.drain()
            scalar.activation(out=sv_t[:, :], in_=sv_t[:, :], func=AF.Exp, bias=zerob[:, :],
                              scale=-0.5).then_inc(SEM["f_sq"], 1)
            for tc in range(TC):
                scalar.wait_ge(SEM["f_ve"], 3 * tc + 1)
                scalar.activation(out=junk[:, :], in_=fsb, func=AF.Square,
                                  accum_out=sumsq[:, :]).then_inc(SEM["f_sc"], 1)
                scalar.wait_ge(SEM["f_ve"], 3 * tc + 2)
                scalar.activation(out=rstd[:, :], in_=var[:, :], func=AF.Ln, bias=eps5[:, :], scale=1.0)
                scalar.drain()
                scalar.activation(out=rstd[:, :], in_=rstd[:, :], func=AF.Exp, bias=zerob[:, :],
                                  scale=-0.5).then_inc(SEM["f_sc"], 1)

        @block.vector
        def _(vector):
            vector.wait_ge(SEM["f_sq"], 5)
            vector.wait_ge(SEM["e_f"], 128)
            for uc in range(4):
                vector.wait_ge(SEM["fv_a" if uc % 2 == 0 else "fv_b"], 16 * ((4 + uc) // 2 + 1))
                vector.scalar_tensor_tensor(vpp[:, uc * 1024:(uc + 1) * 1024],
                                            wstage[uc % 2], sv_t[:, uc:uc + 1], gbc[:, :],
                                            op0=ALU.mult, op1=ALU.mult).then_inc(SEM["f_vpp"], 1)
            for tc in range(TC):
                ob = OA if tc % 2 == 0 else OB
                vector.wait_ge(SEM["f_dyn"], tc + 1)
                for dh in range(2):
                    ins = vector.tensor_tensor(
                        fsb[:, dh * 512:dh * 512 + 512],
                        static[:, tc * 1024 + dh * 512: tc * 1024 + dh * 512 + 512],
                        ob[dh][:, :], op=ALU.add)
                    if dh == 1:
                        ins.then_inc(SEM["f_ve"], 1)
                vector.drain()
                vector.reduce_sum(sumx[:, :], fsb, axis=AX)
                vector.drain()
                vector.wait_ge(SEM["f_sc"], 2 * tc + 1)
                stats_seq(vector).then_inc(SEM["f_ve"], 1)
                vector.wait_ge(SEM["f_sc"], 2 * tc + 2)
                vector.tensor_scalar(fsb, fsb, mean[:, :], rstd[:, :],
                                     op0=ALU.subtract, op1=ALU.mult)
                vector.drain()
                vector.wait_ge(SEM["fx_a" if tc % 2 == 0 else "fx_b"], 16 * (tc // 2 + 1))
                if tc >= 2:
                    vector.wait_ge(SEM["fs_a" if tc % 2 == 0 else "fs_b"], 16 * (tc // 2))
                vector.tensor_tensor(ysb[tc % 2], fsb, xres[tc % 2], op=ALU.add
                                     ).then_inc(SEM["f_ve"], 1)

        @block.tensor
        def _(tensor):
            for tc in range(TC):
                ob = OA if tc % 2 == 0 else OB
                if tc == 0:
                    tensor.wait_ge(SEM["f_vpp"], 4)
                if tc >= 2:
                    tensor.wait_ge(SEM["f_ve"], 3 * (tc - 2) + 1)
                for uc in range(4):
                    for dh in range(2):
                        ins = tensor.matmul(
                            ob[dh][:, :], wh_r[:, uc * 2048 + tc * 128: uc * 2048 + (tc + 1) * 128],
                            vpp[:, uc * 1024 + dh * 512: uc * 1024 + dh * 512 + 512],
                            start=(uc == 0), stop=(uc == 3), **MK)
                        if uc == 3 and dh == 1:
                            ins.then_inc(SEM["f_dyn"], 1)

        @block.gpsimd
        def _(gpsimd):
            for tc in range(TC):
                gpsimd.wait_ge(SEM["f_ve"], 3 * tc + 3)
                gpsimd.dma_start(out=y_d[_ts(tc), :], in_=ysb[tc % 2]
                                 ).then_inc(SEM["fs_a" if tc % 2 == 0 else "fs_b"], 16)
            gpsimd.wait_ge(SEM["fs_a"], 16 * TC // 2)
            gpsimd.wait_ge(SEM["fs_b"], 16 * TC // 2)

        p4block.__exit__(None, None, None)
    except StopPhase:
        p4block.__exit__(None, None, None)
    return nc


SCALE_EXP = 1.0 / 32.0


@functools.lru_cache(maxsize=2)
def _get_nc(debug=False):
    return build_nc(debug=debug)


def build_upto(n, debug=True):
    return build_nc(debug=debug, upto=n)


def _prep_inputs(inputs, core):
    b = core // 4
    j0 = (core % 4) * JS
    x = np.asarray(inputs["L0_attn_x"], dtype=np.float32)
    c = np.ascontiguousarray
    return {
        "x": c(x[b]),
        "xj": c(x[b, j0:j0 + JS]),
        "wq": c(np.asarray(inputs["L0_attn_W_query"], np.float32)),
        "wk": c(np.asarray(inputs["L0_attn_W_key"], np.float32)),
        "wv": c(np.asarray(inputs["L0_attn_W_value"], np.float32)),
        "w1t": c(np.asarray(inputs["L0_ffn_W1"], np.float32).T),
        "w2t": c(np.asarray(inputs["L0_ffn_W2"], np.float32).T),
        "rwt": c(np.asarray(inputs["L0_ffn_router_W"], np.float32).T),
        "ut": c(np.asarray(inputs["L0_ffn_raw_U"], np.float32).T),
        "u": c(np.asarray(inputs["L0_ffn_raw_U"], np.float32)),
        "v": c(np.asarray(inputs["L0_ffn_raw_V"], np.float32)),
        "lng": c(np.asarray(inputs["L0_ffn_ln_g"], np.float32).reshape(DC, 128, 1)),
        "lnb": c(np.asarray(inputs["L0_ffn_ln_b"], np.float32).reshape(DC, 128, 1)),
        "rb": c(np.asarray(inputs["L0_ffn_router_b"], np.float32).reshape(1, NB)),
        "gamma": c(np.asarray(inputs["L0_ffn_gamma"], np.float32).reshape(1, D)),
        "ident": np.eye(128, dtype=np.float32),
    }


def kernel(**inputs):
    nc = _get_nc(False)
    in_maps = [_prep_inputs(inputs, c) for c in range(8)]
    res = run_bass_kernel_spmd(nc, in_maps, core_ids=list(range(8)))
    out = np.empty((2, SEQ, D), dtype=np.float32)
    for c in range(8):
        b = c // 4
        j0 = (c % 4) * JS
        out[b, j0:j0 + JS] = res.results[c]["y"]
    return out


# version 09
# speedup vs baseline: 1.1857x; 1.1289x over previous
import functools
import numpy as np

import concourse.bass as bass
import concourse.mybir as mybir
from concourse.bass_utils import run_bass_kernel_spmd

F32 = mybir.dt.float32
BF16 = mybir.dt.bfloat16
AF = mybir.ActivationFunctionType
ALU = mybir.AluOpType
AX = mybir.AxisListType.X

SEQ, D, HID, NB = 8192, 1024, 4096, 512
JS = 2048
NSUP, SUP = 8, 1024
B_SHIFT = 30.0
DC = D // 128
TC = JS // 128
XC = SEQ // 128
MK = dict(skip_group_check=True)


class StopPhase(Exception):
    pass


def _ts(i, n=128):
    return slice(i * n, (i + 1) * n)


def build_nc(debug=False, upto=5, p4stage=9):
    nc = bass.Bass("TRN2", target_bir_lowering=False)

    x_d = nc.dram_tensor("x", [SEQ, D], F32, kind="ExternalInput").ap()
    xj_d = nc.dram_tensor("xj", [JS, D], F32, kind="ExternalInput").ap()
    wq_d = nc.dram_tensor("wq", [D, D], F32, kind="ExternalInput").ap()
    wk_d = nc.dram_tensor("wk", [D, D], F32, kind="ExternalInput").ap()
    wv_d = nc.dram_tensor("wv", [D, D], F32, kind="ExternalInput").ap()
    w1t_d = nc.dram_tensor("w1t", [D, HID], F32, kind="ExternalInput").ap()
    w2t_d = nc.dram_tensor("w2t", [HID, D], F32, kind="ExternalInput").ap()
    rwt_d = nc.dram_tensor("rwt", [D, NB], F32, kind="ExternalInput").ap()
    ut_d = nc.dram_tensor("ut", [D, NB], F32, kind="ExternalInput").ap()
    u_d = nc.dram_tensor("u", [NB, D], F32, kind="ExternalInput").ap()
    v_d = nc.dram_tensor("v", [NB, D], F32, kind="ExternalInput").ap()
    lng_d = nc.dram_tensor("lng", [DC, 128, 1], F32, kind="ExternalInput").ap()
    lnb_d = nc.dram_tensor("lnb", [DC, 128, 1], F32, kind="ExternalInput").ap()
    rb_d = nc.dram_tensor("rb", [1, NB], F32, kind="ExternalInput").ap()
    gam_d = nc.dram_tensor("gamma", [1, D], F32, kind="ExternalInput").ap()
    id_d = nc.dram_tensor("ident", [128, 128], F32, kind="ExternalInput").ap()
    y_d = nc.dram_tensor("y", [JS, D], F32, kind="ExternalOutput").ap()

    ikind = "ExternalOutput" if debug else "Internal"
    qbuf_d = nc.dram_tensor("qbuf", [NSUP, DC, 128, SUP], BF16, kind=ikind).ap()
    vbuf_d = nc.dram_tensor("vbuf", [XC, 128, D], BF16, kind=ikind).ap()
    x2buf_d = nc.dram_tensor("x2buf", [TC, 128, D], F32, kind=ikind).ap()
    if debug:
        odbg_d = nc.dram_tensor("odbg", [TC, 128, D], F32, kind="ExternalOutput").ap()
        ldbg_d = nc.dram_tensor("ldbg", [128, TC], F32, kind="ExternalOutput").ap()
        kdbg_d = nc.dram_tensor("kdbg", [DC, 128, JS], BF16, kind="ExternalOutput").ap()
        adbg_d = nc.dram_tensor("adbg", [TC, 128, NB], F32, kind="ExternalOutput").ap()
        wdbg_d = nc.dram_tensor("wdbg", [4, 128, JS], BF16, kind="ExternalOutput").ap()
        sdbg_d = nc.dram_tensor("sdbg", [TC, 128, D], F32, kind="ExternalOutput").ap()
        xsdbg_d = nc.dram_tensor("xsdbg", [128, 8192], BF16, kind="ExternalOutput").ap()
        stdbg_d = nc.dram_tensor("stdbg", [128, 8], F32, kind="ExternalOutput").ap()
        xfdbg_d = nc.dram_tensor("xfdbg", [128, 16384], BF16, kind="ExternalOutput").ap()
        zfdbg_d = nc.dram_tensor("zfdbg", [128, 16384], BF16, kind="ExternalOutput").ap()
        wqdbg_d = nc.dram_tensor("wqdbg", [128, 8192], BF16, kind="ExternalOutput").ap()

    BB = nc.alloc_sbuf_tensor("BB", [128, 49152], BF16).ap()
    FF = nc.alloc_sbuf_tensor("FF", [128, 16384], F32).ap()
    stg = nc.alloc_sbuf_tensor("stg", [128, 6144], F32).ap()
    junk = nc.alloc_sbuf_tensor("junkb", [128, 1024], BF16).ap()

    A = nc.alloc_sbuf_tensor
    ident_b = A("ident_b", [128, 128], BF16).ap()
    ones_b = A("ones_b", [128, 1], BF16).ap()
    ones1f = A("ones1f", [1, 128], F32).ap()
    negB = A("negB", [128, 1], F32).ap()
    zerob = A("zerob", [128, 1], F32).ap()
    eps5 = A("eps5", [128, 1], F32).ap()
    onesf = A("onesf", [128, 1], F32).ap()
    eps12 = A("eps12", [128, 1], F32).ap()
    g_t = A("g_t", [128, DC], F32).ap()
    b_t = A("b_t", [128, DC], F32).ap()
    su2 = A("su2", [128, 4], F32).ap()
    su_t = A("su_t", [128, 4], F32).ap()
    sv2 = A("sv2", [128, 4], F32).ap()
    sv_t = A("sv_t", [128, 4], F32).ap()
    SSt = A("SSt", [128, 16], F32).ap()
    SS2 = A("SS2", [128, 16], F32).ap()
    fac = A("fac", [128, 16], F32).ap()
    l_sb = A("l_sb", [128, 16], F32).ap()
    top8 = A("top8", [128, 8], F32).ap()
    rb_row = A("rb_row", [1, NB], F32).ap()
    brow = A("brow", [1, NB], F32).ap()
    grow = A("grow", [1, D], F32).ap()
    sumx = A("sumx", [128, 1], F32).ap()
    sumsq = A("sumsq", [128, 1], F32).ap()
    mean = A("mean", [128, 1], F32).ap()
    var = A("var", [128, 1], F32).ap()
    rstd = A("rstd", [128, 1], F32).ap()
    msq = A("msq", [128, 1], F32).ap()
    lt = A("lt", [128, 1], F32).ap()
    lt1 = A("lt1", [128, 1], F32).ap()
    vtmp = A("vtmp", [128, 1], F32).ap()
    linv = A("linv", [128, 16], F32).ap()
    stdbg = A("stdbg_s", [128, 8], F32).ap()

    wqb = BB[:, 0:8192]
    wkb = BB[:, 8192:16384]
    wvb = BB[:, 16384:24576]
    xsup = [BB[:, 24576:32768], BB[:, 32768:40960]]
    xln = [BB[:, 40960:41984], BB[:, 41984:43008]]
    xln4 = [BB[:, 40960:41984], BB[:, 41984:43008],
            BB[:, 43008:44032], BB[:, 44032:45056]]
    qsb = [BB[:, 45056:46080], BB[:, 46080:47104]]
    xjf = BB[:, 24576:40960]
    kq = [BB[:, 16384:24576], BB[:, 40960:49152]]

    def kfm(e):
        return kq[e // 4][:, (e % 4) * 2048:(e % 4) * 2048 + 2048]

    qstream = BB[:, 0:8192]
    vstream = BB[:, 8192:16384]
    ppair = [BB[:, 24576:25088], BB[:, 25088:25600]]
    xf_fm = BB[:, 0:16384]
    z_fm = BB[:, 16384:32768]
    xf_tm = BB[:, 40960:41984]
    z_tm = BB[:, 41984:43008]
    rw_b = BB[:, 43008:47104]
    w_tm = BB[:, 32768:40960]
    w_fmr = BB[:, 40960:49152]
    wh_r = BB[:, 16384:24576]
    ut_b = [BB[:, 24576:25600], BB[:, 25600:26624]]
    w1sb2 = [BB[:, 24576:28672], BB[:, 28672:32768]]
    w2sb2 = [BB[:, 32768:36864], BB[:, 36864:40960]]
    h1sb = BB[:, 43008:47104]
    vpp = BB[:, 0:4096]

    O_sb = FF[:, 0:16384]
    wmask = FF[:, 0:8192]
    mge = FF[:, 8192:8704]
    static = FF[:, 0:16384]

    xstg = [stg[:, 0:1024], stg[:, 1024:2048]]
    xstg4 = [stg[:, 0:1024], stg[:, 1024:2048], stg[:, 2048:3072], stg[:, 3072:4096]]
    XSEM = ("x_a", "x_b", "x_c", "x_d")
    xres = [stg[:, 0:1024], stg[:, 1024:2048]]
    x2sb = [stg[:, 2048:3072], stg[:, 3072:4096]]
    xf32 = stg[:, 4096:5120]
    gbc = stg[:, 0:1024]
    wstage = [stg[:, 1024:2048], stg[:, 2048:3072]]
    bbc = stg[:, 3072:3584]
    clipb = [stg[:, 3584:4096], stg[:, 4096:4608]]
    alphab = [stg[:, 4608:5120], stg[:, 5120:5632]]
    utstg = [stg[:, 3584:3712], stg[:, 3712:3840]]
    fsb = stg[:, 4608:5632]
    ysb = [stg[:, 2048:3072], stg[:, 3072:4096]]

    P = nc.alloc_psum_tensor
    p0 = P("p0", [128, 512], F32).ap()
    p1 = P("p1", [128, 512], F32).ap()
    p2 = P("p2", [128, 512], F32).ap()
    p3 = P("p3", [128, 512], F32).ap()
    p4 = P("p4", [128, 512], F32).ap()
    p5 = P("p5", [128, 512], F32).ap()
    p6 = P("p6", [128, 512], F32).ap()
    p_t = P("p_t", [128, 1024], BF16).ap()
    p_t2 = p4.bitcast(BF16)

    def ptb(gi):
        return p_t if gi % 2 == 0 else p_t2

    def stats_seq(vector):
        vector.tensor_scalar(mean[:, :], sumx[:, :], 1.0 / D, None, op0=ALU.mult)
        vector.tensor_scalar(vtmp[:, :], sumsq[:, :], 1.0 / D, None, op0=ALU.mult)
        vector.tensor_tensor(msq[:, :], mean[:, :], mean[:, :], op=ALU.mult)
        vector.drain()
        return vector.tensor_tensor(var[:, :], vtmp[:, :], msq[:, :], op=ALU.subtract)

    sem_names = [
        "s0a", "s0b", "s0i", "x_a", "x_b", "x_c", "x_d", "kx_a", "kx_b", "sp_a", "sp_b",
        "qv_a", "qv_b", "x3_a", "x3_b", "sp3_a", "sp3_b", "ra_a", "ra_b",
        "du_a", "du_b", "dut_a", "dut_b", "ee_a", "ee_b", "fv_a", "fv_b",
        "fx_a", "fx_b", "fs_a", "fs_b",
        "s0_ld", "s0_v",
        "s1_x", "s1_sq", "s1_var", "s1_rstd", "s1_app", "s1_tr", "s1_ev",
        "s1_pmm", "s1_pev", "s1_sp",
        "k_x", "k_sq", "k_var", "k_rstd", "k_app", "k_tr", "k_ev", "k_mm", "k_kev",
        "s2_qv", "s2_smm", "s2_sp", "s2_pv", "s2_fold",
        "s3_x", "s3_s", "s3_v", "s3_tr", "s3_sp",
        "a_ld", "a_te", "a_ve",
        "r_mm", "r_ev", "r_sp", "r_tk",
        "c_w", "c_wt", "c_wf",
        "d_ld", "d_sq", "d_utc", "d_h", "d_wh",
        "e_ld", "e_cast", "e_m1", "e_g", "e_m2", "e_f",
        "f_ld", "f_sq", "f_vpp", "f_dyn", "f_ve", "f_sc", "f_x2", "f_st",
        "dbg",
    ]
    SEM = {n: nc.alloc_semaphore(name=n) for n in sem_names}
    lo = min(s.num for s in SEM.values())
    hi = max(s.num for s in SEM.values())
    nc.gpsimd.sem_clear(range(lo, hi + 1))
    nc.all_engine_barrier()

    with nc.Block("p0") as block:
        @block.sync
        def _(sync):
            sync.dma_start(out=stg[:, 5632:5760], in_=id_d[:, :]).then_inc(SEM["s0i"], 16)
            for wi, wd in enumerate((wq_d, wk_d, wv_d)):
                for c in range(DC):
                    gi = wi * DC + c
                    if gi >= 2:
                        sync.wait_ge(SEM["s0_v"], gi - 1)
                    sync.dma_start(out=xstg[gi % 2], in_=wd[_ts(c), :]
                                   ).then_inc(SEM["s0a" if gi % 2 == 0 else "s0b"], 16)

        @block.scalar
        def _(scalar):
            for fn in (AF.Exp, AF.Ln, AF.Gelu, AF.Tanh, AF.Square):
                scalar.activation(out=junk[0:1, 0:1], in_=junk[0:1, 0:1], func=fn,
                                  bias=zerob[0:1, :], scale=1.0)
                scalar.drain()

        @block.vector
        def _(vector):
            vector.memset(ones_b[:, :], 1.0)
            vector.memset(ones1f[:, :], 1.0)
            vector.memset(negB[:, :], -B_SHIFT)
            vector.memset(zerob[:, :], 0.0)
            vector.memset(eps5[:, :], 1e-5)
            vector.memset(onesf[:, :], 1.0)
            vector.memset(eps12[:, :], 1e-12)
            vector.wait_ge(SEM["s0i"], 16)
            vector.tensor_copy(ident_b[:, :], stg[:, 5632:5760])
            for wi in range(3):
                dst = (wqb, wkb, wvb)[wi]
                for c in range(DC):
                    gi = wi * DC + c
                    vector.wait_ge(SEM["s0a" if gi % 2 == 0 else "s0b"], 16 * (gi // 2 + 1))
                    vector.tensor_copy(dst[:, c * 1024:(c + 1) * 1024],
                                       xstg[gi % 2]).then_inc(SEM["s0_v"], 1)

    def emit_ln_sync(sync, n, src_d, sx, sapp):
        sxa, sxb = sx
        for gi in range(n):
            if gi >= 2:
                sync.wait_ge(SEM[sapp], gi - 1)
            sync.dma_start(out=xstg[gi % 2], in_=src_d[_ts(gi), :]
                           ).then_inc(SEM[sxa if gi % 2 == 0 else sxb], 16)

    def emit_ln_vector(vector, n, sx, ssq, svar, srstd, sapp, str_, sev, dst_of):
        sxa, sxb = sx
        for gi in range(n):
            xc = xstg[gi % 2]
            vector.wait_ge(SEM[sxa if gi % 2 == 0 else sxb], 16 * (gi // 2 + 1))
            vector.reduce_sum(sumx[:, :], xc, axis=AX)
            vector.drain()
            vector.wait_ge(SEM[ssq], gi + 1)
            stats_seq(vector).then_inc(SEM[svar], 1)
            vector.wait_ge(SEM[srstd], gi + 1)
            vector.tensor_scalar(xln[gi % 2], xc, mean[:, :], rstd[:, :],
                                 op0=ALU.subtract, op1=ALU.mult).then_inc(SEM[sapp], 1)
            vector.wait_ge(SEM[str_], gi + 1)
            vector.tensor_copy(dst_of(gi), ptb(gi).rearrange("p (d t) -> p d t", d=DC)
                               ).then_inc(SEM[sev], 1)

    def emit_ln_scalar(scalar, n, sx, ssq, svar, srstd):
        sxa, sxb = sx
        for gi in range(n):
            xc = xstg[gi % 2]
            scalar.wait_ge(SEM[sxa if gi % 2 == 0 else sxb], 16 * (gi // 2 + 1))
            scalar.activation(out=junk[:, :], in_=xc, func=AF.Square,
                              accum_out=sumsq[:, :]).then_inc(SEM[ssq], 1)
            scalar.wait_ge(SEM[svar], gi + 1)
            scalar.activation(out=rstd[:, :], in_=var[:, :], func=AF.Ln, bias=eps5[:, :], scale=1.0)
            scalar.drain()
            scalar.activation(out=rstd[:, :], in_=rstd[:, :], func=AF.Exp, bias=zerob[:, :],
                              scale=-0.5).then_inc(SEM[srstd], 1)

    def emit_tr_chunk(tensor, gi, sapp, sev, str_, src4=False):
        tensor.wait_ge(SEM[sapp], gi + 1)
        if gi >= 2:
            tensor.wait_ge(SEM[sev], gi - 1)
        src = xln4[gi % 4] if src4 else xln[gi % 2]
        for dcx in range(DC):
            ins = tensor.transpose(ptb(gi)[:, dcx * 128:(dcx + 1) * 128],
                                   src[:, dcx * 128:(dcx + 1) * 128],
                                   ident_b[:, :])
            if dcx == DC - 1:
                ins.then_inc(SEM[str_], 1)

    if upto < 1:
        return nc
    with nc.Block("p1") as block:
        @block.sync
        def _(sync):
            for gi in range(XC):
                if gi >= 4:
                    sync.wait_ge(SEM["s1_app"], gi - 3)
                sync.dma_start(out=xstg4[gi % 4], in_=x_d[_ts(gi), :]
                               ).then_inc(SEM[XSEM[gi % 4]], 16)

        @block.scalar
        def _(scalar):
            for gi in range(XC):
                xc = xstg4[gi % 4]
                scalar.wait_ge(SEM[XSEM[gi % 4]], 16 * (gi // 4 + 1))
                scalar.activation(out=junk[:, :], in_=xc, func=AF.Square,
                                  accum_out=sumsq[:, :]).then_inc(SEM["s1_sq"], 1)
                scalar.wait_ge(SEM["s1_var"], gi + 1)
                scalar.activation(out=rstd[:, :], in_=var[:, :], func=AF.Ln, bias=eps5[:, :], scale=1.0)
                scalar.drain()
                scalar.activation(out=rstd[:, :], in_=rstd[:, :], func=AF.Exp, bias=zerob[:, :],
                                  scale=-0.5).then_inc(SEM["s1_rstd"], 1)

        @block.tensor
        def _(tensor):
            pg = 0
            for c in range(8):
                emit_tr_chunk(tensor, c, "s1_app", "s1_ev", "s1_tr", src4=True)
            for s in range(NSUP):
                def maybe_trans(k, s_=s):
                    if s_ + 1 < NSUP and k % 2 == 1:
                        c2 = k // 2
                        emit_tr_chunk(tensor, (s_ + 1) * 8 + c2, "s1_app", "s1_ev", "s1_tr",
                                      src4=True)
                tensor.wait_ge(SEM["s1_ev"], (s + 1) * 8)
                for e in range(DC):
                    pp = (p0, p1) if pg % 2 == 0 else (p2, p3)
                    if pg >= 2:
                        tensor.wait_ge(SEM["s1_pev"], pg - 1)
                    for dcx in range(DC):
                        for th in range(2):
                            ins = tensor.matmul(
                                pp[th][:, :],
                                wqb[:, dcx * 1024 + e * 128: dcx * 1024 + (e + 1) * 128],
                                xsup[s % 2][:, dcx * 1024 + th * 512: dcx * 1024 + th * 512 + 512],
                                start=(dcx == 0), stop=(dcx == DC - 1), **MK)
                            if dcx == DC - 1 and th == 1:
                                ins.then_inc(SEM["s1_pmm"], 1)
                    maybe_trans(pg % 16)
                    pg += 1
                for c in range(8):
                    pp = (p0, p1) if pg % 2 == 0 else (p2, p3)
                    if pg >= 2:
                        tensor.wait_ge(SEM["s1_pev"], pg - 1)
                    for dcx in range(DC):
                        for eh in range(2):
                            ins = tensor.matmul(
                                pp[eh][:, :],
                                xsup[s % 2][:, dcx * 1024 + c * 128: dcx * 1024 + (c + 1) * 128],
                                wvb[:, dcx * 1024 + eh * 512: dcx * 1024 + eh * 512 + 512],
                                start=(dcx == 0), stop=(dcx == DC - 1), **MK)
                            if dcx == DC - 1 and eh == 1:
                                ins.then_inc(SEM["s1_pmm"], 1)
                    maybe_trans(pg % 16)
                    pg += 1

        @block.vector
        def _(vector):
            def ln_sa(s, c):
                gi = s * 8 + c
                xc = xstg4[gi % 4]
                vector.wait_ge(SEM[XSEM[gi % 4]], 16 * (gi // 4 + 1))
                vector.reduce_sum(sumx[:, :], xc, axis=AX)
                vector.drain()
                vector.wait_ge(SEM["s1_sq"], gi + 1)
                stats_seq(vector).then_inc(SEM["s1_var"], 1)
                vector.wait_ge(SEM["s1_rstd"], gi + 1)
                if gi >= 4:
                    vector.wait_ge(SEM["s1_tr"], gi - 3)
                vector.tensor_scalar(xln4[gi % 4], xc, mean[:, :], rstd[:, :],
                                     op0=ALU.subtract, op1=ALU.mult).then_inc(SEM["s1_app"], 1)

            def ln_ev(s, c):
                gi = s * 8 + c
                vector.wait_ge(SEM["s1_tr"], gi + 1)
                if s >= 2 and c == 0:
                    vector.wait_ge(SEM["s1_pmm"], 16 * (s - 1))
                dst = xsup[s % 2].rearrange("p (d t) -> p d t", d=DC)[:, :, c * 128:(c + 1) * 128]
                vector.tensor_copy(dst, ptb(gi).rearrange("p (d t) -> p d t", d=DC)
                                   ).then_inc(SEM["s1_ev"], 1)

            for c in range(8):
                ln_sa(0, c)
                ln_ev(0, c)
            for s in range(NSUP):
                if s + 1 < NSUP:
                    ln_sa(s + 1, 0)
                    ln_sa(s + 1, 1)
                for k in range(16):
                    pg = s * 16 + k
                    pp = (p0, p1) if pg % 2 == 0 else (p2, p3)
                    vector.wait_ge(SEM["s1_pmm"], pg + 1)
                    if pg >= 2:
                        vector.wait_ge(SEM["sp_a" if pg % 2 == 0 else "sp_b"], 16 * (pg // 2))
                    vector.tensor_copy(qsb[pg % 2][:, 0:512], pp[0][:, :])
                    vector.tensor_copy(qsb[pg % 2][:, 512:1024], pp[1][:, :]
                                       ).then_inc(SEM["s1_pev"], 1)
                    if s + 1 < NSUP and k % 2 == 1:
                        c2 = k // 2
                        ln_ev(s + 1, c2)
                        if c2 + 2 <= 7:
                            ln_sa(s + 1, c2 + 2)

        @block.gpsimd
        def _(gpsimd):
            if debug:
                gpsimd.wait_ge(SEM["s1_ev"], 8)
                gpsimd.dma_start(out=xsdbg_d[:, :], in_=xsup[0]).then_inc(SEM["dbg"], 16)
                gpsimd.dma_start(out=stdbg_d[:, :], in_=stdbg[:, :]).then_inc(SEM["dbg"], 16)
                gpsimd.dma_start(out=wqdbg_d[:, :], in_=wqb).then_inc(SEM["dbg"], 16)
            for s in range(NSUP):
                for e in range(DC):
                    pg = s * 16 + e
                    gpsimd.wait_ge(SEM["s1_pev"], pg + 1)
                    gpsimd.dma_start(out=qbuf_d[s, e, :, :], in_=qsb[pg % 2]
                                     ).then_inc(SEM["sp_a" if pg % 2 == 0 else "sp_b"], 16)
                for c in range(8):
                    pg = s * 16 + 8 + c
                    gpsimd.wait_ge(SEM["s1_pev"], pg + 1)
                    gpsimd.dma_start(out=vbuf_d[s * 8 + c, :, :], in_=qsb[pg % 2]
                                     ).then_inc(SEM["sp_a" if pg % 2 == 0 else "sp_b"], 16)
            gpsimd.wait_ge(SEM["sp_a"], 16 * NSUP * 8)
            gpsimd.wait_ge(SEM["sp_b"], 16 * NSUP * 8)

    if upto < 2:
        return nc
    with nc.Block("p1k") as block:
        @block.sync
        def _(sync):
            emit_ln_sync(sync, TC, xj_d, ("kx_a", "kx_b"), "k_app")

        @block.scalar
        def _(scalar):
            emit_ln_scalar(scalar, TC, ("kx_a", "kx_b"), "k_sq", "k_var", "k_rstd")

        @block.tensor
        def _(tensor):
            for gi in range(TC):
                emit_tr_chunk(tensor, gi, "k_app", "k_ev", "k_tr")
            tensor.wait_ge(SEM["k_ev"], TC)
            kg = 0
            for e in range(DC):
                for tb in range(4):
                    pp = p0 if kg % 2 == 0 else p1
                    if kg >= 2:
                        tensor.wait_ge(SEM["k_kev"], kg - 1)
                    for dcx in range(DC):
                        ins = tensor.matmul(
                            pp[:, :],
                            wkb[:, dcx * 1024 + e * 128: dcx * 1024 + (e + 1) * 128],
                            xjf[:, dcx * 2048 + tb * 512: dcx * 2048 + tb * 512 + 512],
                            start=(dcx == 0), stop=(dcx == DC - 1), **MK)
                        if dcx == DC - 1:
                            ins.then_inc(SEM["k_mm"], 1)
                    kg += 1

        @block.vector
        def _(vector):
            emit_ln_vector(vector, TC, ("kx_a", "kx_b"), "k_sq", "k_var", "k_rstd", "k_app", "k_tr", "k_ev",
                           lambda gi: xjf.rearrange("p (d t) -> p d t", d=DC)[:, :, gi * 128:(gi + 1) * 128])
            for kg in range(DC * 4):
                e, tb = kg // 4, kg % 4
                pp = p0 if kg % 2 == 0 else p1
                vector.wait_ge(SEM["k_mm"], kg + 1)
                vector.tensor_copy(kfm(e)[:, tb * 512:tb * 512 + 512], pp[:, :]
                                   ).then_inc(SEM["k_kev"], 1)

        if debug:
            @block.gpsimd
            def _(gpsimd):
                gpsimd.wait_ge(SEM["k_kev"], DC * 4)
                for e in range(DC):
                    gpsimd.dma_start(out=kdbg_d[e, :, :], in_=kfm(e)).then_inc(SEM["dbg"], 16)
                gpsimd.wait_ge(SEM["dbg"], 16 * DC)

    if upto < 3:
        return nc
    OA = (p0, p1)
    OB = (p2, p3)
    SBK = (p4, p5)

    with nc.Block("p2") as block:
        @block.sync
        def _(sync):
            for s in range(16):
                if s >= 2:
                    sync.wait_ge(SEM["s2_pv"], 16 * (s - 1))
                qdst = qstream[:, (s % 2) * 4096:(s % 2) * 4096 + 4096]
                vdst = vstream[:, (s % 2) * 4096:(s % 2) * 4096 + 4096]
                qs = "qv_a" if s % 2 == 0 else "qv_b"
                sync.dma_start(out=qdst.rearrange("p (e t) -> p e t", e=DC),
                               in_=qbuf_d[s // 2, :, :, (s % 2) * 512:(s % 2) * 512 + 512]
                               .transpose([1, 0, 2])).then_inc(SEM[qs], 16)
                sync.dma_start(out=vdst.rearrange("p (c e) -> p c e", c=4),
                               in_=vbuf_d[s * 4:(s + 1) * 4].transpose([1, 0, 2])
                               ).then_inc(SEM[qs], 16)

        @block.tensor
        def _(tensor):
            def emit_S(g):
                s = g // 16
                jb = (g // 2) % 8
                t = g % 2
                sb = (s % 2) * 4096
                if t == 0 and jb == 0:
                    tensor.wait_ge(SEM["qv_a" if s % 2 == 0 else "qv_b"], 32 * (s // 2 + 1))
                if g >= 2:
                    tensor.wait_ge(SEM["s2_sp"], g - 1)
                for icl in range(2):
                    ic = t * 2 + icl
                    for e in range(DC):
                        ins = tensor.matmul(
                            SBK[g % 2][:, icl * 256:icl * 256 + 256],
                            qstream[:, sb + e * 512 + ic * 128: sb + e * 512 + (ic + 1) * 128],
                            kfm(e)[:, jb * 256: jb * 256 + 256],
                            start=(e == 0), stop=(e == DC - 1), **MK)
                        if icl == 1 and e == DC - 1:
                            ins.then_inc(SEM["s2_smm"], 1)

            def emit_PV(g):
                G = g // 2
                s = g // 16
                t = g % 2
                sb = (s % 2) * 4096
                tensor.wait_ge(SEM["s2_sp"], g + 1)
                if t == 0:
                    tensor.wait_ge(SEM["s2_fold"], G)
                par = G % 2
                for icl in range(2):
                    ic = t * 2 + icl
                    for jsub in range(2):
                        lhsT = ppair[g % 2][:, icl * 256 + jsub * 128: icl * 256 + (jsub + 1) * 128]
                        ob = OA if jsub == 0 else OB
                        for eh in range(2):
                            tensor.matmul(ob[eh][:, :], lhsT,
                                          vstream[:, sb + ic * 1024 + eh * 512: sb + ic * 1024 + eh * 512 + 512],
                                          start=(ic == 0), stop=(ic == 3), **MK)
                        slot = par * 16 + jsub * 8 + ic
                        ins = tensor.matmul(p6[:, slot:slot + 1], lhsT, ones_b[:, :],
                                            start=True, stop=True, **MK)
                        if icl == 1 and jsub == 1:
                            ins.then_inc(SEM["s2_pv"], 1)

            for s in range(16):
                base = s * 16
                emit_S(base)
                for t in range(15):
                    emit_S(base + t + 1)
                    emit_PV(base + t)
                emit_PV(base + 15)

        @block.scalar
        def _(scalar):
            for g in range(256):
                scalar.wait_ge(SEM["s2_smm"], g + 1)
                if g >= 2:
                    scalar.wait_ge(SEM["s2_pv"], g - 1)
                scalar.activation(out=ppair[g % 2][:, :], in_=SBK[g % 2][:, :], func=AF.Exp,
                                  bias=negB[:, :], scale=SCALE_EXP).then_inc(SEM["s2_sp"], 1)

        @block.vector
        def _(vector):
            for G in range(16 * 8):
                s, jb = G // 8, G % 8
                par = G % 2
                vector.wait_ge(SEM["s2_pv"], 2 * (G + 1))
                for jsub in range(2):
                    jc = jb * 2 + jsub
                    ob = OA if jsub == 0 else OB
                    ltx = lt if jsub == 0 else lt1
                    sl = par * 16 + jsub * 8
                    vector.reduce_sum(ltx[:, :], p6[:, sl:sl + 4], axis=AX)
                    for eh in range(2):
                        dst = O_sb[:, jc * 1024 + eh * 512: jc * 1024 + eh * 512 + 512]
                        if s == 0:
                            vector.tensor_copy(dst, ob[eh][:, :])
                        else:
                            vector.tensor_tensor(dst, dst, ob[eh][:, :], op=ALU.add)
                    if s == 0:
                        ins = vector.tensor_copy(l_sb[:, jc:jc + 1], ltx[:, :])
                    else:
                        ins = vector.tensor_tensor(l_sb[:, jc:jc + 1], l_sb[:, jc:jc + 1],
                                                   ltx[:, :], op=ALU.add)
                    if jsub == 1:
                        ins.then_inc(SEM["s2_fold"], 1)

    if upto < 4:
        return nc
    with nc.Block("p3") as block:
        @block.sync
        def _(sync):
            for tc in range(TC):
                if tc >= 2:
                    sync.wait_ge(SEM["s3_v"], 9 * (tc - 2) + 3)
                sync.dma_start(out=xres[tc % 2], in_=xj_d[_ts(tc), :]
                               ).then_inc(SEM["x3_a" if tc % 2 == 0 else "x3_b"], 16)

        @block.vector
        def _(vector):
            vector.reciprocal(linv[:, :], l_sb[:, :])
            vector.drain()
            for tc in range(TC):
                oc = O_sb[:, tc * 1024:(tc + 1) * 1024]
                vector.tensor_scalar(oc, oc, linv[:, tc:tc + 1], None, op0=ALU.mult
                                     ).then_inc(SEM["s3_v"], 1)
                vector.drain()
                vector.reduce_sum(sumx[:, :], oc, axis=AX)
                vector.drain()
                vector.wait_ge(SEM["s3_s"], 6 * tc + 1)
                stats_seq(vector).then_inc(SEM["s3_v"], 1)
                vector.wait_ge(SEM["s3_s"], 6 * tc + 2)
                if debug:
                    vector.wait_ge(SEM["dbg"], 16 * (DC + 1 + tc + 1))
                vector.tensor_scalar(oc, oc, mean[:, :], rstd[:, :],
                                     op0=ALU.subtract, op1=ALU.mult)
                vector.drain()
                vector.wait_ge(SEM["x3_a" if tc % 2 == 0 else "x3_b"], 16 * (tc // 2 + 1))
                if tc >= 2:
                    vector.wait_ge(SEM["sp3_a" if tc % 2 == 0 else "sp3_b"], 16 * (tc // 2))
                vector.tensor_tensor(x2sb[tc % 2], oc, xres[tc % 2], op=ALU.add
                                     ).then_inc(SEM["s3_v"], 1)
                vector.drain()
                vector.reduce_sum(sumx[:, :], x2sb[tc % 2], axis=AX)
                vector.drain()
                vector.wait_ge(SEM["s3_s"], 6 * tc + 3)
                stats_seq(vector).then_inc(SEM["s3_v"], 1)
                vector.wait_ge(SEM["s3_s"], 6 * tc + 4)
                vector.tensor_scalar(xf32, x2sb[tc % 2], mean[:, :], rstd[:, :],
                                     op0=ALU.subtract, op1=ALU.mult)
                vector.drain()
                vector.tensor_copy(xf_tm, xf32).then_inc(SEM["s3_v"], 1)
                vector.reduce_sum(sumx[:, :], xf32, axis=AX)
                vector.drain()
                vector.wait_ge(SEM["s3_s"], 6 * tc + 5)
                stats_seq(vector).then_inc(SEM["s3_v"], 1)
                vector.wait_ge(SEM["s3_s"], 6 * tc + 6)
                vector.tensor_scalar(z_tm, xf32, mean[:, :], rstd[:, :],
                                     op0=ALU.subtract, op1=ALU.mult).then_inc(SEM["s3_v"], 1)
                vector.wait_ge(SEM["s3_tr"], 2 * tc + 1)
                vector.tensor_copy(
                    xf_fm.rearrange("p (d t) -> p d t", d=DC)[:, :, tc * 128:(tc + 1) * 128],
                    p_t.rearrange("p (d t) -> p d t", d=DC)).then_inc(SEM["s3_v"], 1)
                vector.wait_ge(SEM["s3_tr"], 2 * tc + 2)
                vector.tensor_copy(
                    z_fm.rearrange("p (d t) -> p d t", d=DC)[:, :, tc * 128:(tc + 1) * 128],
                    p_t2.rearrange("p (d t) -> p d t", d=DC)).then_inc(SEM["s3_v"], 1)

        @block.scalar
        def _(scalar):
            for tc in range(TC):
                for j, (srcv, wv_) in enumerate(((O_sb[:, tc * 1024:(tc + 1) * 1024], 9 * tc + 1),
                                                 (x2sb[tc % 2], 9 * tc + 3),
                                                 (xf32, 9 * tc + 5))):
                    scalar.wait_ge(SEM["s3_v"], wv_)
                    scalar.activation(out=junk[:, :], in_=srcv, func=AF.Square,
                                      accum_out=sumsq[:, :]).then_inc(SEM["s3_s"], 1)
                    scalar.wait_ge(SEM["s3_v"], 9 * tc + 2 + 2 * j)
                    scalar.activation(out=rstd[:, :], in_=var[:, :], func=AF.Ln,
                                      bias=eps5[:, :], scale=1.0)
                    scalar.drain()
                    scalar.activation(out=rstd[:, :], in_=rstd[:, :], func=AF.Exp,
                                      bias=zerob[:, :], scale=-0.5).then_inc(SEM["s3_s"], 1)

        @block.tensor
        def _(tensor):
            for tc in range(TC):
                tensor.wait_ge(SEM["s3_v"], 9 * tc + 5)
                if tc > 0:
                    tensor.wait_ge(SEM["s3_v"], 9 * (tc - 1) + 8)
                for dcx in range(DC):
                    ins = tensor.transpose(p_t[:, dcx * 128:(dcx + 1) * 128],
                                           xf_tm[:, dcx * 128:(dcx + 1) * 128], ident_b[:, :])
                    if dcx == DC - 1:
                        ins.then_inc(SEM["s3_tr"], 1)
                tensor.wait_ge(SEM["s3_v"], 9 * tc + 7)
                if tc > 0:
                    tensor.wait_ge(SEM["s3_v"], 9 * (tc - 1) + 9)
                for dcx in range(DC):
                    ins = tensor.transpose(p_t2[:, dcx * 128:(dcx + 1) * 128],
                                           z_tm[:, dcx * 128:(dcx + 1) * 128], ident_b[:, :])
                    if dcx == DC - 1:
                        ins.then_inc(SEM["s3_tr"], 1)

        @block.gpsimd
        def _(gpsimd):
            if debug:
                gpsimd.wait_ge(SEM["s2_fold"], 64)
                gpsimd.dma_start(out=ldbg_d[:, :], in_=l_sb[:, :]).then_inc(SEM["dbg"], 16)
            for tc in range(TC):
                if debug:
                    gpsimd.wait_ge(SEM["s3_v"], 9 * tc + 1)
                    gpsimd.dma_start(out=odbg_d[tc, :, :], in_=O_sb[:, tc * 1024:(tc + 1) * 1024]
                                     ).then_inc(SEM["dbg"], 16)
                gpsimd.wait_ge(SEM["s3_v"], 9 * tc + 3)
                gpsimd.dma_start(out=x2buf_d[tc, :, :], in_=x2sb[tc % 2]
                                 ).then_inc(SEM["sp3_a" if tc % 2 == 0 else "sp3_b"], 16)
            gpsimd.wait_ge(SEM["sp3_a"], 16 * TC // 2)
            gpsimd.wait_ge(SEM["sp3_b"], 16 * TC // 2)

    if upto < 5:
        return nc
    try:
        p4block = nc.Block("p4")
        p4block.__enter__()
        block = p4block
        @block.sync
        def _(sync):
            for c in range(DC):
                sync.dma_start(out=g_t[:, c:c + 1], in_=lng_d[c, :, :]).then_inc(SEM["a_ld"], 16)
                sync.dma_start(out=b_t[:, c:c + 1], in_=lnb_d[c, :, :]).then_inc(SEM["a_ld"], 16)
            sync.dma_start(out=rb_row[:, :], in_=rb_d[:, :]).then_inc(SEM["a_ld"], 16)
            sync.dma_start(out=grow[:, :], in_=gam_d[:, :]).then_inc(SEM["a_ld"], 16)
            for c in range(DC):
                if c >= 2:
                    sync.wait_ge(SEM["a_ve"], c - 1)
                    sync.wait_ge(SEM["a_te"], c)
                sync.dma_start(out=wstage[c % 2][:, 0:512], in_=rwt_d[_ts(c), :]
                               ).then_inc(SEM["ra_a" if c % 2 == 0 else "ra_b"], 16)

        @block.tensor
        def _(tensor):
            tensor.wait_ge(SEM["a_ld"], 16 * 18)
            tensor.matmul(p4[:, :], ones1f[:, :], grow[:, 0:512], start=True, stop=True, **MK)
            tensor.matmul(p5[:, :], ones1f[:, :], grow[:, 512:1024], start=True, stop=True, **MK
                          ).then_inc(SEM["a_te"], 1)
            for c in range(DC):
                tensor.wait_ge(SEM["ra_a" if c % 2 == 0 else "ra_b"], 16 * (c // 2 + 1))
                tensor.matmul(p6[0:1, :], b_t[:, c:c + 1], wstage[c % 2][:, 0:512],
                              start=(c == 0), stop=(c == DC - 1), **MK).then_inc(SEM["a_te"], 1)
            tensor.wait_ge(SEM["a_ve"], DC + 2)
            tensor.matmul(p4[:, :], ones1f[:, :], brow[:, :], start=True, stop=True, **MK
                          ).then_inc(SEM["a_te"], 1)

        @block.vector
        def _(vector):
            vector.wait_ge(SEM["a_te"], 1)
            vector.tensor_copy(gbc[:, 0:512], p4[:, :])
            vector.tensor_copy(gbc[:, 512:1024], p5[:, :])
            for c in range(DC):
                vector.wait_ge(SEM["ra_a" if c % 2 == 0 else "ra_b"], 16 * (c // 2 + 1))
                vector.wait_ge(SEM["a_te"], c + 2)
                vector.tensor_scalar(rw_b[:, c * 512:(c + 1) * 512], wstage[c % 2][:, 0:512],
                                     g_t[:, c:c + 1], None, op0=ALU.mult).then_inc(SEM["a_ve"], 1)
            vector.wait_ge(SEM["a_te"], DC + 1)
            vector.tensor_copy(brow[:, :], p6[0:1, :]).then_inc(SEM["a_ve"], 1)
            vector.drain()
            vector.tensor_tensor(brow[:, :], brow[:, :], rb_row[:, :], op=ALU.add
                                 ).then_inc(SEM["a_ve"], 1)
            vector.wait_ge(SEM["a_te"], DC + 2)
            vector.tensor_copy(bbc[:, :], p4[:, :]).then_inc(SEM["a_ve"], 1)

        if p4stage < 1:
            raise StopPhase()
        if debug:
            @block.gpsimd
            def _(gpsimd):
                gpsimd.dma_start(out=xfdbg_d[:, :], in_=xf_fm).then_inc(SEM["dbg"], 16)
                gpsimd.dma_start(out=zfdbg_d[:, :], in_=z_fm).then_inc(SEM["dbg"], 16)

        @block.tensor
        def _(tensor):
            for tc in range(TC):
                pp = p4 if tc % 2 == 0 else p5
                if tc == 0:
                    tensor.wait_ge(SEM["a_ve"], DC + 3)
                if tc >= 2:
                    tensor.wait_ge(SEM["r_ev"], 2 * (tc - 2) + 1)
                for dcx in range(DC):
                    ins = tensor.matmul(
                        pp[:, :],
                        z_fm[:, dcx * 2048 + tc * 128: dcx * 2048 + (tc + 1) * 128],
                        rw_b[:, dcx * 512:(dcx + 1) * 512],
                        start=(dcx == 0), stop=(dcx == DC - 1), **MK)
                    if dcx == DC - 1:
                        ins.then_inc(SEM["r_mm"], 1)

        @block.vector
        def _(vector):
            for tc in range(TC):
                pp = p4 if tc % 2 == 0 else p5
                vector.wait_ge(SEM["r_mm"], tc + 1)
                if tc >= 2:
                    vector.wait_ge(SEM["r_sp"], tc - 1)
                vector.tensor_tensor(clipb[tc % 2], pp[:, :], bbc[:, :], op=ALU.add
                                     ).then_inc(SEM["r_ev"], 1)
                vector.drain()
                vector.tensor_scalar(clipb[tc % 2], clipb[tc % 2], -10.0, 10.0,
                                     op0=ALU.max, op1=ALU.min).then_inc(SEM["r_ev"], 1)
                vector.wait_ge(SEM["r_sp"], tc + 1)
                vector.max(out=top8[:, :], in_=alphab[tc % 2])
                vector.drain()
                vector.reduce_sum(SSt[:, tc:tc + 1], top8[:, :], axis=AX)
                vector.tensor_scalar(mge[:, :], alphab[tc % 2], top8[:, 7:8], None, op0=ALU.is_ge)
                vector.drain()
                vector.tensor_tensor(wmask[:, tc * 512:(tc + 1) * 512], alphab[tc % 2],
                                     mge[:, :], op=ALU.mult).then_inc(SEM["r_tk"], 1)

        @block.scalar
        def _(scalar):
            for tc in range(TC):
                scalar.wait_ge(SEM["r_ev"], 2 * tc + 2)
                if tc >= 2:
                    scalar.wait_ge(SEM["r_tk"], tc - 1)
                    if debug:
                        scalar.wait_ge(SEM["dbg"], 16 * (DC + 1 + TC) + 16 * (tc - 1))
                scalar.activation(out=alphab[tc % 2], in_=clipb[tc % 2], func=AF.Exp,
                                  bias=zerob[:, :], scale=1.0)
                scalar.drain()
                scalar.activation(out=alphab[tc % 2], in_=alphab[tc % 2], func=AF.Ln,
                                  bias=onesf[:, :], scale=1.0).then_inc(SEM["r_sp"], 1)

        if debug:
            @block.gpsimd
            def _(gpsimd):
                for tc in range(TC):
                    gpsimd.wait_ge(SEM["r_sp"], tc + 1)
                    gpsimd.dma_start(out=adbg_d[tc, :, :], in_=alphab[tc % 2]).then_inc(SEM["dbg"], 16)

        if p4stage < 2:
            raise StopPhase()
        @block.scalar
        def _(scalar):
            scalar.wait_ge(SEM["r_tk"], TC)
            scalar.activation(out=fac[:, :], in_=SSt[:, :], func=AF.Tanh, bias=zerob[:, :],
                              scale=1.0).then_inc(SEM["r_sp"], 1)

        @block.vector
        def _(vector):
            vector.wait_ge(SEM["r_sp"], TC + 1)
            vector.tensor_scalar(SS2[:, :], SSt[:, :], 1e-6, None, op0=ALU.add)
            vector.drain()
            vector.reciprocal(SS2[:, :], SS2[:, :])
            vector.drain()
            vector.tensor_tensor(fac[:, :], fac[:, :], SS2[:, :], op=ALU.mult)
            vector.drain()
            for tc in range(TC):
                vector.tensor_scalar(w_tm[:, tc * 512:(tc + 1) * 512],
                                     wmask[:, tc * 512:(tc + 1) * 512],
                                     fac[:, tc:tc + 1], None, op0=ALU.mult).then_inc(SEM["c_w"], 1)

        @block.tensor
        def _(tensor):
            for tc in range(TC):
                tensor.wait_ge(SEM["c_w"], tc + 1)
                if tc >= 2:
                    tensor.wait_ge(SEM["c_wf"], tc - 1)
                bank = ptb(tc)
                for us in range(4):
                    ins = tensor.transpose(bank[:, us * 128:(us + 1) * 128],
                                           w_tm[:, tc * 512 + us * 128: tc * 512 + (us + 1) * 128],
                                           ident_b[:, :])
                    if us == 3:
                        ins.then_inc(SEM["c_wt"], 1)

        @block.vector
        def _(vector):
            for tc in range(TC):
                vector.wait_ge(SEM["c_wt"], tc + 1)
                vector.tensor_copy(
                    w_fmr.rearrange("p (u t) -> p u t", u=4)[:, :, tc * 128:(tc + 1) * 128],
                    ptb(tc)[:, 0:512].rearrange("p (u t) -> p u t", u=4)
                    ).then_inc(SEM["c_wf"], 1)

        if debug:
            @block.gpsimd
            def _(gpsimd):
                gpsimd.wait_ge(SEM["c_wf"], TC)
                for us in range(4):
                    gpsimd.dma_start(out=wdbg_d[us, :, :],
                                     in_=w_fmr[:, us * 2048:(us + 1) * 2048]).then_inc(SEM["dbg"], 16)

        if p4stage < 3:
            raise StopPhase()
        @block.sync
        def _(sync):
            sync.wait_ge(SEM["a_ve"], DC)
            sync.wait_ge(SEM["a_te"], DC + 1)
            for uc in range(4):
                if uc >= 2:
                    sync.wait_ge(SEM["d_sq"], uc - 1)
                sync.dma_start(out=wstage[uc % 2], in_=u_d[_ts(uc), :]
                               ).then_inc(SEM["du_a" if uc % 2 == 0 else "du_b"], 16)
            sync.wait_ge(SEM["r_sp"], TC)
            for uc in range(4):
                for dcx in range(DC):
                    gi = uc * DC + dcx
                    if gi >= 2:
                        sync.wait_ge(SEM["d_utc"], gi - 1)
                    sync.dma_start(out=utstg[gi % 2], in_=ut_d[_ts(dcx), _ts(uc)]
                                   ).then_inc(SEM["dut_a" if gi % 2 == 0 else "dut_b"], 16)

        @block.scalar
        def _(scalar):
            for uc in range(4):
                scalar.wait_ge(SEM["du_a" if uc % 2 == 0 else "du_b"], 16 * (uc // 2 + 1))
                scalar.activation(out=junk[:, :], in_=wstage[uc % 2], func=AF.Square,
                                  accum_out=su2[:, uc:uc + 1]).then_inc(SEM["d_sq"], 1)
            scalar.activation(out=su_t[:, :], in_=su2[:, :], func=AF.Ln, bias=eps12[:, :], scale=1.0)
            scalar.drain()
            scalar.activation(out=su_t[:, :], in_=su_t[:, :], func=AF.Exp, bias=zerob[:, :],
                              scale=-0.5).then_inc(SEM["d_sq"], 1)

        @block.vector
        def _(vector):
            def stt_group(uc_):
                for tb in range(4):
                    hg = uc_ * 4 + tb
                    pp = p4 if hg % 2 == 0 else p5
                    vector.wait_ge(SEM["d_h"], hg + 1)
                    vector.scalar_tensor_tensor(
                        wh_r[:, uc_ * 2048 + tb * 512: uc_ * 2048 + tb * 512 + 512],
                        pp[:, :], su_t[:, uc_:uc_ + 1],
                        w_fmr[:, uc_ * 2048 + tb * 512: uc_ * 2048 + tb * 512 + 512],
                        op0=ALU.mult, op1=ALU.mult).then_inc(SEM["d_wh"], 1)

            vector.wait_ge(SEM["d_sq"], 5)
            for uc in range(4):
                for dcx in range(DC):
                    gi = uc * DC + dcx
                    vector.wait_ge(SEM["dut_a" if gi % 2 == 0 else "dut_b"], 16 * (gi // 2 + 1))
                    if gi >= 4 and dcx == 0:
                        vector.wait_ge(SEM["d_h"], 4 * (uc - 1))
                    vector.tensor_copy(ut_b[uc % 2][:, dcx * 128:(dcx + 1) * 128],
                                       utstg[gi % 2]).then_inc(SEM["d_utc"], 1)
                if uc >= 1:
                    stt_group(uc - 1)
            stt_group(3)

        @block.tensor
        def _(tensor):
            tensor.wait_ge(SEM["d_sq"], 5)
            tensor.wait_ge(SEM["c_wf"], TC)
            hg = 0
            for uc in range(4):
                tensor.wait_ge(SEM["d_utc"], DC * (uc + 1))
                for tb in range(4):
                    pp = p4 if hg % 2 == 0 else p5
                    if hg >= 2:
                        tensor.wait_ge(SEM["d_wh"], hg - 1)
                    for dcx in range(DC):
                        ins = tensor.matmul(
                            pp[:, :], ut_b[uc % 2][:, dcx * 128:(dcx + 1) * 128],
                            xf_fm[:, dcx * 2048 + tb * 512: dcx * 2048 + tb * 512 + 512],
                            start=(dcx == 0), stop=(dcx == DC - 1), **MK)
                        if dcx == DC - 1:
                            ins.then_inc(SEM["d_h"], 1)
                    hg += 1

        if p4stage < 4:
            raise StopPhase()
        @block.sync
        def _(sync):
            sync.wait_ge(SEM["d_sq"], 5)
            for hs in range(8):
                for k in range(12):
                    gi = hs * 12 + k
                    if gi >= 2:
                        sync.wait_ge(SEM["e_cast"], gi - 1)
                    if k < 8:
                        src = w1t_d[_ts(k), hs * 512:(hs + 1) * 512]
                        dst = wstage[gi % 2][:, 0:512]
                    else:
                        src = w2t_d[_ts(hs * 4 + (k - 8)), :]
                        dst = wstage[gi % 2]
                    sync.dma_start(out=dst, in_=src
                                   ).then_inc(SEM["ee_a" if gi % 2 == 0 else "ee_b"], 16)

        @block.vector
        def _(vector):
            def fold_group(hs_):
                for k in range(16):
                    g2 = hs_ * 16 + k
                    tb, tc2 = k // 4, k % 4
                    tcg = tb * 4 + tc2
                    ob = OA if g2 % 2 == 0 else OB
                    vector.wait_ge(SEM["e_m2"], g2 + 1)
                    for dh in range(2):
                        dst = static[:, tcg * 1024 + dh * 512: tcg * 1024 + dh * 512 + 512]
                        if hs_ == 0:
                            ins = vector.tensor_copy(dst, ob[dh][:, :])
                        else:
                            ins = vector.tensor_tensor(dst, dst, ob[dh][:, :], op=ALU.add)
                        if dh == 1:
                            ins.then_inc(SEM["e_f"], 1)

            for hs in range(8):
                par = hs % 2
                for k in range(12):
                    gi = hs * 12 + k
                    vector.wait_ge(SEM["ee_a" if gi % 2 == 0 else "ee_b"], 16 * (gi // 2 + 1))
                    if hs >= 2 and k == 0:
                        vector.wait_ge(SEM["e_m2"], 16 * (hs - 1))
                    if k < 8:
                        vector.tensor_copy(w1sb2[par][:, k * 512:(k + 1) * 512],
                                           wstage[gi % 2][:, 0:512]).then_inc(SEM["e_cast"], 1)
                    else:
                        vector.tensor_copy(w2sb2[par][:, (k - 8) * 1024:(k - 7) * 1024],
                                           wstage[gi % 2]).then_inc(SEM["e_cast"], 1)
                if hs >= 1:
                    fold_group(hs - 1)
            fold_group(7)

        @block.tensor
        def _(tensor):
            tensor.wait_ge(SEM["d_wh"], 16)
            for hs in range(8):
                par = hs % 2
                tensor.wait_ge(SEM["e_cast"], 12 * (hs + 1))
                for tb in range(4):
                    gE = hs * 4 + tb
                    for hc in range(4):
                        pp = p4 if hc % 2 == 0 else p5
                        if gE * 4 + hc - 1 > 0:
                            tensor.wait_ge(SEM["e_g"], gE * 4 + hc - 1)
                        for dcx in range(DC):
                            ins = tensor.matmul(
                                pp[:, :], w1sb2[par][:, dcx * 512 + hc * 128: dcx * 512 + (hc + 1) * 128],
                                xf_fm[:, dcx * 2048 + tb * 512: dcx * 2048 + tb * 512 + 512],
                                start=(dcx == 0), stop=(dcx == DC - 1), **MK)
                            if dcx == DC - 1:
                                ins.then_inc(SEM["e_m1"], 1)
                    tensor.wait_ge(SEM["e_g"], 4 * (gE + 1))
                    for tc2 in range(4):
                        g2 = gE * 4 + tc2
                        ob = OA if g2 % 2 == 0 else OB
                        if g2 >= 2:
                            tensor.wait_ge(SEM["e_f"], g2 - 1)
                        for hc in range(4):
                            for dh in range(2):
                                ins = tensor.matmul(
                                    ob[dh][:, :],
                                    h1sb[:, hc * 512 + tc2 * 128: hc * 512 + (tc2 + 1) * 128],
                                    w2sb2[par][:, hc * 1024 + dh * 512: hc * 1024 + dh * 512 + 512],
                                    start=(hc == 0), stop=(hc == 3), **MK)
                                if hc == 3 and dh == 1:
                                    ins.then_inc(SEM["e_m2"], 1)

        @block.scalar
        def _(scalar):
            for gE in range(32):
                for hc in range(4):
                    pp = p4 if hc % 2 == 0 else p5
                    scalar.wait_ge(SEM["e_m1"], gE * 4 + hc + 1)
                    scalar.activation(out=h1sb[:, hc * 512:(hc + 1) * 512], in_=pp[:, :],
                                      func=AF.Gelu, bias=zerob[:, :], scale=1.0
                                      ).then_inc(SEM["e_g"], 1)

        if debug:
            @block.gpsimd
            def _(gpsimd):
                gpsimd.wait_ge(SEM["e_f"], 128)
                for tc in range(TC):
                    gpsimd.dma_start(out=sdbg_d[tc, :, :],
                                     in_=static[:, tc * 1024:(tc + 1) * 1024]).then_inc(SEM["dbg"], 16)

        if p4stage < 5:
            raise StopPhase()
        @block.sync
        def _(sync):
            sync.wait_ge(SEM["e_cast"], 96)
            for uc in range(4):
                if uc >= 2:
                    sync.wait_ge(SEM["f_sq"], uc - 1)
                sync.dma_start(out=wstage[uc % 2], in_=v_d[_ts(uc), :]
                               ).then_inc(SEM["fv_a" if uc % 2 == 0 else "fv_b"], 16)
            for uc in range(4):
                sync.wait_ge(SEM["f_sq"], min(uc + 2, 5))
                if uc >= 2:
                    sync.wait_ge(SEM["f_vpp"], uc - 1)
                sync.dma_start(out=wstage[uc % 2], in_=v_d[_ts(uc), :]
                               ).then_inc(SEM["fv_a" if uc % 2 == 0 else "fv_b"], 16)
            sync.wait_ge(SEM["f_vpp"], 4)
            for tc in range(TC):
                if tc >= 2:
                    sync.wait_ge(SEM["f_ve"], 3 * (tc - 2) + 3)
                sync.dma_start(out=xres[tc % 2], in_=x2buf_d[tc, :, :]
                               ).then_inc(SEM["fx_a" if tc % 2 == 0 else "fx_b"], 16)

        @block.scalar
        def _(scalar):
            for uc in range(4):
                scalar.wait_ge(SEM["fv_a" if uc % 2 == 0 else "fv_b"], 16 * (uc // 2 + 1))
                scalar.activation(out=junk[:, :], in_=wstage[uc % 2], func=AF.Square,
                                  accum_out=sv2[:, uc:uc + 1]).then_inc(SEM["f_sq"], 1)
            scalar.activation(out=sv_t[:, :], in_=sv2[:, :], func=AF.Ln, bias=eps12[:, :], scale=1.0)
            scalar.drain()
            scalar.activation(out=sv_t[:, :], in_=sv_t[:, :], func=AF.Exp, bias=zerob[:, :],
                              scale=-0.5).then_inc(SEM["f_sq"], 1)
            for tc in range(TC):
                scalar.wait_ge(SEM["f_ve"], 3 * tc + 1)
                scalar.activation(out=junk[:, :], in_=fsb, func=AF.Square,
                                  accum_out=sumsq[:, :]).then_inc(SEM["f_sc"], 1)
                scalar.wait_ge(SEM["f_ve"], 3 * tc + 2)
                scalar.activation(out=rstd[:, :], in_=var[:, :], func=AF.Ln, bias=eps5[:, :], scale=1.0)
                scalar.drain()
                scalar.activation(out=rstd[:, :], in_=rstd[:, :], func=AF.Exp, bias=zerob[:, :],
                                  scale=-0.5).then_inc(SEM["f_sc"], 1)

        @block.vector
        def _(vector):
            vector.wait_ge(SEM["f_sq"], 5)
            vector.wait_ge(SEM["e_f"], 128)
            for uc in range(4):
                vector.wait_ge(SEM["fv_a" if uc % 2 == 0 else "fv_b"], 16 * ((4 + uc) // 2 + 1))
                vector.scalar_tensor_tensor(vpp[:, uc * 1024:(uc + 1) * 1024],
                                            wstage[uc % 2], sv_t[:, uc:uc + 1], gbc[:, :],
                                            op0=ALU.mult, op1=ALU.mult).then_inc(SEM["f_vpp"], 1)
            for tc in range(TC):
                ob = OA if tc % 2 == 0 else OB
                vector.wait_ge(SEM["f_dyn"], tc + 1)
                for dh in range(2):
                    ins = vector.tensor_tensor(
                        fsb[:, dh * 512:dh * 512 + 512],
                        static[:, tc * 1024 + dh * 512: tc * 1024 + dh * 512 + 512],
                        ob[dh][:, :], op=ALU.add)
                    if dh == 1:
                        ins.then_inc(SEM["f_ve"], 1)
                vector.drain()
                vector.reduce_sum(sumx[:, :], fsb, axis=AX)
                vector.drain()
                vector.wait_ge(SEM["f_sc"], 2 * tc + 1)
                stats_seq(vector).then_inc(SEM["f_ve"], 1)
                vector.wait_ge(SEM["f_sc"], 2 * tc + 2)
                vector.tensor_scalar(fsb, fsb, mean[:, :], rstd[:, :],
                                     op0=ALU.subtract, op1=ALU.mult)
                vector.drain()
                vector.wait_ge(SEM["fx_a" if tc % 2 == 0 else "fx_b"], 16 * (tc // 2 + 1))
                if tc >= 2:
                    vector.wait_ge(SEM["fs_a" if tc % 2 == 0 else "fs_b"], 16 * (tc // 2))
                vector.tensor_tensor(ysb[tc % 2], fsb, xres[tc % 2], op=ALU.add
                                     ).then_inc(SEM["f_ve"], 1)

        @block.tensor
        def _(tensor):
            for tc in range(TC):
                ob = OA if tc % 2 == 0 else OB
                if tc == 0:
                    tensor.wait_ge(SEM["f_vpp"], 4)
                if tc >= 2:
                    tensor.wait_ge(SEM["f_ve"], 3 * (tc - 2) + 1)
                for uc in range(4):
                    for dh in range(2):
                        ins = tensor.matmul(
                            ob[dh][:, :], wh_r[:, uc * 2048 + tc * 128: uc * 2048 + (tc + 1) * 128],
                            vpp[:, uc * 1024 + dh * 512: uc * 1024 + dh * 512 + 512],
                            start=(uc == 0), stop=(uc == 3), **MK)
                        if uc == 3 and dh == 1:
                            ins.then_inc(SEM["f_dyn"], 1)

        @block.gpsimd
        def _(gpsimd):
            for tc in range(TC):
                gpsimd.wait_ge(SEM["f_ve"], 3 * tc + 3)
                gpsimd.dma_start(out=y_d[_ts(tc), :], in_=ysb[tc % 2]
                                 ).then_inc(SEM["fs_a" if tc % 2 == 0 else "fs_b"], 16)
            gpsimd.wait_ge(SEM["fs_a"], 16 * TC // 2)
            gpsimd.wait_ge(SEM["fs_b"], 16 * TC // 2)

        p4block.__exit__(None, None, None)
    except StopPhase:
        p4block.__exit__(None, None, None)
    return nc


SCALE_EXP = 1.0 / 32.0


@functools.lru_cache(maxsize=2)
def _get_nc(debug=False):
    return build_nc(debug=debug)


def build_upto(n, debug=True):
    return build_nc(debug=debug, upto=n)


def _prep_inputs(inputs, core):
    b = core // 4
    j0 = (core % 4) * JS
    x = np.asarray(inputs["L0_attn_x"], dtype=np.float32)
    c = np.ascontiguousarray
    return {
        "x": c(x[b]),
        "xj": c(x[b, j0:j0 + JS]),
        "wq": c(np.asarray(inputs["L0_attn_W_query"], np.float32)),
        "wk": c(np.asarray(inputs["L0_attn_W_key"], np.float32)),
        "wv": c(np.asarray(inputs["L0_attn_W_value"], np.float32)),
        "w1t": c(np.asarray(inputs["L0_ffn_W1"], np.float32).T),
        "w2t": c(np.asarray(inputs["L0_ffn_W2"], np.float32).T),
        "rwt": c(np.asarray(inputs["L0_ffn_router_W"], np.float32).T),
        "ut": c(np.asarray(inputs["L0_ffn_raw_U"], np.float32).T),
        "u": c(np.asarray(inputs["L0_ffn_raw_U"], np.float32)),
        "v": c(np.asarray(inputs["L0_ffn_raw_V"], np.float32)),
        "lng": c(np.asarray(inputs["L0_ffn_ln_g"], np.float32).reshape(DC, 128, 1)),
        "lnb": c(np.asarray(inputs["L0_ffn_ln_b"], np.float32).reshape(DC, 128, 1)),
        "rb": c(np.asarray(inputs["L0_ffn_router_b"], np.float32).reshape(1, NB)),
        "gamma": c(np.asarray(inputs["L0_ffn_gamma"], np.float32).reshape(1, D)),
        "ident": np.eye(128, dtype=np.float32),
    }


def kernel(**inputs):
    nc = _get_nc(False)
    in_maps = [_prep_inputs(inputs, c) for c in range(8)]
    res = run_bass_kernel_spmd(nc, in_maps, core_ids=list(range(8)))
    out = np.empty((2, SEQ, D), dtype=np.float32)
    for c in range(8):
        b = c // 4
        j0 = (c % 4) * JS
        out[b, j0:j0 + JS] = res.results[c]["y"]
    return out


# version 10
# speedup vs baseline: 1.2184x; 1.0276x over previous
import functools
import numpy as np

import concourse.bass as bass
import concourse.mybir as mybir
from concourse.bass_utils import run_bass_kernel_spmd

F32 = mybir.dt.float32
BF16 = mybir.dt.bfloat16
AF = mybir.ActivationFunctionType
ALU = mybir.AluOpType
AX = mybir.AxisListType.X

SEQ, D, HID, NB = 8192, 1024, 4096, 512
JS = 2048
NSUP, SUP = 8, 1024
B_SHIFT = 30.0
DC = D // 128
TC = JS // 128
XC = SEQ // 128
MK = dict(skip_group_check=True)


class StopPhase(Exception):
    pass


def _ts(i, n=128):
    return slice(i * n, (i + 1) * n)


def build_nc(debug=False, upto=5, p4stage=9):
    nc = bass.Bass("TRN2", target_bir_lowering=False)

    x_d = nc.dram_tensor("x", [SEQ, D], F32, kind="ExternalInput").ap()
    xj_d = nc.dram_tensor("xj", [JS, D], F32, kind="ExternalInput").ap()
    wq_d = nc.dram_tensor("wq", [D, D], F32, kind="ExternalInput").ap()
    wk_d = nc.dram_tensor("wk", [D, D], F32, kind="ExternalInput").ap()
    wv_d = nc.dram_tensor("wv", [D, D], F32, kind="ExternalInput").ap()
    w1t_d = nc.dram_tensor("w1t", [D, HID], F32, kind="ExternalInput").ap()
    w2t_d = nc.dram_tensor("w2t", [HID, D], F32, kind="ExternalInput").ap()
    rwt_d = nc.dram_tensor("rwt", [D, NB], F32, kind="ExternalInput").ap()
    ut_d = nc.dram_tensor("ut", [D, NB], F32, kind="ExternalInput").ap()
    u_d = nc.dram_tensor("u", [NB, D], F32, kind="ExternalInput").ap()
    v_d = nc.dram_tensor("v", [NB, D], F32, kind="ExternalInput").ap()
    lng_d = nc.dram_tensor("lng", [DC, 128, 1], F32, kind="ExternalInput").ap()
    lnb_d = nc.dram_tensor("lnb", [DC, 128, 1], F32, kind="ExternalInput").ap()
    rb_d = nc.dram_tensor("rb", [1, NB], F32, kind="ExternalInput").ap()
    gam_d = nc.dram_tensor("gamma", [1, D], F32, kind="ExternalInput").ap()
    id_d = nc.dram_tensor("ident", [128, 128], F32, kind="ExternalInput").ap()
    y_d = nc.dram_tensor("y", [JS, D], F32, kind="ExternalOutput").ap()

    ikind = "ExternalOutput" if debug else "Internal"
    qbuf_d = nc.dram_tensor("qbuf", [NSUP, DC, 128, SUP], BF16, kind=ikind).ap()
    vbuf_d = nc.dram_tensor("vbuf", [XC, 128, D], BF16, kind=ikind).ap()
    x2buf_d = nc.dram_tensor("x2buf", [TC, 128, D], F32, kind=ikind).ap()
    if debug:
        odbg_d = nc.dram_tensor("odbg", [TC, 128, D], F32, kind="ExternalOutput").ap()
        ldbg_d = nc.dram_tensor("ldbg", [128, TC], F32, kind="ExternalOutput").ap()
        kdbg_d = nc.dram_tensor("kdbg", [DC, 128, JS], BF16, kind="ExternalOutput").ap()
        adbg_d = nc.dram_tensor("adbg", [TC, 128, NB], F32, kind="ExternalOutput").ap()
        wdbg_d = nc.dram_tensor("wdbg", [4, 128, JS], BF16, kind="ExternalOutput").ap()
        sdbg_d = nc.dram_tensor("sdbg", [TC, 128, D], F32, kind="ExternalOutput").ap()
        xsdbg_d = nc.dram_tensor("xsdbg", [128, 8192], BF16, kind="ExternalOutput").ap()
        stdbg_d = nc.dram_tensor("stdbg", [128, 8], F32, kind="ExternalOutput").ap()
        xfdbg_d = nc.dram_tensor("xfdbg", [128, 16384], BF16, kind="ExternalOutput").ap()
        zfdbg_d = nc.dram_tensor("zfdbg", [128, 16384], BF16, kind="ExternalOutput").ap()
        wqdbg_d = nc.dram_tensor("wqdbg", [128, 8192], BF16, kind="ExternalOutput").ap()

    BB = nc.alloc_sbuf_tensor("BB", [128, 49152], BF16).ap()
    FF = nc.alloc_sbuf_tensor("FF", [128, 16384], F32).ap()
    stg = nc.alloc_sbuf_tensor("stg", [128, 6144], F32).ap()
    junk = nc.alloc_sbuf_tensor("junkb", [128, 1024], BF16).ap()

    A = nc.alloc_sbuf_tensor
    ident_b = A("ident_b", [128, 128], BF16).ap()
    ones_b = A("ones_b", [128, 1], BF16).ap()
    ones1f = A("ones1f", [1, 128], F32).ap()
    negB = A("negB", [128, 1], F32).ap()
    zerob = A("zerob", [128, 1], F32).ap()
    eps5 = A("eps5", [128, 1], F32).ap()
    onesf = A("onesf", [128, 1], F32).ap()
    eps12 = A("eps12", [128, 1], F32).ap()
    g_t = A("g_t", [128, DC], F32).ap()
    b_t = A("b_t", [128, DC], F32).ap()
    su2 = A("su2", [128, 4], F32).ap()
    su_t = A("su_t", [128, 4], F32).ap()
    sv2 = A("sv2", [128, 4], F32).ap()
    sv_t = A("sv_t", [128, 4], F32).ap()
    SSt = A("SSt", [128, 16], F32).ap()
    SS2 = A("SS2", [128, 16], F32).ap()
    fac = A("fac", [128, 16], F32).ap()
    l_sb = A("l_sb", [128, 16], F32).ap()
    top8 = A("top8", [128, 8], F32).ap()
    rb_row = A("rb_row", [1, NB], F32).ap()
    brow = A("brow", [1, NB], F32).ap()
    grow = A("grow", [1, D], F32).ap()
    sumx = A("sumx", [128, 1], F32).ap()
    sumsq = A("sumsq", [128, 1], F32).ap()
    mean = A("mean", [128, 1], F32).ap()
    var = A("var", [128, 1], F32).ap()
    rstd = A("rstd", [128, 1], F32).ap()
    msq = A("msq", [128, 1], F32).ap()
    lt = A("lt", [128, 1], F32).ap()
    lt1 = A("lt1", [128, 1], F32).ap()
    vtmp = A("vtmp", [128, 1], F32).ap()
    linv = A("linv", [128, 16], F32).ap()
    stdbg = A("stdbg_s", [128, 8], F32).ap()

    wqb = BB[:, 0:8192]
    wkb = BB[:, 8192:16384]
    wvb = BB[:, 16384:24576]
    xsup = [BB[:, 24576:32768], BB[:, 32768:40960]]
    xln = [BB[:, 40960:41984], BB[:, 41984:43008]]
    xln4 = [BB[:, 40960:41984], BB[:, 41984:43008],
            BB[:, 43008:44032], BB[:, 44032:45056]]
    qsb = [BB[:, 45056:46080], BB[:, 46080:47104]]
    xjf = BB[:, 24576:40960]
    kq = [BB[:, 16384:24576], BB[:, 40960:49152]]

    def kfm(e):
        return kq[e // 4][:, (e % 4) * 2048:(e % 4) * 2048 + 2048]

    qstream = BB[:, 0:8192]
    vstream = BB[:, 8192:16384]
    ppair = [BB[:, 24576:25088], BB[:, 25088:25600]]
    xf_fm = BB[:, 0:16384]
    z_fm = BB[:, 16384:32768]
    xf_tm = BB[:, 40960:41984]
    z_tm = BB[:, 41984:43008]
    rw_b = BB[:, 43008:47104]
    w_tm = BB[:, 32768:40960]
    w_fmr = BB[:, 40960:49152]
    wh_r = BB[:, 16384:24576]
    ut_b = [BB[:, 24576:25600], BB[:, 25600:26624]]
    w1sb2 = [BB[:, 24576:28672], BB[:, 28672:32768]]
    w2sb2 = [BB[:, 32768:36864], BB[:, 36864:40960]]
    h1sb = BB[:, 43008:47104]
    vpp = BB[:, 0:4096]

    O_sb = FF[:, 0:16384]
    wmask = FF[:, 0:8192]
    mge = FF[:, 8192:8704]
    static = FF[:, 0:16384]

    xstg = [stg[:, 0:1024], stg[:, 1024:2048]]
    xstg4 = [stg[:, 0:1024], stg[:, 1024:2048], stg[:, 2048:3072], stg[:, 3072:4096]]
    XSEM = ("x_a", "x_b", "x_c", "x_d")
    xres = [stg[:, 0:1024], stg[:, 1024:2048]]
    x2sb = [stg[:, 2048:3072], stg[:, 3072:4096]]
    xf32 = stg[:, 4096:5120]
    gbc = stg[:, 0:1024]
    wstage = [stg[:, 1024:2048], stg[:, 2048:3072]]
    bbc = stg[:, 3072:3584]
    clipb = [stg[:, 3584:4096], stg[:, 4096:4608]]
    alphab = [stg[:, 4608:5120], stg[:, 5120:5632]]
    utstg = [stg[:, 3584:3712], stg[:, 3712:3840]]
    wstgE = [stg[:, 1024:3072], stg[:, 3072:5120]]
    fsb = stg[:, 4608:5632]
    ysb = [stg[:, 2048:3072], stg[:, 3072:4096]]

    P = nc.alloc_psum_tensor
    p0 = P("p0", [128, 512], F32).ap()
    p1 = P("p1", [128, 512], F32).ap()
    p2 = P("p2", [128, 512], F32).ap()
    p3 = P("p3", [128, 512], F32).ap()
    p4 = P("p4", [128, 512], F32).ap()
    p5 = P("p5", [128, 512], F32).ap()
    p6 = P("p6", [128, 512], F32).ap()
    p_t = P("p_t", [128, 1024], BF16).ap()
    p_t2 = p4.bitcast(BF16)

    def ptb(gi):
        return p_t if gi % 2 == 0 else p_t2

    def stats_seq(vector):
        vector.tensor_scalar(mean[:, :], sumx[:, :], 1.0 / D, None, op0=ALU.mult)
        vector.tensor_scalar(vtmp[:, :], sumsq[:, :], 1.0 / D, None, op0=ALU.mult)
        vector.tensor_tensor(msq[:, :], mean[:, :], mean[:, :], op=ALU.mult)
        vector.drain()
        return vector.tensor_tensor(var[:, :], vtmp[:, :], msq[:, :], op=ALU.subtract)

    sem_names = [
        "s0a", "s0b", "s0i", "x_a", "x_b", "x_c", "x_d", "kx_a", "kx_b", "sp_a", "sp_b",
        "qv_a", "qv_b", "x3_a", "x3_b", "sp3_a", "sp3_b", "ra_a", "ra_b",
        "du_a", "du_b", "dut_a", "dut_b", "ee_a", "ee_b", "fv_a", "fv_b",
        "fx_a", "fx_b", "fs_a", "fs_b",
        "s0_ld", "s0_v",
        "s1_x", "s1_sq", "s1_var", "s1_rstd", "s1_app", "s1_tr", "s1_ev",
        "s1_pmm", "s1_pev", "s1_sp",
        "k_x", "k_sq", "k_var", "k_rstd", "k_app", "k_tr", "k_ev", "k_mm", "k_kev",
        "s2_qv", "s2_smm", "s2_sp", "s2_pv", "s2_fold",
        "s3_x", "s3_s", "s3_v", "s3_tr", "s3_sp",
        "a_ld", "a_te", "a_ve",
        "r_mm", "r_ev", "r_sp", "r_tk",
        "c_w", "c_wt", "c_wf",
        "d_ld", "d_sq", "d_utc", "d_h", "d_wh",
        "e_ld", "e_cast", "e_m1", "e_g", "e_m2", "e_f",
        "f_ld", "f_sq", "f_vpp", "f_dyn", "f_ve", "f_sc", "f_x2", "f_st",
        "dbg",
    ]
    SEM = {n: nc.alloc_semaphore(name=n) for n in sem_names}
    lo = min(s.num for s in SEM.values())
    hi = max(s.num for s in SEM.values())
    nc.gpsimd.sem_clear(range(lo, hi + 1))
    nc.all_engine_barrier()

    with nc.Block("p0") as block:
        @block.sync
        def _(sync):
            sync.dma_start(out=stg[:, 5632:5760], in_=id_d[:, :]).then_inc(SEM["s0i"], 16)
            for wi, wd in enumerate((wq_d, wk_d, wv_d)):
                for c in range(DC):
                    gi = wi * DC + c
                    if gi >= 2:
                        sync.wait_ge(SEM["s0_v"], gi - 1)
                    sync.dma_start(out=xstg[gi % 2], in_=wd[_ts(c), :]
                                   ).then_inc(SEM["s0a" if gi % 2 == 0 else "s0b"], 16)

        @block.scalar
        def _(scalar):
            for fn in (AF.Exp, AF.Ln, AF.Gelu, AF.Tanh, AF.Square):
                scalar.activation(out=junk[0:1, 0:1], in_=junk[0:1, 0:1], func=fn,
                                  bias=zerob[0:1, :], scale=1.0)
                scalar.drain()

        @block.vector
        def _(vector):
            vector.memset(ones_b[:, :], 1.0)
            vector.memset(ones1f[:, :], 1.0)
            vector.memset(negB[:, :], -B_SHIFT)
            vector.memset(zerob[:, :], 0.0)
            vector.memset(eps5[:, :], 1e-5)
            vector.memset(onesf[:, :], 1.0)
            vector.memset(eps12[:, :], 1e-12)
            vector.wait_ge(SEM["s0i"], 16)
            vector.tensor_copy(ident_b[:, :], stg[:, 5632:5760])
            for wi in range(3):
                dst = (wqb, wkb, wvb)[wi]
                for c in range(DC):
                    gi = wi * DC + c
                    vector.wait_ge(SEM["s0a" if gi % 2 == 0 else "s0b"], 16 * (gi // 2 + 1))
                    vector.tensor_copy(dst[:, c * 1024:(c + 1) * 1024],
                                       xstg[gi % 2]).then_inc(SEM["s0_v"], 1)

    def emit_ln_sync(sync, n, src_d, sx, sapp):
        sxa, sxb = sx
        for gi in range(n):
            if gi >= 2:
                sync.wait_ge(SEM[sapp], gi - 1)
            sync.dma_start(out=xstg[gi % 2], in_=src_d[_ts(gi), :]
                           ).then_inc(SEM[sxa if gi % 2 == 0 else sxb], 16)

    def emit_ln_vector(vector, n, sx, ssq, svar, srstd, sapp, str_, sev, dst_of):
        sxa, sxb = sx
        for gi in range(n):
            xc = xstg[gi % 2]
            vector.wait_ge(SEM[sxa if gi % 2 == 0 else sxb], 16 * (gi // 2 + 1))
            vector.reduce_sum(sumx[:, :], xc, axis=AX)
            vector.drain()
            vector.wait_ge(SEM[ssq], gi + 1)
            stats_seq(vector).then_inc(SEM[svar], 1)
            vector.wait_ge(SEM[srstd], gi + 1)
            vector.tensor_scalar(xln[gi % 2], xc, mean[:, :], rstd[:, :],
                                 op0=ALU.subtract, op1=ALU.mult).then_inc(SEM[sapp], 1)
            vector.wait_ge(SEM[str_], gi + 1)
            vector.tensor_copy(dst_of(gi), ptb(gi).rearrange("p (d t) -> p d t", d=DC)
                               ).then_inc(SEM[sev], 1)

    def emit_ln_scalar(scalar, n, sx, ssq, svar, srstd):
        sxa, sxb = sx
        for gi in range(n):
            xc = xstg[gi % 2]
            scalar.wait_ge(SEM[sxa if gi % 2 == 0 else sxb], 16 * (gi // 2 + 1))
            scalar.activation(out=junk[:, :], in_=xc, func=AF.Square,
                              accum_out=sumsq[:, :]).then_inc(SEM[ssq], 1)
            scalar.wait_ge(SEM[svar], gi + 1)
            scalar.activation(out=rstd[:, :], in_=var[:, :], func=AF.Ln, bias=eps5[:, :], scale=1.0)
            scalar.drain()
            scalar.activation(out=rstd[:, :], in_=rstd[:, :], func=AF.Exp, bias=zerob[:, :],
                              scale=-0.5).then_inc(SEM[srstd], 1)

    def emit_tr_chunk(tensor, gi, sapp, sev, str_, src4=False):
        tensor.wait_ge(SEM[sapp], gi + 1)
        if gi >= 2:
            tensor.wait_ge(SEM[sev], gi - 1)
        src = xln4[gi % 4] if src4 else xln[gi % 2]
        for dcx in range(DC):
            ins = tensor.transpose(ptb(gi)[:, dcx * 128:(dcx + 1) * 128],
                                   src[:, dcx * 128:(dcx + 1) * 128],
                                   ident_b[:, :])
            if dcx == DC - 1:
                ins.then_inc(SEM[str_], 1)

    if upto < 1:
        return nc
    with nc.Block("p1") as block:
        @block.sync
        def _(sync):
            for gi in range(XC):
                if gi >= 4:
                    sync.wait_ge(SEM["s1_app"], gi - 3)
                sync.dma_start(out=xstg4[gi % 4], in_=x_d[_ts(gi), :]
                               ).then_inc(SEM[XSEM[gi % 4]], 16)

        @block.scalar
        def _(scalar):
            for gi in range(XC):
                xc = xstg4[gi % 4]
                scalar.wait_ge(SEM[XSEM[gi % 4]], 16 * (gi // 4 + 1))
                scalar.activation(out=junk[:, :], in_=xc, func=AF.Square,
                                  accum_out=sumsq[:, :]).then_inc(SEM["s1_sq"], 1)
                scalar.wait_ge(SEM["s1_var"], gi + 1)
                scalar.activation(out=rstd[:, :], in_=var[:, :], func=AF.Ln, bias=eps5[:, :], scale=1.0)
                scalar.drain()
                scalar.activation(out=rstd[:, :], in_=rstd[:, :], func=AF.Exp, bias=zerob[:, :],
                                  scale=-0.5).then_inc(SEM["s1_rstd"], 1)

        @block.tensor
        def _(tensor):
            pg = 0
            for c in range(8):
                emit_tr_chunk(tensor, c, "s1_app", "s1_ev", "s1_tr", src4=True)
            for s in range(NSUP):
                def maybe_trans(k, s_=s):
                    if s_ + 1 < NSUP and k % 2 == 1:
                        c2 = k // 2
                        emit_tr_chunk(tensor, (s_ + 1) * 8 + c2, "s1_app", "s1_ev", "s1_tr",
                                      src4=True)
                tensor.wait_ge(SEM["s1_ev"], (s + 1) * 8)
                for e in range(DC):
                    pp = (p0, p1) if pg % 2 == 0 else (p2, p3)
                    if pg >= 2:
                        tensor.wait_ge(SEM["s1_pev"], pg - 1)
                    for dcx in range(DC):
                        for th in range(2):
                            ins = tensor.matmul(
                                pp[th][:, :],
                                wqb[:, dcx * 1024 + e * 128: dcx * 1024 + (e + 1) * 128],
                                xsup[s % 2][:, dcx * 1024 + th * 512: dcx * 1024 + th * 512 + 512],
                                start=(dcx == 0), stop=(dcx == DC - 1), **MK)
                            if dcx == DC - 1 and th == 1:
                                ins.then_inc(SEM["s1_pmm"], 1)
                    maybe_trans(pg % 16)
                    pg += 1
                for c in range(8):
                    pp = (p0, p1) if pg % 2 == 0 else (p2, p3)
                    if pg >= 2:
                        tensor.wait_ge(SEM["s1_pev"], pg - 1)
                    for dcx in range(DC):
                        for eh in range(2):
                            ins = tensor.matmul(
                                pp[eh][:, :],
                                xsup[s % 2][:, dcx * 1024 + c * 128: dcx * 1024 + (c + 1) * 128],
                                wvb[:, dcx * 1024 + eh * 512: dcx * 1024 + eh * 512 + 512],
                                start=(dcx == 0), stop=(dcx == DC - 1), **MK)
                            if dcx == DC - 1 and eh == 1:
                                ins.then_inc(SEM["s1_pmm"], 1)
                    maybe_trans(pg % 16)
                    pg += 1

        @block.vector
        def _(vector):
            def ln_sa(s, c):
                gi = s * 8 + c
                xc = xstg4[gi % 4]
                vector.wait_ge(SEM[XSEM[gi % 4]], 16 * (gi // 4 + 1))
                vector.reduce_sum(sumx[:, :], xc, axis=AX)
                vector.drain()
                vector.wait_ge(SEM["s1_sq"], gi + 1)
                stats_seq(vector).then_inc(SEM["s1_var"], 1)
                vector.wait_ge(SEM["s1_rstd"], gi + 1)
                if gi >= 4:
                    vector.wait_ge(SEM["s1_tr"], gi - 3)
                vector.tensor_scalar(xln4[gi % 4], xc, mean[:, :], rstd[:, :],
                                     op0=ALU.subtract, op1=ALU.mult).then_inc(SEM["s1_app"], 1)

            def ln_ev(s, c):
                gi = s * 8 + c
                vector.wait_ge(SEM["s1_tr"], gi + 1)
                if s >= 2 and c == 0:
                    vector.wait_ge(SEM["s1_pmm"], 16 * (s - 1))
                dst = xsup[s % 2].rearrange("p (d t) -> p d t", d=DC)[:, :, c * 128:(c + 1) * 128]
                vector.tensor_copy(dst, ptb(gi).rearrange("p (d t) -> p d t", d=DC)
                                   ).then_inc(SEM["s1_ev"], 1)

            for c in range(8):
                ln_sa(0, c)
                ln_ev(0, c)
            for s in range(NSUP):
                if s + 1 < NSUP:
                    ln_sa(s + 1, 0)
                    ln_sa(s + 1, 1)
                for k in range(16):
                    pg = s * 16 + k
                    pp = (p0, p1) if pg % 2 == 0 else (p2, p3)
                    vector.wait_ge(SEM["s1_pmm"], pg + 1)
                    if pg >= 2:
                        vector.wait_ge(SEM["sp_a" if pg % 2 == 0 else "sp_b"], 16 * (pg // 2))
                    vector.tensor_copy(qsb[pg % 2][:, 0:512], pp[0][:, :])
                    vector.tensor_copy(qsb[pg % 2][:, 512:1024], pp[1][:, :]
                                       ).then_inc(SEM["s1_pev"], 1)
                    if s + 1 < NSUP and k % 2 == 1:
                        c2 = k // 2
                        ln_ev(s + 1, c2)
                        if c2 + 2 <= 7:
                            ln_sa(s + 1, c2 + 2)

        @block.gpsimd
        def _(gpsimd):
            if debug:
                gpsimd.wait_ge(SEM["s1_ev"], 8)
                gpsimd.dma_start(out=xsdbg_d[:, :], in_=xsup[0]).then_inc(SEM["dbg"], 16)
                gpsimd.dma_start(out=stdbg_d[:, :], in_=stdbg[:, :]).then_inc(SEM["dbg"], 16)
                gpsimd.dma_start(out=wqdbg_d[:, :], in_=wqb).then_inc(SEM["dbg"], 16)
            for s in range(NSUP):
                for e in range(DC):
                    pg = s * 16 + e
                    gpsimd.wait_ge(SEM["s1_pev"], pg + 1)
                    gpsimd.dma_start(out=qbuf_d[s, e, :, :], in_=qsb[pg % 2]
                                     ).then_inc(SEM["sp_a" if pg % 2 == 0 else "sp_b"], 16)
                for c in range(8):
                    pg = s * 16 + 8 + c
                    gpsimd.wait_ge(SEM["s1_pev"], pg + 1)
                    gpsimd.dma_start(out=vbuf_d[s * 8 + c, :, :], in_=qsb[pg % 2]
                                     ).then_inc(SEM["sp_a" if pg % 2 == 0 else "sp_b"], 16)
            gpsimd.wait_ge(SEM["sp_a"], 16 * NSUP * 8)
            gpsimd.wait_ge(SEM["sp_b"], 16 * NSUP * 8)

    if upto < 2:
        return nc
    with nc.Block("p1k") as block:
        @block.sync
        def _(sync):
            emit_ln_sync(sync, TC, xj_d, ("kx_a", "kx_b"), "k_app")

        @block.scalar
        def _(scalar):
            emit_ln_scalar(scalar, TC, ("kx_a", "kx_b"), "k_sq", "k_var", "k_rstd")

        @block.tensor
        def _(tensor):
            for gi in range(TC):
                emit_tr_chunk(tensor, gi, "k_app", "k_ev", "k_tr")
            tensor.wait_ge(SEM["k_ev"], TC)
            kg = 0
            for e in range(DC):
                for tb in range(4):
                    pp = p0 if kg % 2 == 0 else p1
                    if kg >= 2:
                        tensor.wait_ge(SEM["k_kev"], kg - 1)
                    for dcx in range(DC):
                        ins = tensor.matmul(
                            pp[:, :],
                            wkb[:, dcx * 1024 + e * 128: dcx * 1024 + (e + 1) * 128],
                            xjf[:, dcx * 2048 + tb * 512: dcx * 2048 + tb * 512 + 512],
                            start=(dcx == 0), stop=(dcx == DC - 1), **MK)
                        if dcx == DC - 1:
                            ins.then_inc(SEM["k_mm"], 1)
                    kg += 1

        @block.vector
        def _(vector):
            emit_ln_vector(vector, TC, ("kx_a", "kx_b"), "k_sq", "k_var", "k_rstd", "k_app", "k_tr", "k_ev",
                           lambda gi: xjf.rearrange("p (d t) -> p d t", d=DC)[:, :, gi * 128:(gi + 1) * 128])
            for kg in range(DC * 4):
                e, tb = kg // 4, kg % 4
                pp = p0 if kg % 2 == 0 else p1
                vector.wait_ge(SEM["k_mm"], kg + 1)
                vector.tensor_copy(kfm(e)[:, tb * 512:tb * 512 + 512], pp[:, :]
                                   ).then_inc(SEM["k_kev"], 1)

        if debug:
            @block.gpsimd
            def _(gpsimd):
                gpsimd.wait_ge(SEM["k_kev"], DC * 4)
                for e in range(DC):
                    gpsimd.dma_start(out=kdbg_d[e, :, :], in_=kfm(e)).then_inc(SEM["dbg"], 16)
                gpsimd.wait_ge(SEM["dbg"], 16 * DC)

    if upto < 3:
        return nc
    OA = (p0, p1)
    OB = (p2, p3)
    SBK = (p4, p5)

    with nc.Block("p2") as block:
        @block.sync
        def _(sync):
            for s in range(16):
                if s >= 2:
                    sync.wait_ge(SEM["s2_pv"], 16 * (s - 1))
                qdst = qstream[:, (s % 2) * 4096:(s % 2) * 4096 + 4096]
                vdst = vstream[:, (s % 2) * 4096:(s % 2) * 4096 + 4096]
                qs = "qv_a" if s % 2 == 0 else "qv_b"
                sync.dma_start(out=qdst.rearrange("p (e t) -> p e t", e=DC),
                               in_=qbuf_d[s // 2, :, :, (s % 2) * 512:(s % 2) * 512 + 512]
                               .transpose([1, 0, 2])).then_inc(SEM[qs], 16)
                sync.dma_start(out=vdst.rearrange("p (c e) -> p c e", c=4),
                               in_=vbuf_d[s * 4:(s + 1) * 4].transpose([1, 0, 2])
                               ).then_inc(SEM[qs], 16)

        @block.tensor
        def _(tensor):
            def emit_S(g):
                s = g // 16
                jb = (g // 2) % 8
                t = g % 2
                sb = (s % 2) * 4096
                if t == 0 and jb == 0:
                    tensor.wait_ge(SEM["qv_a" if s % 2 == 0 else "qv_b"], 32 * (s // 2 + 1))
                if g >= 2:
                    tensor.wait_ge(SEM["s2_sp"], g - 1)
                for icl in range(2):
                    ic = t * 2 + icl
                    for e in range(DC):
                        ins = tensor.matmul(
                            SBK[g % 2][:, icl * 256:icl * 256 + 256],
                            qstream[:, sb + e * 512 + ic * 128: sb + e * 512 + (ic + 1) * 128],
                            kfm(e)[:, jb * 256: jb * 256 + 256],
                            start=(e == 0), stop=(e == DC - 1), **MK)
                        if icl == 1 and e == DC - 1:
                            ins.then_inc(SEM["s2_smm"], 1)

            def emit_PV(g):
                G = g // 2
                s = g // 16
                t = g % 2
                sb = (s % 2) * 4096
                tensor.wait_ge(SEM["s2_sp"], g + 1)
                if t == 0:
                    tensor.wait_ge(SEM["s2_fold"], G)
                par = G % 2
                for icl in range(2):
                    ic = t * 2 + icl
                    for jsub in range(2):
                        lhsT = ppair[g % 2][:, icl * 256 + jsub * 128: icl * 256 + (jsub + 1) * 128]
                        ob = OA if jsub == 0 else OB
                        for eh in range(2):
                            tensor.matmul(ob[eh][:, :], lhsT,
                                          vstream[:, sb + ic * 1024 + eh * 512: sb + ic * 1024 + eh * 512 + 512],
                                          start=(ic == 0), stop=(ic == 3), **MK)
                        slot = par * 16 + jsub * 8 + ic
                        ins = tensor.matmul(p6[:, slot:slot + 1], lhsT, ones_b[:, :],
                                            start=True, stop=True, **MK)
                        if icl == 1 and jsub == 1:
                            ins.then_inc(SEM["s2_pv"], 1)

            for s in range(16):
                base = s * 16
                emit_S(base)
                for t in range(15):
                    emit_S(base + t + 1)
                    emit_PV(base + t)
                emit_PV(base + 15)

        @block.scalar
        def _(scalar):
            for g in range(256):
                scalar.wait_ge(SEM["s2_smm"], g + 1)
                if g >= 2:
                    scalar.wait_ge(SEM["s2_pv"], g - 1)
                scalar.activation(out=ppair[g % 2][:, :], in_=SBK[g % 2][:, :], func=AF.Exp,
                                  bias=negB[:, :], scale=SCALE_EXP).then_inc(SEM["s2_sp"], 1)

        @block.vector
        def _(vector):
            for G in range(16 * 8):
                s, jb = G // 8, G % 8
                par = G % 2
                vector.wait_ge(SEM["s2_pv"], 2 * (G + 1))
                for jsub in range(2):
                    jc = jb * 2 + jsub
                    ob = OA if jsub == 0 else OB
                    ltx = lt if jsub == 0 else lt1
                    sl = par * 16 + jsub * 8
                    vector.reduce_sum(ltx[:, :], p6[:, sl:sl + 4], axis=AX)
                    for eh in range(2):
                        dst = O_sb[:, jc * 1024 + eh * 512: jc * 1024 + eh * 512 + 512]
                        if s == 0:
                            vector.tensor_copy(dst, ob[eh][:, :])
                        else:
                            vector.tensor_tensor(dst, dst, ob[eh][:, :], op=ALU.add)
                    if s == 0:
                        ins = vector.tensor_copy(l_sb[:, jc:jc + 1], ltx[:, :])
                    else:
                        ins = vector.tensor_tensor(l_sb[:, jc:jc + 1], l_sb[:, jc:jc + 1],
                                                   ltx[:, :], op=ALU.add)
                    if jsub == 1:
                        ins.then_inc(SEM["s2_fold"], 1)

    if upto < 4:
        return nc
    with nc.Block("p3") as block:
        @block.sync
        def _(sync):
            for tc in range(TC):
                if tc >= 2:
                    sync.wait_ge(SEM["s3_v"], 9 * (tc - 2) + 3)
                sync.dma_start(out=xres[tc % 2], in_=xj_d[_ts(tc), :]
                               ).then_inc(SEM["x3_a" if tc % 2 == 0 else "x3_b"], 16)

        @block.vector
        def _(vector):
            vector.reciprocal(linv[:, :], l_sb[:, :])
            vector.drain()
            for tc in range(TC):
                oc = O_sb[:, tc * 1024:(tc + 1) * 1024]
                vector.tensor_scalar(oc, oc, linv[:, tc:tc + 1], None, op0=ALU.mult
                                     ).then_inc(SEM["s3_v"], 1)
                vector.drain()
                vector.reduce_sum(sumx[:, :], oc, axis=AX)
                vector.drain()
                vector.wait_ge(SEM["s3_s"], 6 * tc + 1)
                stats_seq(vector).then_inc(SEM["s3_v"], 1)
                vector.wait_ge(SEM["s3_s"], 6 * tc + 2)
                if debug:
                    vector.wait_ge(SEM["dbg"], 16 * (DC + 1 + tc + 1))
                vector.tensor_scalar(oc, oc, mean[:, :], rstd[:, :],
                                     op0=ALU.subtract, op1=ALU.mult)
                vector.drain()
                vector.wait_ge(SEM["x3_a" if tc % 2 == 0 else "x3_b"], 16 * (tc // 2 + 1))
                if tc >= 2:
                    vector.wait_ge(SEM["sp3_a" if tc % 2 == 0 else "sp3_b"], 16 * (tc // 2))
                vector.tensor_tensor(x2sb[tc % 2], oc, xres[tc % 2], op=ALU.add
                                     ).then_inc(SEM["s3_v"], 1)
                vector.drain()
                vector.reduce_sum(sumx[:, :], x2sb[tc % 2], axis=AX)
                vector.drain()
                vector.wait_ge(SEM["s3_s"], 6 * tc + 3)
                stats_seq(vector).then_inc(SEM["s3_v"], 1)
                vector.wait_ge(SEM["s3_s"], 6 * tc + 4)
                vector.tensor_scalar(xf32, x2sb[tc % 2], mean[:, :], rstd[:, :],
                                     op0=ALU.subtract, op1=ALU.mult)
                vector.drain()
                vector.tensor_copy(xf_tm, xf32).then_inc(SEM["s3_v"], 1)
                vector.reduce_sum(sumx[:, :], xf32, axis=AX)
                vector.drain()
                vector.wait_ge(SEM["s3_s"], 6 * tc + 5)
                stats_seq(vector).then_inc(SEM["s3_v"], 1)
                vector.wait_ge(SEM["s3_s"], 6 * tc + 6)
                vector.tensor_scalar(z_tm, xf32, mean[:, :], rstd[:, :],
                                     op0=ALU.subtract, op1=ALU.mult).then_inc(SEM["s3_v"], 1)
                vector.wait_ge(SEM["s3_tr"], 2 * tc + 1)
                vector.tensor_copy(
                    xf_fm.rearrange("p (d t) -> p d t", d=DC)[:, :, tc * 128:(tc + 1) * 128],
                    p_t.rearrange("p (d t) -> p d t", d=DC)).then_inc(SEM["s3_v"], 1)
                vector.wait_ge(SEM["s3_tr"], 2 * tc + 2)
                vector.tensor_copy(
                    z_fm.rearrange("p (d t) -> p d t", d=DC)[:, :, tc * 128:(tc + 1) * 128],
                    p_t2.rearrange("p (d t) -> p d t", d=DC)).then_inc(SEM["s3_v"], 1)

        @block.scalar
        def _(scalar):
            for tc in range(TC):
                for j, (srcv, wv_) in enumerate(((O_sb[:, tc * 1024:(tc + 1) * 1024], 9 * tc + 1),
                                                 (x2sb[tc % 2], 9 * tc + 3),
                                                 (xf32, 9 * tc + 5))):
                    scalar.wait_ge(SEM["s3_v"], wv_)
                    scalar.activation(out=junk[:, :], in_=srcv, func=AF.Square,
                                      accum_out=sumsq[:, :]).then_inc(SEM["s3_s"], 1)
                    scalar.wait_ge(SEM["s3_v"], 9 * tc + 2 + 2 * j)
                    scalar.activation(out=rstd[:, :], in_=var[:, :], func=AF.Ln,
                                      bias=eps5[:, :], scale=1.0)
                    scalar.drain()
                    scalar.activation(out=rstd[:, :], in_=rstd[:, :], func=AF.Exp,
                                      bias=zerob[:, :], scale=-0.5).then_inc(SEM["s3_s"], 1)

        @block.tensor
        def _(tensor):
            for tc in range(TC):
                tensor.wait_ge(SEM["s3_v"], 9 * tc + 5)
                if tc > 0:
                    tensor.wait_ge(SEM["s3_v"], 9 * (tc - 1) + 8)
                for dcx in range(DC):
                    ins = tensor.transpose(p_t[:, dcx * 128:(dcx + 1) * 128],
                                           xf_tm[:, dcx * 128:(dcx + 1) * 128], ident_b[:, :])
                    if dcx == DC - 1:
                        ins.then_inc(SEM["s3_tr"], 1)
                tensor.wait_ge(SEM["s3_v"], 9 * tc + 7)
                if tc > 0:
                    tensor.wait_ge(SEM["s3_v"], 9 * (tc - 1) + 9)
                for dcx in range(DC):
                    ins = tensor.transpose(p_t2[:, dcx * 128:(dcx + 1) * 128],
                                           z_tm[:, dcx * 128:(dcx + 1) * 128], ident_b[:, :])
                    if dcx == DC - 1:
                        ins.then_inc(SEM["s3_tr"], 1)

        @block.gpsimd
        def _(gpsimd):
            if debug:
                gpsimd.wait_ge(SEM["s2_fold"], 64)
                gpsimd.dma_start(out=ldbg_d[:, :], in_=l_sb[:, :]).then_inc(SEM["dbg"], 16)
            for tc in range(TC):
                if debug:
                    gpsimd.wait_ge(SEM["s3_v"], 9 * tc + 1)
                    gpsimd.dma_start(out=odbg_d[tc, :, :], in_=O_sb[:, tc * 1024:(tc + 1) * 1024]
                                     ).then_inc(SEM["dbg"], 16)
                gpsimd.wait_ge(SEM["s3_v"], 9 * tc + 3)
                gpsimd.dma_start(out=x2buf_d[tc, :, :], in_=x2sb[tc % 2]
                                 ).then_inc(SEM["sp3_a" if tc % 2 == 0 else "sp3_b"], 16)
            gpsimd.wait_ge(SEM["sp3_a"], 16 * TC // 2)
            gpsimd.wait_ge(SEM["sp3_b"], 16 * TC // 2)

    if upto < 5:
        return nc
    try:
        p4block = nc.Block("p4")
        p4block.__enter__()
        block = p4block
        @block.sync
        def _(sync):
            for c in range(DC):
                sync.dma_start(out=g_t[:, c:c + 1], in_=lng_d[c, :, :]).then_inc(SEM["a_ld"], 16)
                sync.dma_start(out=b_t[:, c:c + 1], in_=lnb_d[c, :, :]).then_inc(SEM["a_ld"], 16)
            sync.dma_start(out=rb_row[:, :], in_=rb_d[:, :]).then_inc(SEM["a_ld"], 16)
            sync.dma_start(out=grow[:, :], in_=gam_d[:, :]).then_inc(SEM["a_ld"], 16)
            for c in range(DC):
                if c >= 2:
                    sync.wait_ge(SEM["a_ve"], c - 1)
                    sync.wait_ge(SEM["a_te"], c)
                sync.dma_start(out=wstage[c % 2][:, 0:512], in_=rwt_d[_ts(c), :]
                               ).then_inc(SEM["ra_a" if c % 2 == 0 else "ra_b"], 16)

        @block.tensor
        def _(tensor):
            tensor.wait_ge(SEM["a_ld"], 16 * 18)
            tensor.matmul(p4[:, :], ones1f[:, :], grow[:, 0:512], start=True, stop=True, **MK)
            tensor.matmul(p5[:, :], ones1f[:, :], grow[:, 512:1024], start=True, stop=True, **MK
                          ).then_inc(SEM["a_te"], 1)
            for c in range(DC):
                tensor.wait_ge(SEM["ra_a" if c % 2 == 0 else "ra_b"], 16 * (c // 2 + 1))
                tensor.matmul(p6[0:1, :], b_t[:, c:c + 1], wstage[c % 2][:, 0:512],
                              start=(c == 0), stop=(c == DC - 1), **MK).then_inc(SEM["a_te"], 1)
            tensor.wait_ge(SEM["a_ve"], DC + 2)
            tensor.matmul(p4[:, :], ones1f[:, :], brow[:, :], start=True, stop=True, **MK
                          ).then_inc(SEM["a_te"], 1)

        @block.vector
        def _(vector):
            vector.wait_ge(SEM["a_te"], 1)
            vector.tensor_copy(gbc[:, 0:512], p4[:, :])
            vector.tensor_copy(gbc[:, 512:1024], p5[:, :])
            for c in range(DC):
                vector.wait_ge(SEM["ra_a" if c % 2 == 0 else "ra_b"], 16 * (c // 2 + 1))
                vector.wait_ge(SEM["a_te"], c + 2)
                vector.tensor_scalar(rw_b[:, c * 512:(c + 1) * 512], wstage[c % 2][:, 0:512],
                                     g_t[:, c:c + 1], None, op0=ALU.mult).then_inc(SEM["a_ve"], 1)
            vector.wait_ge(SEM["a_te"], DC + 1)
            vector.tensor_copy(brow[:, :], p6[0:1, :]).then_inc(SEM["a_ve"], 1)
            vector.drain()
            vector.tensor_tensor(brow[:, :], brow[:, :], rb_row[:, :], op=ALU.add
                                 ).then_inc(SEM["a_ve"], 1)
            vector.wait_ge(SEM["a_te"], DC + 2)
            vector.tensor_copy(bbc[:, :], p4[:, :]).then_inc(SEM["a_ve"], 1)

        if p4stage < 1:
            raise StopPhase()
        if debug:
            @block.gpsimd
            def _(gpsimd):
                gpsimd.dma_start(out=xfdbg_d[:, :], in_=xf_fm).then_inc(SEM["dbg"], 16)
                gpsimd.dma_start(out=zfdbg_d[:, :], in_=z_fm).then_inc(SEM["dbg"], 16)

        @block.tensor
        def _(tensor):
            for tc in range(TC):
                pp = p4 if tc % 2 == 0 else p5
                if tc == 0:
                    tensor.wait_ge(SEM["a_ve"], DC + 3)
                if tc >= 2:
                    tensor.wait_ge(SEM["r_ev"], 2 * (tc - 2) + 1)
                for dcx in range(DC):
                    ins = tensor.matmul(
                        pp[:, :],
                        z_fm[:, dcx * 2048 + tc * 128: dcx * 2048 + (tc + 1) * 128],
                        rw_b[:, dcx * 512:(dcx + 1) * 512],
                        start=(dcx == 0), stop=(dcx == DC - 1), **MK)
                    if dcx == DC - 1:
                        ins.then_inc(SEM["r_mm"], 1)

        @block.vector
        def _(vector):
            for tc in range(TC):
                pp = p4 if tc % 2 == 0 else p5
                vector.wait_ge(SEM["r_mm"], tc + 1)
                if tc >= 2:
                    vector.wait_ge(SEM["r_sp"], tc - 1)
                vector.tensor_tensor(clipb[tc % 2], pp[:, :], bbc[:, :], op=ALU.add
                                     ).then_inc(SEM["r_ev"], 1)
                vector.drain()
                vector.tensor_scalar(clipb[tc % 2], clipb[tc % 2], -10.0, 10.0,
                                     op0=ALU.max, op1=ALU.min).then_inc(SEM["r_ev"], 1)
                vector.wait_ge(SEM["r_sp"], tc + 1)
                vector.max(out=top8[:, :], in_=alphab[tc % 2])
                vector.drain()
                vector.reduce_sum(SSt[:, tc:tc + 1], top8[:, :], axis=AX)
                vector.tensor_scalar(mge[:, :], alphab[tc % 2], top8[:, 7:8], None, op0=ALU.is_ge)
                vector.drain()
                vector.tensor_tensor(wmask[:, tc * 512:(tc + 1) * 512], alphab[tc % 2],
                                     mge[:, :], op=ALU.mult).then_inc(SEM["r_tk"], 1)

        @block.scalar
        def _(scalar):
            for tc in range(TC):
                scalar.wait_ge(SEM["r_ev"], 2 * tc + 2)
                if tc >= 2:
                    scalar.wait_ge(SEM["r_tk"], tc - 1)
                    if debug:
                        scalar.wait_ge(SEM["dbg"], 16 * (DC + 1 + TC) + 16 * (tc - 1))
                scalar.activation(out=alphab[tc % 2], in_=clipb[tc % 2], func=AF.Exp,
                                  bias=zerob[:, :], scale=1.0)
                scalar.drain()
                scalar.activation(out=alphab[tc % 2], in_=alphab[tc % 2], func=AF.Ln,
                                  bias=onesf[:, :], scale=1.0).then_inc(SEM["r_sp"], 1)

        if debug:
            @block.gpsimd
            def _(gpsimd):
                for tc in range(TC):
                    gpsimd.wait_ge(SEM["r_sp"], tc + 1)
                    gpsimd.dma_start(out=adbg_d[tc, :, :], in_=alphab[tc % 2]).then_inc(SEM["dbg"], 16)

        if p4stage < 2:
            raise StopPhase()
        @block.scalar
        def _(scalar):
            scalar.wait_ge(SEM["r_tk"], TC)
            scalar.activation(out=fac[:, :], in_=SSt[:, :], func=AF.Tanh, bias=zerob[:, :],
                              scale=1.0).then_inc(SEM["r_sp"], 1)

        @block.vector
        def _(vector):
            vector.wait_ge(SEM["r_sp"], TC + 1)
            vector.tensor_scalar(SS2[:, :], SSt[:, :], 1e-6, None, op0=ALU.add)
            vector.drain()
            vector.reciprocal(SS2[:, :], SS2[:, :])
            vector.drain()
            vector.tensor_tensor(fac[:, :], fac[:, :], SS2[:, :], op=ALU.mult)
            vector.drain()
            for tc in range(TC):
                vector.tensor_scalar(w_tm[:, tc * 512:(tc + 1) * 512],
                                     wmask[:, tc * 512:(tc + 1) * 512],
                                     fac[:, tc:tc + 1], None, op0=ALU.mult).then_inc(SEM["c_w"], 1)

        @block.tensor
        def _(tensor):
            for tc in range(TC):
                tensor.wait_ge(SEM["c_w"], tc + 1)
                if tc >= 2:
                    tensor.wait_ge(SEM["c_wf"], tc - 1)
                bank = ptb(tc)
                for us in range(4):
                    ins = tensor.transpose(bank[:, us * 128:(us + 1) * 128],
                                           w_tm[:, tc * 512 + us * 128: tc * 512 + (us + 1) * 128],
                                           ident_b[:, :])
                    if us == 3:
                        ins.then_inc(SEM["c_wt"], 1)

        @block.vector
        def _(vector):
            for tc in range(TC):
                vector.wait_ge(SEM["c_wt"], tc + 1)
                vector.tensor_copy(
                    w_fmr.rearrange("p (u t) -> p u t", u=4)[:, :, tc * 128:(tc + 1) * 128],
                    ptb(tc)[:, 0:512].rearrange("p (u t) -> p u t", u=4)
                    ).then_inc(SEM["c_wf"], 1)

        if debug:
            @block.gpsimd
            def _(gpsimd):
                gpsimd.wait_ge(SEM["c_wf"], TC)
                for us in range(4):
                    gpsimd.dma_start(out=wdbg_d[us, :, :],
                                     in_=w_fmr[:, us * 2048:(us + 1) * 2048]).then_inc(SEM["dbg"], 16)

        if p4stage < 3:
            raise StopPhase()
        @block.sync
        def _(sync):
            sync.wait_ge(SEM["a_ve"], DC)
            sync.wait_ge(SEM["a_te"], DC + 1)
            for uc in range(4):
                if uc >= 2:
                    sync.wait_ge(SEM["d_sq"], uc - 1)
                sync.dma_start(out=wstage[uc % 2], in_=u_d[_ts(uc), :]
                               ).then_inc(SEM["du_a" if uc % 2 == 0 else "du_b"], 16)
            sync.wait_ge(SEM["r_sp"], TC)
            for uc in range(4):
                for dcx in range(DC):
                    gi = uc * DC + dcx
                    if gi >= 2:
                        sync.wait_ge(SEM["d_utc"], gi - 1)
                    sync.dma_start(out=utstg[gi % 2], in_=ut_d[_ts(dcx), _ts(uc)]
                                   ).then_inc(SEM["dut_a" if gi % 2 == 0 else "dut_b"], 16)

        @block.scalar
        def _(scalar):
            for uc in range(4):
                scalar.wait_ge(SEM["du_a" if uc % 2 == 0 else "du_b"], 16 * (uc // 2 + 1))
                scalar.activation(out=junk[:, :], in_=wstage[uc % 2], func=AF.Square,
                                  accum_out=su2[:, uc:uc + 1]).then_inc(SEM["d_sq"], 1)
            scalar.activation(out=su_t[:, :], in_=su2[:, :], func=AF.Ln, bias=eps12[:, :], scale=1.0)
            scalar.drain()
            scalar.activation(out=su_t[:, :], in_=su_t[:, :], func=AF.Exp, bias=zerob[:, :],
                              scale=-0.5).then_inc(SEM["d_sq"], 1)

        @block.vector
        def _(vector):
            def stt_group(uc_):
                for tb in range(4):
                    hg = uc_ * 4 + tb
                    pp = p4 if hg % 2 == 0 else p5
                    vector.wait_ge(SEM["d_h"], hg + 1)
                    vector.scalar_tensor_tensor(
                        wh_r[:, uc_ * 2048 + tb * 512: uc_ * 2048 + tb * 512 + 512],
                        pp[:, :], su_t[:, uc_:uc_ + 1],
                        w_fmr[:, uc_ * 2048 + tb * 512: uc_ * 2048 + tb * 512 + 512],
                        op0=ALU.mult, op1=ALU.mult).then_inc(SEM["d_wh"], 1)

            vector.wait_ge(SEM["d_sq"], 5)
            for uc in range(4):
                for dcx in range(DC):
                    gi = uc * DC + dcx
                    vector.wait_ge(SEM["dut_a" if gi % 2 == 0 else "dut_b"], 16 * (gi // 2 + 1))
                    if gi >= 4 and dcx == 0:
                        vector.wait_ge(SEM["d_h"], 4 * (uc - 1))
                    vector.tensor_copy(ut_b[uc % 2][:, dcx * 128:(dcx + 1) * 128],
                                       utstg[gi % 2]).then_inc(SEM["d_utc"], 1)
                if uc >= 1:
                    stt_group(uc - 1)
            stt_group(3)

        @block.tensor
        def _(tensor):
            tensor.wait_ge(SEM["d_sq"], 5)
            tensor.wait_ge(SEM["c_wf"], TC)
            hg = 0
            for uc in range(4):
                tensor.wait_ge(SEM["d_utc"], DC * (uc + 1))
                for tb in range(4):
                    pp = p4 if hg % 2 == 0 else p5
                    if hg >= 2:
                        tensor.wait_ge(SEM["d_wh"], hg - 1)
                    for dcx in range(DC):
                        ins = tensor.matmul(
                            pp[:, :], ut_b[uc % 2][:, dcx * 128:(dcx + 1) * 128],
                            xf_fm[:, dcx * 2048 + tb * 512: dcx * 2048 + tb * 512 + 512],
                            start=(dcx == 0), stop=(dcx == DC - 1), **MK)
                        if dcx == DC - 1:
                            ins.then_inc(SEM["d_h"], 1)
                    hg += 1

        if p4stage < 4:
            raise StopPhase()
        @block.sync
        def _(sync):
            sync.wait_ge(SEM["d_sq"], 5)
            for hs in range(8):
                for k in range(4):
                    gi = hs * 4 + k
                    if gi >= 2:
                        sync.wait_ge(SEM["e_cast"], gi - 1)
                    if k < 2:
                        src = w1t_d[_ts(k, 512), hs * 512:(hs + 1) * 512]                             .rearrange("(c p) h -> p c h", p=128)
                    else:
                        src = w2t_d[_ts(hs * 2 + (k - 2), 256), :]                             .rearrange("(c p) h -> p c h", p=128)
                    sync.dma_start(out=wstgE[gi % 2].rearrange("p (c h) -> p c h", h=512 if k < 2 else 1024),
                                   in_=src).then_inc(SEM["ee_a" if gi % 2 == 0 else "ee_b"], 16)

        @block.vector
        def _(vector):
            def fold_group(hs_):
                for k in range(16):
                    g2 = hs_ * 16 + k
                    tb, tc2 = k // 4, k % 4
                    tcg = tb * 4 + tc2
                    ob = OA if g2 % 2 == 0 else OB
                    vector.wait_ge(SEM["e_m2"], g2 + 1)
                    for dh in range(2):
                        dst = static[:, tcg * 1024 + dh * 512: tcg * 1024 + dh * 512 + 512]
                        if hs_ == 0:
                            ins = vector.tensor_copy(dst, ob[dh][:, :])
                        else:
                            ins = vector.tensor_tensor(dst, dst, ob[dh][:, :], op=ALU.add)
                        if dh == 1:
                            ins.then_inc(SEM["e_f"], 1)

            for hs in range(8):
                par = hs % 2
                for k in range(4):
                    gi = hs * 4 + k
                    vector.wait_ge(SEM["ee_a" if gi % 2 == 0 else "ee_b"], 16 * (gi // 2 + 1))
                    if hs >= 2 and k == 0:
                        vector.wait_ge(SEM["e_m2"], 16 * (hs - 1))
                    if k < 2:
                        vector.tensor_copy(w1sb2[par][:, k * 2048:(k + 1) * 2048],
                                           wstgE[gi % 2]).then_inc(SEM["e_cast"], 1)
                    else:
                        vector.tensor_copy(w2sb2[par][:, (k - 2) * 2048:(k - 1) * 2048],
                                           wstgE[gi % 2]).then_inc(SEM["e_cast"], 1)
                if hs >= 1:
                    fold_group(hs - 1)
            fold_group(7)

        @block.tensor
        def _(tensor):
            tensor.wait_ge(SEM["d_wh"], 16)
            for hs in range(8):
                par = hs % 2
                tensor.wait_ge(SEM["e_cast"], 4 * (hs + 1))
                for tb in range(4):
                    gE = hs * 4 + tb
                    for hc in range(4):
                        pp = p4 if hc % 2 == 0 else p5
                        if gE * 4 + hc - 1 > 0:
                            tensor.wait_ge(SEM["e_g"], gE * 4 + hc - 1)
                        for dcx in range(DC):
                            ins = tensor.matmul(
                                pp[:, :], w1sb2[par][:, dcx * 512 + hc * 128: dcx * 512 + (hc + 1) * 128],
                                xf_fm[:, dcx * 2048 + tb * 512: dcx * 2048 + tb * 512 + 512],
                                start=(dcx == 0), stop=(dcx == DC - 1), **MK)
                            if dcx == DC - 1:
                                ins.then_inc(SEM["e_m1"], 1)
                    tensor.wait_ge(SEM["e_g"], 4 * (gE + 1))
                    for tc2 in range(4):
                        g2 = gE * 4 + tc2
                        ob = OA if g2 % 2 == 0 else OB
                        if g2 >= 2:
                            tensor.wait_ge(SEM["e_f"], g2 - 1)
                        for hc in range(4):
                            for dh in range(2):
                                ins = tensor.matmul(
                                    ob[dh][:, :],
                                    h1sb[:, hc * 512 + tc2 * 128: hc * 512 + (tc2 + 1) * 128],
                                    w2sb2[par][:, hc * 1024 + dh * 512: hc * 1024 + dh * 512 + 512],
                                    start=(hc == 0), stop=(hc == 3), **MK)
                                if hc == 3 and dh == 1:
                                    ins.then_inc(SEM["e_m2"], 1)

        @block.scalar
        def _(scalar):
            for gE in range(32):
                for hc in range(4):
                    pp = p4 if hc % 2 == 0 else p5
                    scalar.wait_ge(SEM["e_m1"], gE * 4 + hc + 1)
                    scalar.activation(out=h1sb[:, hc * 512:(hc + 1) * 512], in_=pp[:, :],
                                      func=AF.Gelu, bias=zerob[:, :], scale=1.0
                                      ).then_inc(SEM["e_g"], 1)

        if debug:
            @block.gpsimd
            def _(gpsimd):
                gpsimd.wait_ge(SEM["e_f"], 128)
                for tc in range(TC):
                    gpsimd.dma_start(out=sdbg_d[tc, :, :],
                                     in_=static[:, tc * 1024:(tc + 1) * 1024]).then_inc(SEM["dbg"], 16)

        if p4stage < 5:
            raise StopPhase()
        @block.sync
        def _(sync):
            sync.wait_ge(SEM["e_cast"], 32)
            for uc in range(4):
                if uc >= 2:
                    sync.wait_ge(SEM["f_sq"], uc - 1)
                sync.dma_start(out=wstage[uc % 2], in_=v_d[_ts(uc), :]
                               ).then_inc(SEM["fv_a" if uc % 2 == 0 else "fv_b"], 16)
            for uc in range(4):
                sync.wait_ge(SEM["f_sq"], min(uc + 2, 5))
                if uc >= 2:
                    sync.wait_ge(SEM["f_vpp"], uc - 1)
                sync.dma_start(out=wstage[uc % 2], in_=v_d[_ts(uc), :]
                               ).then_inc(SEM["fv_a" if uc % 2 == 0 else "fv_b"], 16)
            sync.wait_ge(SEM["f_vpp"], 4)
            for tc in range(TC):
                if tc >= 2:
                    sync.wait_ge(SEM["f_ve"], 3 * (tc - 2) + 3)
                sync.dma_start(out=xres[tc % 2], in_=x2buf_d[tc, :, :]
                               ).then_inc(SEM["fx_a" if tc % 2 == 0 else "fx_b"], 16)

        @block.scalar
        def _(scalar):
            for uc in range(4):
                scalar.wait_ge(SEM["fv_a" if uc % 2 == 0 else "fv_b"], 16 * (uc // 2 + 1))
                scalar.activation(out=junk[:, :], in_=wstage[uc % 2], func=AF.Square,
                                  accum_out=sv2[:, uc:uc + 1]).then_inc(SEM["f_sq"], 1)
            scalar.activation(out=sv_t[:, :], in_=sv2[:, :], func=AF.Ln, bias=eps12[:, :], scale=1.0)
            scalar.drain()
            scalar.activation(out=sv_t[:, :], in_=sv_t[:, :], func=AF.Exp, bias=zerob[:, :],
                              scale=-0.5).then_inc(SEM["f_sq"], 1)
            for tc in range(TC):
                scalar.wait_ge(SEM["f_ve"], 3 * tc + 1)
                scalar.activation(out=junk[:, :], in_=fsb, func=AF.Square,
                                  accum_out=sumsq[:, :]).then_inc(SEM["f_sc"], 1)
                scalar.wait_ge(SEM["f_ve"], 3 * tc + 2)
                scalar.activation(out=rstd[:, :], in_=var[:, :], func=AF.Ln, bias=eps5[:, :], scale=1.0)
                scalar.drain()
                scalar.activation(out=rstd[:, :], in_=rstd[:, :], func=AF.Exp, bias=zerob[:, :],
                                  scale=-0.5).then_inc(SEM["f_sc"], 1)

        @block.vector
        def _(vector):
            vector.wait_ge(SEM["f_sq"], 5)
            vector.wait_ge(SEM["e_f"], 128)
            for uc in range(4):
                vector.wait_ge(SEM["fv_a" if uc % 2 == 0 else "fv_b"], 16 * ((4 + uc) // 2 + 1))
                vector.scalar_tensor_tensor(vpp[:, uc * 1024:(uc + 1) * 1024],
                                            wstage[uc % 2], sv_t[:, uc:uc + 1], gbc[:, :],
                                            op0=ALU.mult, op1=ALU.mult).then_inc(SEM["f_vpp"], 1)
            for tc in range(TC):
                ob = OA if tc % 2 == 0 else OB
                vector.wait_ge(SEM["f_dyn"], tc + 1)
                for dh in range(2):
                    ins = vector.tensor_tensor(
                        fsb[:, dh * 512:dh * 512 + 512],
                        static[:, tc * 1024 + dh * 512: tc * 1024 + dh * 512 + 512],
                        ob[dh][:, :], op=ALU.add)
                    if dh == 1:
                        ins.then_inc(SEM["f_ve"], 1)
                vector.drain()
                vector.reduce_sum(sumx[:, :], fsb, axis=AX)
                vector.drain()
                vector.wait_ge(SEM["f_sc"], 2 * tc + 1)
                stats_seq(vector).then_inc(SEM["f_ve"], 1)
                vector.wait_ge(SEM["f_sc"], 2 * tc + 2)
                vector.tensor_scalar(fsb, fsb, mean[:, :], rstd[:, :],
                                     op0=ALU.subtract, op1=ALU.mult)
                vector.drain()
                vector.wait_ge(SEM["fx_a" if tc % 2 == 0 else "fx_b"], 16 * (tc // 2 + 1))
                if tc >= 2:
                    vector.wait_ge(SEM["fs_a" if tc % 2 == 0 else "fs_b"], 16 * (tc // 2))
                vector.tensor_tensor(ysb[tc % 2], fsb, xres[tc % 2], op=ALU.add
                                     ).then_inc(SEM["f_ve"], 1)

        @block.tensor
        def _(tensor):
            for tc in range(TC):
                ob = OA if tc % 2 == 0 else OB
                if tc == 0:
                    tensor.wait_ge(SEM["f_vpp"], 4)
                if tc >= 2:
                    tensor.wait_ge(SEM["f_ve"], 3 * (tc - 2) + 1)
                for uc in range(4):
                    for dh in range(2):
                        ins = tensor.matmul(
                            ob[dh][:, :], wh_r[:, uc * 2048 + tc * 128: uc * 2048 + (tc + 1) * 128],
                            vpp[:, uc * 1024 + dh * 512: uc * 1024 + dh * 512 + 512],
                            start=(uc == 0), stop=(uc == 3), **MK)
                        if uc == 3 and dh == 1:
                            ins.then_inc(SEM["f_dyn"], 1)

        @block.gpsimd
        def _(gpsimd):
            for tc in range(TC):
                gpsimd.wait_ge(SEM["f_ve"], 3 * tc + 3)
                gpsimd.dma_start(out=y_d[_ts(tc), :], in_=ysb[tc % 2]
                                 ).then_inc(SEM["fs_a" if tc % 2 == 0 else "fs_b"], 16)
            gpsimd.wait_ge(SEM["fs_a"], 16 * TC // 2)
            gpsimd.wait_ge(SEM["fs_b"], 16 * TC // 2)

        p4block.__exit__(None, None, None)
    except StopPhase:
        p4block.__exit__(None, None, None)
    return nc


SCALE_EXP = 1.0 / 32.0


@functools.lru_cache(maxsize=2)
def _get_nc(debug=False):
    return build_nc(debug=debug)


def build_upto(n, debug=True):
    return build_nc(debug=debug, upto=n)


def _prep_inputs(inputs, core):
    b = core // 4
    j0 = (core % 4) * JS
    x = np.asarray(inputs["L0_attn_x"], dtype=np.float32)
    c = np.ascontiguousarray
    return {
        "x": c(x[b]),
        "xj": c(x[b, j0:j0 + JS]),
        "wq": c(np.asarray(inputs["L0_attn_W_query"], np.float32)),
        "wk": c(np.asarray(inputs["L0_attn_W_key"], np.float32)),
        "wv": c(np.asarray(inputs["L0_attn_W_value"], np.float32)),
        "w1t": c(np.asarray(inputs["L0_ffn_W1"], np.float32).T),
        "w2t": c(np.asarray(inputs["L0_ffn_W2"], np.float32).T),
        "rwt": c(np.asarray(inputs["L0_ffn_router_W"], np.float32).T),
        "ut": c(np.asarray(inputs["L0_ffn_raw_U"], np.float32).T),
        "u": c(np.asarray(inputs["L0_ffn_raw_U"], np.float32)),
        "v": c(np.asarray(inputs["L0_ffn_raw_V"], np.float32)),
        "lng": c(np.asarray(inputs["L0_ffn_ln_g"], np.float32).reshape(DC, 128, 1)),
        "lnb": c(np.asarray(inputs["L0_ffn_ln_b"], np.float32).reshape(DC, 128, 1)),
        "rb": c(np.asarray(inputs["L0_ffn_router_b"], np.float32).reshape(1, NB)),
        "gamma": c(np.asarray(inputs["L0_ffn_gamma"], np.float32).reshape(1, D)),
        "ident": np.eye(128, dtype=np.float32),
    }


def kernel(**inputs):
    nc = _get_nc(False)
    in_maps = [_prep_inputs(inputs, c) for c in range(8)]
    res = run_bass_kernel_spmd(nc, in_maps, core_ids=list(range(8)))
    out = np.empty((2, SEQ, D), dtype=np.float32)
    for c in range(8):
        b = c // 4
        j0 = (c % 4) * JS
        out[b, j0:j0 + JS] = res.results[c]["y"]
    return out


# version 11
# speedup vs baseline: 1.2461x; 1.0228x over previous
import functools
import numpy as np

import concourse.bass as bass
import concourse.mybir as mybir
from concourse.bass_utils import run_bass_kernel_spmd

F32 = mybir.dt.float32
BF16 = mybir.dt.bfloat16
AF = mybir.ActivationFunctionType
ALU = mybir.AluOpType
AX = mybir.AxisListType.X

SEQ, D, HID, NB = 8192, 1024, 4096, 512
JS = 2048
NSUP, SUP = 8, 1024
B_SHIFT = 30.0
DC = D // 128
TC = JS // 128
XC = SEQ // 128
MK = dict(skip_group_check=True)


class StopPhase(Exception):
    pass


def _ts(i, n=128):
    return slice(i * n, (i + 1) * n)


def build_nc(debug=False, upto=5, p4stage=9):
    nc = bass.Bass("TRN2", target_bir_lowering=False)

    x_d = nc.dram_tensor("x", [SEQ, D], F32, kind="ExternalInput").ap()
    xj_d = nc.dram_tensor("xj", [JS, D], F32, kind="ExternalInput").ap()
    wq_d = nc.dram_tensor("wq", [D, D], F32, kind="ExternalInput").ap()
    wk_d = nc.dram_tensor("wk", [D, D], F32, kind="ExternalInput").ap()
    wv_d = nc.dram_tensor("wv", [D, D], F32, kind="ExternalInput").ap()
    w1t_d = nc.dram_tensor("w1t", [D, HID], F32, kind="ExternalInput").ap()
    w2t_d = nc.dram_tensor("w2t", [HID, D], F32, kind="ExternalInput").ap()
    rwt_d = nc.dram_tensor("rwt", [D, NB], F32, kind="ExternalInput").ap()
    ut_d = nc.dram_tensor("ut", [D, NB], F32, kind="ExternalInput").ap()
    u_d = nc.dram_tensor("u", [NB, D], F32, kind="ExternalInput").ap()
    v_d = nc.dram_tensor("v", [NB, D], F32, kind="ExternalInput").ap()
    lng_d = nc.dram_tensor("lng", [DC, 128, 1], F32, kind="ExternalInput").ap()
    lnb_d = nc.dram_tensor("lnb", [DC, 128, 1], F32, kind="ExternalInput").ap()
    rb_d = nc.dram_tensor("rb", [1, NB], F32, kind="ExternalInput").ap()
    gam_d = nc.dram_tensor("gamma", [1, D], F32, kind="ExternalInput").ap()
    id_d = nc.dram_tensor("ident", [128, 128], F32, kind="ExternalInput").ap()
    y_d = nc.dram_tensor("y", [JS, D], F32, kind="ExternalOutput").ap()

    ikind = "ExternalOutput" if debug else "Internal"
    qbuf_d = nc.dram_tensor("qbuf", [NSUP, DC, 128, SUP], BF16, kind=ikind).ap()
    vbuf_d = nc.dram_tensor("vbuf", [XC, 128, D], BF16, kind=ikind).ap()
    x2buf_d = nc.dram_tensor("x2buf", [TC, 128, D], F32, kind=ikind).ap()
    if debug:
        odbg_d = nc.dram_tensor("odbg", [TC, 128, D], F32, kind="ExternalOutput").ap()
        ldbg_d = nc.dram_tensor("ldbg", [128, TC], F32, kind="ExternalOutput").ap()
        kdbg_d = nc.dram_tensor("kdbg", [DC, 128, JS], BF16, kind="ExternalOutput").ap()
        adbg_d = nc.dram_tensor("adbg", [TC, 128, NB], F32, kind="ExternalOutput").ap()
        wdbg_d = nc.dram_tensor("wdbg", [4, 128, JS], BF16, kind="ExternalOutput").ap()
        sdbg_d = nc.dram_tensor("sdbg", [TC, 128, D], F32, kind="ExternalOutput").ap()
        xsdbg_d = nc.dram_tensor("xsdbg", [128, 8192], BF16, kind="ExternalOutput").ap()
        stdbg_d = nc.dram_tensor("stdbg", [128, 8], F32, kind="ExternalOutput").ap()
        xfdbg_d = nc.dram_tensor("xfdbg", [128, 16384], BF16, kind="ExternalOutput").ap()
        zfdbg_d = nc.dram_tensor("zfdbg", [128, 16384], BF16, kind="ExternalOutput").ap()
        wqdbg_d = nc.dram_tensor("wqdbg", [128, 8192], BF16, kind="ExternalOutput").ap()

    BB = nc.alloc_sbuf_tensor("BB", [128, 49152], BF16).ap()
    FF = nc.alloc_sbuf_tensor("FF", [128, 16384], F32).ap()
    stg = nc.alloc_sbuf_tensor("stg", [128, 6144], F32).ap()
    junk = nc.alloc_sbuf_tensor("junkb", [128, 1024], BF16).ap()

    A = nc.alloc_sbuf_tensor
    ident_b = A("ident_b", [128, 128], BF16).ap()
    ones_b = A("ones_b", [128, 1], BF16).ap()
    ones1f = A("ones1f", [1, 128], F32).ap()
    negB = A("negB", [128, 1], F32).ap()
    zerob = A("zerob", [128, 1], F32).ap()
    eps5 = A("eps5", [128, 1], F32).ap()
    onesf = A("onesf", [128, 1], F32).ap()
    eps12 = A("eps12", [128, 1], F32).ap()
    g_t = A("g_t", [128, DC], F32).ap()
    b_t = A("b_t", [128, DC], F32).ap()
    su2 = A("su2", [128, 4], F32).ap()
    su_t = A("su_t", [128, 4], F32).ap()
    sv2 = A("sv2", [128, 4], F32).ap()
    sv_t = A("sv_t", [128, 4], F32).ap()
    SSt = A("SSt", [128, 16], F32).ap()
    SS2 = A("SS2", [128, 16], F32).ap()
    fac = A("fac", [128, 16], F32).ap()
    l_sb = A("l_sb", [128, 16], F32).ap()
    top8 = A("top8", [128, 8], F32).ap()
    rb_row = A("rb_row", [1, NB], F32).ap()
    brow = A("brow", [1, NB], F32).ap()
    grow = A("grow", [1, D], F32).ap()
    sumx = A("sumx", [128, 1], F32).ap()
    sumsq = A("sumsq", [128, 1], F32).ap()
    mean = A("mean", [128, 1], F32).ap()
    var = A("var", [128, 1], F32).ap()
    rstd = A("rstd", [128, 1], F32).ap()
    msq = A("msq", [128, 1], F32).ap()
    lt = A("lt", [128, 1], F32).ap()
    lt1 = A("lt1", [128, 1], F32).ap()
    vtmp = A("vtmp", [128, 1], F32).ap()
    linv = A("linv", [128, 16], F32).ap()
    stdbg = A("stdbg_s", [128, 8], F32).ap()

    wqb = BB[:, 0:8192]
    wkb = BB[:, 8192:16384]
    wvb = BB[:, 16384:24576]
    xsup = [BB[:, 24576:32768], BB[:, 32768:40960]]
    xln = [BB[:, 40960:41984], BB[:, 41984:43008]]
    xln4 = [BB[:, 40960:41984], BB[:, 41984:43008],
            BB[:, 43008:44032], BB[:, 44032:45056]]
    qsb = [BB[:, 45056:46080], BB[:, 46080:47104]]
    xjf = BB[:, 24576:40960]
    kq = [BB[:, 16384:24576], BB[:, 40960:49152]]

    def kfm(e):
        return kq[e // 4][:, (e % 4) * 2048:(e % 4) * 2048 + 2048]

    qstream = BB[:, 0:8192]
    vstream = BB[:, 8192:16384]
    ppair = [BB[:, 24576:25088], BB[:, 25088:25600]]
    xf_fm = BB[:, 0:16384]
    z_fm = BB[:, 16384:32768]
    xf_tm = BB[:, 40960:41984]
    z_tm = BB[:, 41984:43008]
    rw_b = BB[:, 43008:47104]
    w_tm = BB[:, 32768:40960]
    w_fmr = BB[:, 40960:49152]
    wh_r = BB[:, 16384:24576]
    ut_b = [BB[:, 24576:25600], BB[:, 25600:26624]]
    w1sb2 = [BB[:, 24576:28672], BB[:, 28672:32768]]
    w2sb2 = [BB[:, 32768:36864], BB[:, 36864:40960]]
    h1sb = BB[:, 43008:47104]
    vpp = BB[:, 0:4096]

    O_sb = FF[:, 0:16384]
    wmask = FF[:, 0:8192]
    mge = FF[:, 8192:8704]
    static = FF[:, 0:16384]

    xstg = [stg[:, 0:1024], stg[:, 1024:2048]]
    xstg4 = [stg[:, 0:1024], stg[:, 1024:2048], stg[:, 2048:3072], stg[:, 3072:4096]]
    XSEM = ("x_a", "x_b", "x_c", "x_d")
    xres = [stg[:, 0:1024], stg[:, 1024:2048]]
    x2sb = [stg[:, 2048:3072], stg[:, 3072:4096]]
    xf32 = stg[:, 4096:5120]
    gbc = stg[:, 0:1024]
    wstage = [stg[:, 1024:2048], stg[:, 2048:3072]]
    bbc = stg[:, 3072:3584]
    clipb = [stg[:, 3584:4096], stg[:, 4096:4608]]
    alphab = [stg[:, 4608:5120], stg[:, 5120:5632]]
    utstg = [stg[:, 3584:3712], stg[:, 3712:3840]]
    wstgE = [stg[:, 1024:3072], stg[:, 3072:5120]]
    fsb = stg[:, 4608:5632]
    fsbB = BB.bitcast(F32)[:, 2048:3072]
    ysb = [stg[:, 2048:3072], stg[:, 3072:4096]]

    P = nc.alloc_psum_tensor
    p0 = P("p0", [128, 512], F32).ap()
    p1 = P("p1", [128, 512], F32).ap()
    p2 = P("p2", [128, 512], F32).ap()
    p3 = P("p3", [128, 512], F32).ap()
    p4 = P("p4", [128, 512], F32).ap()
    p5 = P("p5", [128, 512], F32).ap()
    p6 = P("p6", [128, 512], F32).ap()
    p_t = P("p_t", [128, 1024], BF16).ap()
    p_t2 = p4.bitcast(BF16)

    def ptb(gi):
        return p_t if gi % 2 == 0 else p_t2

    def stats_seq(vector):
        vector.tensor_scalar(mean[:, :], sumx[:, :], 1.0 / D, None, op0=ALU.mult)
        vector.tensor_scalar(vtmp[:, :], sumsq[:, :], 1.0 / D, None, op0=ALU.mult)
        vector.tensor_tensor(msq[:, :], mean[:, :], mean[:, :], op=ALU.mult)
        vector.drain()
        return vector.tensor_tensor(var[:, :], vtmp[:, :], msq[:, :], op=ALU.subtract)

    sem_names = [
        "s0a", "s0b", "s0i", "x_a", "x_b", "x_c", "x_d", "kx_a", "kx_b", "sp_a", "sp_b",
        "qv_a", "qv_b", "x3_a", "x3_b", "sp3_a", "sp3_b", "ra_a", "ra_b",
        "du_a", "du_b", "dut_a", "dut_b", "ee_a", "ee_b", "fv_a", "fv_b",
        "fx_a", "fx_b", "fs_a", "fs_b",
        "s0_ld", "s0_v",
        "s1_x", "s1_sq", "s1_var", "s1_rstd", "s1_app", "s1_tr", "s1_ev",
        "s1_pmm", "s1_pev", "s1_sp",
        "k_x", "k_sq", "k_var", "k_rstd", "k_app", "k_tr", "k_ev", "k_mm", "k_kev",
        "s2_qv", "s2_smm", "s2_sp", "s2_pv", "s2_fold",
        "s3_x", "s3_s", "s3_v", "s3_tr", "s3_sp",
        "a_ld", "a_te", "a_ve",
        "r_mm", "r_ev", "r_sp", "r_tk",
        "c_w", "c_wt", "c_wf",
        "d_ld", "d_sq", "d_utc", "d_h", "d_wh",
        "e_ld", "e_cast", "e_m1", "e_g", "e_m2", "e_f",
        "f_ld", "f_sq", "f_vpp", "f_dyn", "f_ve", "f_sc", "f_x2", "f_st",
        "dbg",
    ]
    SEM = {n: nc.alloc_semaphore(name=n) for n in sem_names}
    lo = min(s.num for s in SEM.values())
    hi = max(s.num for s in SEM.values())
    nc.gpsimd.sem_clear(range(lo, hi + 1))
    nc.all_engine_barrier()

    with nc.Block("p0") as block:
        @block.sync
        def _(sync):
            sync.dma_start(out=stg[:, 5632:5760], in_=id_d[:, :]).then_inc(SEM["s0i"], 16)
            for wi, wd in enumerate((wq_d, wk_d, wv_d)):
                for c in range(DC):
                    gi = wi * DC + c
                    if gi >= 2:
                        sync.wait_ge(SEM["s0_v"], gi - 1)
                    sync.dma_start(out=xstg[gi % 2], in_=wd[_ts(c), :]
                                   ).then_inc(SEM["s0a" if gi % 2 == 0 else "s0b"], 16)

        @block.scalar
        def _(scalar):
            for fn in (AF.Exp, AF.Ln, AF.Gelu, AF.Tanh, AF.Square):
                scalar.activation(out=junk[0:1, 0:1], in_=junk[0:1, 0:1], func=fn,
                                  bias=zerob[0:1, :], scale=1.0)
                scalar.drain()

        @block.vector
        def _(vector):
            vector.memset(ones_b[:, :], 1.0)
            vector.memset(ones1f[:, :], 1.0)
            vector.memset(negB[:, :], -B_SHIFT)
            vector.memset(zerob[:, :], 0.0)
            vector.memset(eps5[:, :], 1e-5)
            vector.memset(onesf[:, :], 1.0)
            vector.memset(eps12[:, :], 1e-12)
            vector.wait_ge(SEM["s0i"], 16)
            vector.tensor_copy(ident_b[:, :], stg[:, 5632:5760])
            for wi in range(3):
                dst = (wqb, wkb, wvb)[wi]
                for c in range(DC):
                    gi = wi * DC + c
                    vector.wait_ge(SEM["s0a" if gi % 2 == 0 else "s0b"], 16 * (gi // 2 + 1))
                    vector.tensor_copy(dst[:, c * 1024:(c + 1) * 1024],
                                       xstg[gi % 2]).then_inc(SEM["s0_v"], 1)

    def emit_ln_sync(sync, n, src_d, sx, sapp):
        sxa, sxb = sx
        for gi in range(n):
            if gi >= 2:
                sync.wait_ge(SEM[sapp], gi - 1)
            sync.dma_start(out=xstg[gi % 2], in_=src_d[_ts(gi), :]
                           ).then_inc(SEM[sxa if gi % 2 == 0 else sxb], 16)

    def emit_ln_vector(vector, n, sx, ssq, svar, srstd, sapp, str_, sev, dst_of):
        sxa, sxb = sx
        for gi in range(n):
            xc = xstg[gi % 2]
            vector.wait_ge(SEM[sxa if gi % 2 == 0 else sxb], 16 * (gi // 2 + 1))
            vector.reduce_sum(sumx[:, :], xc, axis=AX)
            vector.drain()
            vector.wait_ge(SEM[ssq], gi + 1)
            stats_seq(vector).then_inc(SEM[svar], 1)
            vector.wait_ge(SEM[srstd], gi + 1)
            vector.tensor_scalar(xln[gi % 2], xc, mean[:, :], rstd[:, :],
                                 op0=ALU.subtract, op1=ALU.mult).then_inc(SEM[sapp], 1)
            vector.wait_ge(SEM[str_], gi + 1)
            vector.tensor_copy(dst_of(gi), ptb(gi).rearrange("p (d t) -> p d t", d=DC)
                               ).then_inc(SEM[sev], 1)

    def emit_ln_scalar(scalar, n, sx, ssq, svar, srstd):
        sxa, sxb = sx
        for gi in range(n):
            xc = xstg[gi % 2]
            scalar.wait_ge(SEM[sxa if gi % 2 == 0 else sxb], 16 * (gi // 2 + 1))
            scalar.activation(out=junk[:, :], in_=xc, func=AF.Square,
                              accum_out=sumsq[:, :]).then_inc(SEM[ssq], 1)
            scalar.wait_ge(SEM[svar], gi + 1)
            scalar.activation(out=rstd[:, :], in_=var[:, :], func=AF.Ln, bias=eps5[:, :], scale=1.0)
            scalar.drain()
            scalar.activation(out=rstd[:, :], in_=rstd[:, :], func=AF.Exp, bias=zerob[:, :],
                              scale=-0.5).then_inc(SEM[srstd], 1)

    def emit_tr_chunk(tensor, gi, sapp, sev, str_, src4=False):
        tensor.wait_ge(SEM[sapp], gi + 1)
        if gi >= 2:
            tensor.wait_ge(SEM[sev], gi - 1)
        src = xln4[gi % 4] if src4 else xln[gi % 2]
        for dcx in range(DC):
            ins = tensor.transpose(ptb(gi)[:, dcx * 128:(dcx + 1) * 128],
                                   src[:, dcx * 128:(dcx + 1) * 128],
                                   ident_b[:, :])
            if dcx == DC - 1:
                ins.then_inc(SEM[str_], 1)

    if upto < 1:
        return nc
    with nc.Block("p1") as block:
        @block.sync
        def _(sync):
            for gi in range(XC):
                if gi >= 4:
                    sync.wait_ge(SEM["s1_app"], gi - 3)
                sync.dma_start(out=xstg4[gi % 4], in_=x_d[_ts(gi), :]
                               ).then_inc(SEM[XSEM[gi % 4]], 16)

        @block.scalar
        def _(scalar):
            for gi in range(XC):
                xc = xstg4[gi % 4]
                scalar.wait_ge(SEM[XSEM[gi % 4]], 16 * (gi // 4 + 1))
                scalar.activation(out=junk[:, :], in_=xc, func=AF.Square,
                                  accum_out=sumsq[:, :]).then_inc(SEM["s1_sq"], 1)
                scalar.wait_ge(SEM["s1_var"], gi + 1)
                scalar.activation(out=rstd[:, :], in_=var[:, :], func=AF.Ln, bias=eps5[:, :], scale=1.0)
                scalar.drain()
                scalar.activation(out=rstd[:, :], in_=rstd[:, :], func=AF.Exp, bias=zerob[:, :],
                                  scale=-0.5).then_inc(SEM["s1_rstd"], 1)

        @block.tensor
        def _(tensor):
            pg = 0
            for c in range(8):
                emit_tr_chunk(tensor, c, "s1_app", "s1_ev", "s1_tr", src4=True)
            for s in range(NSUP):
                def maybe_trans(k, s_=s):
                    if s_ + 1 < NSUP and k % 2 == 1:
                        c2 = k // 2
                        emit_tr_chunk(tensor, (s_ + 1) * 8 + c2, "s1_app", "s1_ev", "s1_tr",
                                      src4=True)
                tensor.wait_ge(SEM["s1_ev"], (s + 1) * 8)
                for e in range(DC):
                    pp = (p0, p1) if pg % 2 == 0 else (p2, p3)
                    if pg >= 2:
                        tensor.wait_ge(SEM["s1_pev"], pg - 1)
                    for dcx in range(DC):
                        for th in range(2):
                            ins = tensor.matmul(
                                pp[th][:, :],
                                wqb[:, dcx * 1024 + e * 128: dcx * 1024 + (e + 1) * 128],
                                xsup[s % 2][:, dcx * 1024 + th * 512: dcx * 1024 + th * 512 + 512],
                                start=(dcx == 0), stop=(dcx == DC - 1), **MK)
                            if dcx == DC - 1 and th == 1:
                                ins.then_inc(SEM["s1_pmm"], 1)
                    maybe_trans(pg % 16)
                    pg += 1
                for c in range(8):
                    pp = (p0, p1) if pg % 2 == 0 else (p2, p3)
                    if pg >= 2:
                        tensor.wait_ge(SEM["s1_pev"], pg - 1)
                    for dcx in range(DC):
                        for eh in range(2):
                            ins = tensor.matmul(
                                pp[eh][:, :],
                                xsup[s % 2][:, dcx * 1024 + c * 128: dcx * 1024 + (c + 1) * 128],
                                wvb[:, dcx * 1024 + eh * 512: dcx * 1024 + eh * 512 + 512],
                                start=(dcx == 0), stop=(dcx == DC - 1), **MK)
                            if dcx == DC - 1 and eh == 1:
                                ins.then_inc(SEM["s1_pmm"], 1)
                    maybe_trans(pg % 16)
                    pg += 1

        @block.vector
        def _(vector):
            def ln_sa(s, c):
                gi = s * 8 + c
                xc = xstg4[gi % 4]
                vector.wait_ge(SEM[XSEM[gi % 4]], 16 * (gi // 4 + 1))
                vector.reduce_sum(sumx[:, :], xc, axis=AX)
                vector.drain()
                vector.wait_ge(SEM["s1_sq"], gi + 1)
                stats_seq(vector).then_inc(SEM["s1_var"], 1)
                vector.wait_ge(SEM["s1_rstd"], gi + 1)
                if gi >= 4:
                    vector.wait_ge(SEM["s1_tr"], gi - 3)
                vector.tensor_scalar(xln4[gi % 4], xc, mean[:, :], rstd[:, :],
                                     op0=ALU.subtract, op1=ALU.mult).then_inc(SEM["s1_app"], 1)

            def ln_ev(s, c):
                gi = s * 8 + c
                vector.wait_ge(SEM["s1_tr"], gi + 1)
                if s >= 2 and c == 0:
                    vector.wait_ge(SEM["s1_pmm"], 16 * (s - 1))
                dst = xsup[s % 2].rearrange("p (d t) -> p d t", d=DC)[:, :, c * 128:(c + 1) * 128]
                vector.tensor_copy(dst, ptb(gi).rearrange("p (d t) -> p d t", d=DC)
                                   ).then_inc(SEM["s1_ev"], 1)

            for c in range(8):
                ln_sa(0, c)
                ln_ev(0, c)
            for s in range(NSUP):
                if s + 1 < NSUP:
                    ln_sa(s + 1, 0)
                    ln_sa(s + 1, 1)
                for k in range(16):
                    pg = s * 16 + k
                    pp = (p0, p1) if pg % 2 == 0 else (p2, p3)
                    vector.wait_ge(SEM["s1_pmm"], pg + 1)
                    if pg >= 2:
                        vector.wait_ge(SEM["sp_a" if pg % 2 == 0 else "sp_b"], 16 * (pg // 2))
                    vector.tensor_copy(qsb[pg % 2][:, 0:512], pp[0][:, :])
                    vector.tensor_copy(qsb[pg % 2][:, 512:1024], pp[1][:, :]
                                       ).then_inc(SEM["s1_pev"], 1)
                    if s + 1 < NSUP and k % 2 == 1:
                        c2 = k // 2
                        ln_ev(s + 1, c2)
                        if c2 + 2 <= 7:
                            ln_sa(s + 1, c2 + 2)

        @block.gpsimd
        def _(gpsimd):
            if debug:
                gpsimd.wait_ge(SEM["s1_ev"], 8)
                gpsimd.dma_start(out=xsdbg_d[:, :], in_=xsup[0]).then_inc(SEM["dbg"], 16)
                gpsimd.dma_start(out=stdbg_d[:, :], in_=stdbg[:, :]).then_inc(SEM["dbg"], 16)
                gpsimd.dma_start(out=wqdbg_d[:, :], in_=wqb).then_inc(SEM["dbg"], 16)
            for s in range(NSUP):
                for e in range(DC):
                    pg = s * 16 + e
                    gpsimd.wait_ge(SEM["s1_pev"], pg + 1)
                    gpsimd.dma_start(out=qbuf_d[s, e, :, :], in_=qsb[pg % 2]
                                     ).then_inc(SEM["sp_a" if pg % 2 == 0 else "sp_b"], 16)
                for c in range(8):
                    pg = s * 16 + 8 + c
                    gpsimd.wait_ge(SEM["s1_pev"], pg + 1)
                    gpsimd.dma_start(out=vbuf_d[s * 8 + c, :, :], in_=qsb[pg % 2]
                                     ).then_inc(SEM["sp_a" if pg % 2 == 0 else "sp_b"], 16)
            gpsimd.wait_ge(SEM["sp_a"], 16 * NSUP * 8)
            gpsimd.wait_ge(SEM["sp_b"], 16 * NSUP * 8)

    if upto < 2:
        return nc
    with nc.Block("p1k") as block:
        @block.sync
        def _(sync):
            emit_ln_sync(sync, TC, xj_d, ("kx_a", "kx_b"), "k_app")

        @block.scalar
        def _(scalar):
            emit_ln_scalar(scalar, TC, ("kx_a", "kx_b"), "k_sq", "k_var", "k_rstd")

        @block.tensor
        def _(tensor):
            for gi in range(TC):
                emit_tr_chunk(tensor, gi, "k_app", "k_ev", "k_tr")
            kg = 0
            for tb in range(4):
                tensor.wait_ge(SEM["k_ev"], 4 * (tb + 1))
                for e in range(DC):
                    pp = p0 if kg % 2 == 0 else p1
                    if kg >= 2:
                        tensor.wait_ge(SEM["k_kev"], kg - 1)
                    for dcx in range(DC):
                        ins = tensor.matmul(
                            pp[:, :],
                            wkb[:, dcx * 1024 + e * 128: dcx * 1024 + (e + 1) * 128],
                            xjf[:, dcx * 2048 + tb * 512: dcx * 2048 + tb * 512 + 512],
                            start=(dcx == 0), stop=(dcx == DC - 1), **MK)
                        if dcx == DC - 1:
                            ins.then_inc(SEM["k_mm"], 1)
                    kg += 1

        @block.vector
        def _(vector):
            emit_ln_vector(vector, TC, ("kx_a", "kx_b"), "k_sq", "k_var", "k_rstd", "k_app", "k_tr", "k_ev",
                           lambda gi: xjf.rearrange("p (d t) -> p d t", d=DC)[:, :, gi * 128:(gi + 1) * 128])
            for kg in range(DC * 4):
                tb, e = kg // 8, kg % 8
                pp = p0 if kg % 2 == 0 else p1
                vector.wait_ge(SEM["k_mm"], kg + 1)
                vector.tensor_copy(kfm(e)[:, tb * 512:tb * 512 + 512], pp[:, :]
                                   ).then_inc(SEM["k_kev"], 1)

        if debug:
            @block.gpsimd
            def _(gpsimd):
                gpsimd.wait_ge(SEM["k_kev"], DC * 4)
                for e in range(DC):
                    gpsimd.dma_start(out=kdbg_d[e, :, :], in_=kfm(e)).then_inc(SEM["dbg"], 16)
                gpsimd.wait_ge(SEM["dbg"], 16 * DC)

    if upto < 3:
        return nc
    OA = (p0, p1)
    OB = (p2, p3)
    SBK = (p4, p5)

    with nc.Block("p2") as block:
        @block.sync
        def _(sync):
            for s in range(16):
                if s >= 2:
                    sync.wait_ge(SEM["s2_pv"], 16 * (s - 1))
                qdst = qstream[:, (s % 2) * 4096:(s % 2) * 4096 + 4096]
                vdst = vstream[:, (s % 2) * 4096:(s % 2) * 4096 + 4096]
                qs = "qv_a" if s % 2 == 0 else "qv_b"
                sync.dma_start(out=qdst.rearrange("p (e t) -> p e t", e=DC),
                               in_=qbuf_d[s // 2, :, :, (s % 2) * 512:(s % 2) * 512 + 512]
                               .transpose([1, 0, 2])).then_inc(SEM[qs], 16)
                sync.dma_start(out=vdst.rearrange("p (c e) -> p c e", c=4),
                               in_=vbuf_d[s * 4:(s + 1) * 4].transpose([1, 0, 2])
                               ).then_inc(SEM[qs], 16)

        @block.tensor
        def _(tensor):
            def emit_S(g):
                s = g // 16
                jb = (g // 2) % 8
                t = g % 2
                sb = (s % 2) * 4096
                if t == 0 and jb == 0:
                    tensor.wait_ge(SEM["qv_a" if s % 2 == 0 else "qv_b"], 32 * (s // 2 + 1))
                if g >= 2:
                    tensor.wait_ge(SEM["s2_sp"], g - 1)
                for icl in range(2):
                    ic = t * 2 + icl
                    for e in range(DC):
                        ins = tensor.matmul(
                            SBK[g % 2][:, icl * 256:icl * 256 + 256],
                            qstream[:, sb + e * 512 + ic * 128: sb + e * 512 + (ic + 1) * 128],
                            kfm(e)[:, jb * 256: jb * 256 + 256],
                            start=(e == 0), stop=(e == DC - 1), **MK)
                        if icl == 1 and e == DC - 1:
                            ins.then_inc(SEM["s2_smm"], 1)

            def emit_PV(g):
                G = g // 2
                s = g // 16
                t = g % 2
                sb = (s % 2) * 4096
                tensor.wait_ge(SEM["s2_sp"], g + 1)
                if t == 0:
                    tensor.wait_ge(SEM["s2_fold"], G)
                par = G % 2
                for icl in range(2):
                    ic = t * 2 + icl
                    for jsub in range(2):
                        lhsT = ppair[g % 2][:, icl * 256 + jsub * 128: icl * 256 + (jsub + 1) * 128]
                        ob = OA if jsub == 0 else OB
                        for eh in range(2):
                            tensor.matmul(ob[eh][:, :], lhsT,
                                          vstream[:, sb + ic * 1024 + eh * 512: sb + ic * 1024 + eh * 512 + 512],
                                          start=(ic == 0), stop=(ic == 3), **MK)
                        slot = par * 16 + jsub * 8 + ic
                        ins = tensor.matmul(p6[:, slot:slot + 1], lhsT, ones_b[:, :],
                                            start=True, stop=True, **MK)
                        if icl == 1 and jsub == 1:
                            ins.then_inc(SEM["s2_pv"], 1)

            for s in range(16):
                base = s * 16
                emit_S(base)
                for t in range(15):
                    emit_S(base + t + 1)
                    emit_PV(base + t)
                emit_PV(base + 15)

        @block.scalar
        def _(scalar):
            for g in range(256):
                scalar.wait_ge(SEM["s2_smm"], g + 1)
                if g >= 2:
                    scalar.wait_ge(SEM["s2_pv"], g - 1)
                scalar.activation(out=ppair[g % 2][:, :], in_=SBK[g % 2][:, :], func=AF.Exp,
                                  bias=negB[:, :], scale=SCALE_EXP).then_inc(SEM["s2_sp"], 1)

        @block.vector
        def _(vector):
            for G in range(16 * 8):
                s, jb = G // 8, G % 8
                par = G % 2
                vector.wait_ge(SEM["s2_pv"], 2 * (G + 1))
                for jsub in range(2):
                    jc = jb * 2 + jsub
                    ob = OA if jsub == 0 else OB
                    ltx = lt if jsub == 0 else lt1
                    sl = par * 16 + jsub * 8
                    vector.reduce_sum(ltx[:, :], p6[:, sl:sl + 4], axis=AX)
                    for eh in range(2):
                        dst = O_sb[:, jc * 1024 + eh * 512: jc * 1024 + eh * 512 + 512]
                        if s == 0:
                            vector.tensor_copy(dst, ob[eh][:, :])
                        else:
                            vector.tensor_tensor(dst, dst, ob[eh][:, :], op=ALU.add)
                    if s == 0:
                        ins = vector.tensor_copy(l_sb[:, jc:jc + 1], ltx[:, :])
                    else:
                        ins = vector.tensor_tensor(l_sb[:, jc:jc + 1], l_sb[:, jc:jc + 1],
                                                   ltx[:, :], op=ALU.add)
                    if jsub == 1:
                        ins.then_inc(SEM["s2_fold"], 1)

    if upto < 4:
        return nc
    with nc.Block("p3") as block:
        @block.sync
        def _(sync):
            for tc in range(TC):
                if tc >= 2:
                    sync.wait_ge(SEM["s3_v"], 9 * (tc - 2) + 3)
                sync.dma_start(out=xres[tc % 2], in_=xj_d[_ts(tc), :]
                               ).then_inc(SEM["x3_a" if tc % 2 == 0 else "x3_b"], 16)

        @block.vector
        def _(vector):
            vector.reciprocal(linv[:, :], l_sb[:, :])
            vector.drain()
            for tc in range(TC):
                oc = O_sb[:, tc * 1024:(tc + 1) * 1024]
                vector.tensor_scalar(oc, oc, linv[:, tc:tc + 1], None, op0=ALU.mult
                                     ).then_inc(SEM["s3_v"], 1)
                vector.drain()
                vector.reduce_sum(sumx[:, :], oc, axis=AX)
                vector.drain()
                vector.wait_ge(SEM["s3_s"], 6 * tc + 1)
                stats_seq(vector).then_inc(SEM["s3_v"], 1)
                vector.wait_ge(SEM["s3_s"], 6 * tc + 2)
                if debug:
                    vector.wait_ge(SEM["dbg"], 16 * (DC + 1 + tc + 1))
                vector.tensor_scalar(oc, oc, mean[:, :], rstd[:, :],
                                     op0=ALU.subtract, op1=ALU.mult)
                vector.drain()
                vector.wait_ge(SEM["x3_a" if tc % 2 == 0 else "x3_b"], 16 * (tc // 2 + 1))
                if tc >= 2:
                    vector.wait_ge(SEM["sp3_a" if tc % 2 == 0 else "sp3_b"], 16 * (tc // 2))
                vector.tensor_tensor(x2sb[tc % 2], oc, xres[tc % 2], op=ALU.add
                                     ).then_inc(SEM["s3_v"], 1)
                vector.drain()
                vector.reduce_sum(sumx[:, :], x2sb[tc % 2], axis=AX)
                vector.drain()
                vector.wait_ge(SEM["s3_s"], 6 * tc + 3)
                stats_seq(vector).then_inc(SEM["s3_v"], 1)
                vector.wait_ge(SEM["s3_s"], 6 * tc + 4)
                vector.tensor_scalar(xf32, x2sb[tc % 2], mean[:, :], rstd[:, :],
                                     op0=ALU.subtract, op1=ALU.mult)
                vector.drain()
                vector.tensor_copy(xf_tm, xf32).then_inc(SEM["s3_v"], 1)
                vector.reduce_sum(sumx[:, :], xf32, axis=AX)
                vector.drain()
                vector.wait_ge(SEM["s3_s"], 6 * tc + 5)
                stats_seq(vector).then_inc(SEM["s3_v"], 1)
                vector.wait_ge(SEM["s3_s"], 6 * tc + 6)
                vector.tensor_scalar(z_tm, xf32, mean[:, :], rstd[:, :],
                                     op0=ALU.subtract, op1=ALU.mult).then_inc(SEM["s3_v"], 1)
                vector.wait_ge(SEM["s3_tr"], 2 * tc + 1)
                vector.tensor_copy(
                    xf_fm.rearrange("p (d t) -> p d t", d=DC)[:, :, tc * 128:(tc + 1) * 128],
                    p_t.rearrange("p (d t) -> p d t", d=DC)).then_inc(SEM["s3_v"], 1)
                vector.wait_ge(SEM["s3_tr"], 2 * tc + 2)
                vector.tensor_copy(
                    z_fm.rearrange("p (d t) -> p d t", d=DC)[:, :, tc * 128:(tc + 1) * 128],
                    p_t2.rearrange("p (d t) -> p d t", d=DC)).then_inc(SEM["s3_v"], 1)

        @block.scalar
        def _(scalar):
            for tc in range(TC):
                for j, (srcv, wv_) in enumerate(((O_sb[:, tc * 1024:(tc + 1) * 1024], 9 * tc + 1),
                                                 (x2sb[tc % 2], 9 * tc + 3),
                                                 (xf32, 9 * tc + 5))):
                    scalar.wait_ge(SEM["s3_v"], wv_)
                    scalar.activation(out=junk[:, :], in_=srcv, func=AF.Square,
                                      accum_out=sumsq[:, :]).then_inc(SEM["s3_s"], 1)
                    scalar.wait_ge(SEM["s3_v"], 9 * tc + 2 + 2 * j)
                    scalar.activation(out=rstd[:, :], in_=var[:, :], func=AF.Ln,
                                      bias=eps5[:, :], scale=1.0)
                    scalar.drain()
                    scalar.activation(out=rstd[:, :], in_=rstd[:, :], func=AF.Exp,
                                      bias=zerob[:, :], scale=-0.5).then_inc(SEM["s3_s"], 1)

        @block.tensor
        def _(tensor):
            for tc in range(TC):
                tensor.wait_ge(SEM["s3_v"], 9 * tc + 5)
                if tc > 0:
                    tensor.wait_ge(SEM["s3_v"], 9 * (tc - 1) + 8)
                for dcx in range(DC):
                    ins = tensor.transpose(p_t[:, dcx * 128:(dcx + 1) * 128],
                                           xf_tm[:, dcx * 128:(dcx + 1) * 128], ident_b[:, :])
                    if dcx == DC - 1:
                        ins.then_inc(SEM["s3_tr"], 1)
                tensor.wait_ge(SEM["s3_v"], 9 * tc + 7)
                if tc > 0:
                    tensor.wait_ge(SEM["s3_v"], 9 * (tc - 1) + 9)
                for dcx in range(DC):
                    ins = tensor.transpose(p_t2[:, dcx * 128:(dcx + 1) * 128],
                                           z_tm[:, dcx * 128:(dcx + 1) * 128], ident_b[:, :])
                    if dcx == DC - 1:
                        ins.then_inc(SEM["s3_tr"], 1)

        @block.gpsimd
        def _(gpsimd):
            if debug:
                gpsimd.wait_ge(SEM["s2_fold"], 64)
                gpsimd.dma_start(out=ldbg_d[:, :], in_=l_sb[:, :]).then_inc(SEM["dbg"], 16)
            for tc in range(TC):
                if debug:
                    gpsimd.wait_ge(SEM["s3_v"], 9 * tc + 1)
                    gpsimd.dma_start(out=odbg_d[tc, :, :], in_=O_sb[:, tc * 1024:(tc + 1) * 1024]
                                     ).then_inc(SEM["dbg"], 16)
                gpsimd.wait_ge(SEM["s3_v"], 9 * tc + 3)
                gpsimd.dma_start(out=x2buf_d[tc, :, :], in_=x2sb[tc % 2]
                                 ).then_inc(SEM["sp3_a" if tc % 2 == 0 else "sp3_b"], 16)
            gpsimd.wait_ge(SEM["sp3_a"], 16 * TC // 2)
            gpsimd.wait_ge(SEM["sp3_b"], 16 * TC // 2)

    if upto < 5:
        return nc
    try:
        p4block = nc.Block("p4")
        p4block.__enter__()
        block = p4block
        @block.sync
        def _(sync):
            for c in range(DC):
                sync.dma_start(out=g_t[:, c:c + 1], in_=lng_d[c, :, :]).then_inc(SEM["a_ld"], 16)
                sync.dma_start(out=b_t[:, c:c + 1], in_=lnb_d[c, :, :]).then_inc(SEM["a_ld"], 16)
            sync.dma_start(out=rb_row[:, :], in_=rb_d[:, :]).then_inc(SEM["a_ld"], 16)
            sync.dma_start(out=grow[:, :], in_=gam_d[:, :]).then_inc(SEM["a_ld"], 16)
            for c in range(DC):
                if c >= 2:
                    sync.wait_ge(SEM["a_ve"], c - 1)
                    sync.wait_ge(SEM["a_te"], c)
                sync.dma_start(out=wstage[c % 2][:, 0:512], in_=rwt_d[_ts(c), :]
                               ).then_inc(SEM["ra_a" if c % 2 == 0 else "ra_b"], 16)

        @block.tensor
        def _(tensor):
            tensor.wait_ge(SEM["a_ld"], 16 * 18)
            tensor.matmul(p4[:, :], ones1f[:, :], grow[:, 0:512], start=True, stop=True, **MK)
            tensor.matmul(p5[:, :], ones1f[:, :], grow[:, 512:1024], start=True, stop=True, **MK
                          ).then_inc(SEM["a_te"], 1)
            for c in range(DC):
                tensor.wait_ge(SEM["ra_a" if c % 2 == 0 else "ra_b"], 16 * (c // 2 + 1))
                tensor.matmul(p6[0:1, :], b_t[:, c:c + 1], wstage[c % 2][:, 0:512],
                              start=(c == 0), stop=(c == DC - 1), **MK).then_inc(SEM["a_te"], 1)
            tensor.wait_ge(SEM["a_ve"], DC + 2)
            tensor.matmul(p4[:, :], ones1f[:, :], brow[:, :], start=True, stop=True, **MK
                          ).then_inc(SEM["a_te"], 1)

        @block.vector
        def _(vector):
            vector.wait_ge(SEM["a_te"], 1)
            vector.tensor_copy(gbc[:, 0:512], p4[:, :])
            vector.tensor_copy(gbc[:, 512:1024], p5[:, :])
            for c in range(DC):
                vector.wait_ge(SEM["ra_a" if c % 2 == 0 else "ra_b"], 16 * (c // 2 + 1))
                vector.wait_ge(SEM["a_te"], c + 2)
                vector.tensor_scalar(rw_b[:, c * 512:(c + 1) * 512], wstage[c % 2][:, 0:512],
                                     g_t[:, c:c + 1], None, op0=ALU.mult).then_inc(SEM["a_ve"], 1)
            vector.wait_ge(SEM["a_te"], DC + 1)
            vector.tensor_copy(brow[:, :], p6[0:1, :]).then_inc(SEM["a_ve"], 1)
            vector.drain()
            vector.tensor_tensor(brow[:, :], brow[:, :], rb_row[:, :], op=ALU.add
                                 ).then_inc(SEM["a_ve"], 1)
            vector.wait_ge(SEM["a_te"], DC + 2)
            vector.tensor_copy(bbc[:, :], p4[:, :]).then_inc(SEM["a_ve"], 1)

        if p4stage < 1:
            raise StopPhase()
        if debug:
            @block.gpsimd
            def _(gpsimd):
                gpsimd.dma_start(out=xfdbg_d[:, :], in_=xf_fm).then_inc(SEM["dbg"], 16)
                gpsimd.dma_start(out=zfdbg_d[:, :], in_=z_fm).then_inc(SEM["dbg"], 16)

        @block.tensor
        def _(tensor):
            for tc in range(TC):
                pp = p4 if tc % 2 == 0 else p5
                if tc == 0:
                    tensor.wait_ge(SEM["a_ve"], DC + 3)
                if tc >= 2:
                    tensor.wait_ge(SEM["r_ev"], 2 * (tc - 2) + 1)
                for dcx in range(DC):
                    ins = tensor.matmul(
                        pp[:, :],
                        z_fm[:, dcx * 2048 + tc * 128: dcx * 2048 + (tc + 1) * 128],
                        rw_b[:, dcx * 512:(dcx + 1) * 512],
                        start=(dcx == 0), stop=(dcx == DC - 1), **MK)
                    if dcx == DC - 1:
                        ins.then_inc(SEM["r_mm"], 1)

        @block.vector
        def _(vector):
            def topk(tc):
                vector.wait_ge(SEM["r_sp"], tc + 1)
                vector.max(out=top8[:, :], in_=alphab[tc % 2])
                vector.drain()
                vector.reduce_sum(SSt[:, tc:tc + 1], top8[:, :], axis=AX)
                vector.tensor_scalar(mge[:, :], alphab[tc % 2], top8[:, 7:8], None, op0=ALU.is_ge)
                vector.drain()
                vector.tensor_tensor(wmask[:, tc * 512:(tc + 1) * 512], alphab[tc % 2],
                                     mge[:, :], op=ALU.mult).then_inc(SEM["r_tk"], 1)

            for tc in range(TC):
                pp = p4 if tc % 2 == 0 else p5
                vector.wait_ge(SEM["r_mm"], tc + 1)
                if tc >= 2:
                    vector.wait_ge(SEM["r_sp"], tc - 1)
                vector.tensor_tensor(clipb[tc % 2], pp[:, :], bbc[:, :], op=ALU.add
                                     ).then_inc(SEM["r_ev"], 1)
                vector.drain()
                vector.tensor_scalar(clipb[tc % 2], clipb[tc % 2], -10.0, 10.0,
                                     op0=ALU.max, op1=ALU.min).then_inc(SEM["r_ev"], 1)
                if tc >= 1:
                    topk(tc - 1)
            topk(TC - 1)

        @block.scalar
        def _(scalar):
            for tc in range(TC):
                scalar.wait_ge(SEM["r_ev"], 2 * tc + 2)
                if tc >= 2:
                    scalar.wait_ge(SEM["r_tk"], tc - 1)
                    if debug:
                        scalar.wait_ge(SEM["dbg"], 16 * (DC + 1 + TC) + 16 * (tc - 1))
                scalar.activation(out=alphab[tc % 2], in_=clipb[tc % 2], func=AF.Exp,
                                  bias=zerob[:, :], scale=1.0)
                scalar.drain()
                scalar.activation(out=alphab[tc % 2], in_=alphab[tc % 2], func=AF.Ln,
                                  bias=onesf[:, :], scale=1.0).then_inc(SEM["r_sp"], 1)

        if debug:
            @block.gpsimd
            def _(gpsimd):
                for tc in range(TC):
                    gpsimd.wait_ge(SEM["r_sp"], tc + 1)
                    gpsimd.dma_start(out=adbg_d[tc, :, :], in_=alphab[tc % 2]).then_inc(SEM["dbg"], 16)

        if p4stage < 2:
            raise StopPhase()
        @block.scalar
        def _(scalar):
            scalar.wait_ge(SEM["r_tk"], TC)
            scalar.activation(out=fac[:, :], in_=SSt[:, :], func=AF.Tanh, bias=zerob[:, :],
                              scale=1.0).then_inc(SEM["r_sp"], 1)

        @block.vector
        def _(vector):
            vector.wait_ge(SEM["r_sp"], TC + 1)
            vector.tensor_scalar(SS2[:, :], SSt[:, :], 1e-6, None, op0=ALU.add)
            vector.drain()
            vector.reciprocal(SS2[:, :], SS2[:, :])
            vector.drain()
            vector.tensor_tensor(fac[:, :], fac[:, :], SS2[:, :], op=ALU.mult)
            vector.drain()
            for tc in range(TC):
                vector.tensor_scalar(w_tm[:, tc * 512:(tc + 1) * 512],
                                     wmask[:, tc * 512:(tc + 1) * 512],
                                     fac[:, tc:tc + 1], None, op0=ALU.mult).then_inc(SEM["c_w"], 1)

        @block.tensor
        def _(tensor):
            for tc in range(TC):
                tensor.wait_ge(SEM["c_w"], tc + 1)
                if tc >= 2:
                    tensor.wait_ge(SEM["c_wf"], tc - 1)
                bank = ptb(tc)
                for us in range(4):
                    ins = tensor.transpose(bank[:, us * 128:(us + 1) * 128],
                                           w_tm[:, tc * 512 + us * 128: tc * 512 + (us + 1) * 128],
                                           ident_b[:, :])
                    if us == 3:
                        ins.then_inc(SEM["c_wt"], 1)

        @block.vector
        def _(vector):
            for tc in range(TC):
                vector.wait_ge(SEM["c_wt"], tc + 1)
                vector.tensor_copy(
                    w_fmr.rearrange("p (u t) -> p u t", u=4)[:, :, tc * 128:(tc + 1) * 128],
                    ptb(tc)[:, 0:512].rearrange("p (u t) -> p u t", u=4)
                    ).then_inc(SEM["c_wf"], 1)

        if debug:
            @block.gpsimd
            def _(gpsimd):
                gpsimd.wait_ge(SEM["c_wf"], TC)
                for us in range(4):
                    gpsimd.dma_start(out=wdbg_d[us, :, :],
                                     in_=w_fmr[:, us * 2048:(us + 1) * 2048]).then_inc(SEM["dbg"], 16)

        if p4stage < 3:
            raise StopPhase()
        @block.sync
        def _(sync):
            sync.wait_ge(SEM["a_ve"], DC)
            sync.wait_ge(SEM["a_te"], DC + 1)
            for uc in range(4):
                if uc >= 2:
                    sync.wait_ge(SEM["d_sq"], uc - 1)
                sync.dma_start(out=wstage[uc % 2], in_=u_d[_ts(uc), :]
                               ).then_inc(SEM["du_a" if uc % 2 == 0 else "du_b"], 16)
            sync.wait_ge(SEM["r_sp"], TC)
            for uc in range(4):
                for dcx in range(DC):
                    gi = uc * DC + dcx
                    if gi >= 2:
                        sync.wait_ge(SEM["d_utc"], gi - 1)
                    sync.dma_start(out=utstg[gi % 2], in_=ut_d[_ts(dcx), _ts(uc)]
                                   ).then_inc(SEM["dut_a" if gi % 2 == 0 else "dut_b"], 16)

        @block.scalar
        def _(scalar):
            for uc in range(4):
                scalar.wait_ge(SEM["du_a" if uc % 2 == 0 else "du_b"], 16 * (uc // 2 + 1))
                scalar.activation(out=junk[:, :], in_=wstage[uc % 2], func=AF.Square,
                                  accum_out=su2[:, uc:uc + 1]).then_inc(SEM["d_sq"], 1)
            scalar.activation(out=su_t[:, :], in_=su2[:, :], func=AF.Ln, bias=eps12[:, :], scale=1.0)
            scalar.drain()
            scalar.activation(out=su_t[:, :], in_=su_t[:, :], func=AF.Exp, bias=zerob[:, :],
                              scale=-0.5).then_inc(SEM["d_sq"], 1)

        @block.vector
        def _(vector):
            def stt_group(uc_):
                for tb in range(4):
                    hg = uc_ * 4 + tb
                    pp = p4 if hg % 2 == 0 else p5
                    vector.wait_ge(SEM["d_h"], hg + 1)
                    vector.scalar_tensor_tensor(
                        wh_r[:, uc_ * 2048 + tb * 512: uc_ * 2048 + tb * 512 + 512],
                        pp[:, :], su_t[:, uc_:uc_ + 1],
                        w_fmr[:, uc_ * 2048 + tb * 512: uc_ * 2048 + tb * 512 + 512],
                        op0=ALU.mult, op1=ALU.mult).then_inc(SEM["d_wh"], 1)

            vector.wait_ge(SEM["d_sq"], 5)
            for uc in range(4):
                for dcx in range(DC):
                    gi = uc * DC + dcx
                    vector.wait_ge(SEM["dut_a" if gi % 2 == 0 else "dut_b"], 16 * (gi // 2 + 1))
                    if gi >= 4 and dcx == 0:
                        vector.wait_ge(SEM["d_h"], 4 * (uc - 1))
                    vector.tensor_copy(ut_b[uc % 2][:, dcx * 128:(dcx + 1) * 128],
                                       utstg[gi % 2]).then_inc(SEM["d_utc"], 1)
                if uc >= 1:
                    stt_group(uc - 1)
            stt_group(3)

        @block.tensor
        def _(tensor):
            tensor.wait_ge(SEM["d_sq"], 5)
            tensor.wait_ge(SEM["c_wf"], TC)
            hg = 0
            for uc in range(4):
                tensor.wait_ge(SEM["d_utc"], DC * (uc + 1))
                for tb in range(4):
                    pp = p4 if hg % 2 == 0 else p5
                    if hg >= 2:
                        tensor.wait_ge(SEM["d_wh"], hg - 1)
                    for dcx in range(DC):
                        ins = tensor.matmul(
                            pp[:, :], ut_b[uc % 2][:, dcx * 128:(dcx + 1) * 128],
                            xf_fm[:, dcx * 2048 + tb * 512: dcx * 2048 + tb * 512 + 512],
                            start=(dcx == 0), stop=(dcx == DC - 1), **MK)
                        if dcx == DC - 1:
                            ins.then_inc(SEM["d_h"], 1)
                    hg += 1

        if p4stage < 4:
            raise StopPhase()
        @block.sync
        def _(sync):
            sync.wait_ge(SEM["d_sq"], 5)
            for hs in range(8):
                for k in range(4):
                    gi = hs * 4 + k
                    if gi >= 2:
                        sync.wait_ge(SEM["e_cast"], gi - 1)
                    if k < 2:
                        src = w1t_d[_ts(k, 512), hs * 512:(hs + 1) * 512]                             .rearrange("(c p) h -> p c h", p=128)
                    else:
                        src = w2t_d[_ts(hs * 2 + (k - 2), 256), :]                             .rearrange("(c p) h -> p c h", p=128)
                    sync.dma_start(out=wstgE[gi % 2].rearrange("p (c h) -> p c h", h=512 if k < 2 else 1024),
                                   in_=src).then_inc(SEM["ee_a" if gi % 2 == 0 else "ee_b"], 16)

        @block.vector
        def _(vector):
            def fold_group(hs_):
                for k in range(16):
                    g2 = hs_ * 16 + k
                    tb, tc2 = k // 4, k % 4
                    tcg = tb * 4 + tc2
                    ob = OA if g2 % 2 == 0 else OB
                    vector.wait_ge(SEM["e_m2"], g2 + 1)
                    for dh in range(2):
                        dst = static[:, tcg * 1024 + dh * 512: tcg * 1024 + dh * 512 + 512]
                        if hs_ == 0:
                            ins = vector.tensor_copy(dst, ob[dh][:, :])
                        else:
                            ins = vector.tensor_tensor(dst, dst, ob[dh][:, :], op=ALU.add)
                        if dh == 1:
                            ins.then_inc(SEM["e_f"], 1)

            for hs in range(8):
                par = hs % 2
                for k in range(4):
                    gi = hs * 4 + k
                    vector.wait_ge(SEM["ee_a" if gi % 2 == 0 else "ee_b"], 16 * (gi // 2 + 1))
                    if hs >= 2 and k == 0:
                        vector.wait_ge(SEM["e_m2"], 16 * (hs - 1))
                    if k < 2:
                        vector.tensor_copy(w1sb2[par][:, k * 2048:(k + 1) * 2048],
                                           wstgE[gi % 2]).then_inc(SEM["e_cast"], 1)
                    else:
                        vector.tensor_copy(w2sb2[par][:, (k - 2) * 2048:(k - 1) * 2048],
                                           wstgE[gi % 2]).then_inc(SEM["e_cast"], 1)
                if hs >= 1:
                    fold_group(hs - 1)
            fold_group(7)

        @block.tensor
        def _(tensor):
            tensor.wait_ge(SEM["d_wh"], 16)
            for hs in range(8):
                par = hs % 2
                tensor.wait_ge(SEM["e_cast"], 4 * (hs + 1))
                for tb in range(4):
                    gE = hs * 4 + tb
                    for hc in range(4):
                        pp = p4 if hc % 2 == 0 else p5
                        if gE * 4 + hc - 1 > 0:
                            tensor.wait_ge(SEM["e_g"], gE * 4 + hc - 1)
                        for dcx in range(DC):
                            ins = tensor.matmul(
                                pp[:, :], w1sb2[par][:, dcx * 512 + hc * 128: dcx * 512 + (hc + 1) * 128],
                                xf_fm[:, dcx * 2048 + tb * 512: dcx * 2048 + tb * 512 + 512],
                                start=(dcx == 0), stop=(dcx == DC - 1), **MK)
                            if dcx == DC - 1:
                                ins.then_inc(SEM["e_m1"], 1)
                    tensor.wait_ge(SEM["e_g"], 4 * (gE + 1))
                    for tc2 in range(4):
                        g2 = gE * 4 + tc2
                        ob = OA if g2 % 2 == 0 else OB
                        if g2 >= 2:
                            tensor.wait_ge(SEM["e_f"], g2 - 1)
                        for hc in range(4):
                            for dh in range(2):
                                ins = tensor.matmul(
                                    ob[dh][:, :],
                                    h1sb[:, hc * 512 + tc2 * 128: hc * 512 + (tc2 + 1) * 128],
                                    w2sb2[par][:, hc * 1024 + dh * 512: hc * 1024 + dh * 512 + 512],
                                    start=(hc == 0), stop=(hc == 3), **MK)
                                if hc == 3 and dh == 1:
                                    ins.then_inc(SEM["e_m2"], 1)

        @block.scalar
        def _(scalar):
            for gE in range(32):
                for hc in range(4):
                    pp = p4 if hc % 2 == 0 else p5
                    scalar.wait_ge(SEM["e_m1"], gE * 4 + hc + 1)
                    scalar.activation(out=h1sb[:, hc * 512:(hc + 1) * 512], in_=pp[:, :],
                                      func=AF.Gelu, bias=zerob[:, :], scale=1.0
                                      ).then_inc(SEM["e_g"], 1)

        if debug:
            @block.gpsimd
            def _(gpsimd):
                gpsimd.wait_ge(SEM["e_f"], 128)
                for tc in range(TC):
                    gpsimd.dma_start(out=sdbg_d[tc, :, :],
                                     in_=static[:, tc * 1024:(tc + 1) * 1024]).then_inc(SEM["dbg"], 16)

        if p4stage < 5:
            raise StopPhase()
        fsum_at = {}
        var_at = {}
        y_at = {}
        fcnt = 0
        for _tc in range(TC):
            fcnt += 1; fsum_at[_tc] = fcnt
            if _tc >= 1:
                fcnt += 1; y_at[_tc - 1] = fcnt
            fcnt += 1; var_at[_tc] = fcnt
        fcnt += 1; y_at[TC - 1] = fcnt

        @block.sync
        def _(sync):
            sync.wait_ge(SEM["e_cast"], 32)
            for uc in range(4):
                if uc >= 2:
                    sync.wait_ge(SEM["f_sq"], uc - 1)
                sync.dma_start(out=wstage[uc % 2], in_=v_d[_ts(uc), :]
                               ).then_inc(SEM["fv_a" if uc % 2 == 0 else "fv_b"], 16)
            for uc in range(4):
                sync.wait_ge(SEM["f_sq"], min(uc + 2, 5))
                if uc >= 2:
                    sync.wait_ge(SEM["f_vpp"], uc - 1)
                sync.dma_start(out=wstage[uc % 2], in_=v_d[_ts(uc), :]
                               ).then_inc(SEM["fv_a" if uc % 2 == 0 else "fv_b"], 16)
            sync.wait_ge(SEM["f_vpp"], 4)
            for tc in range(TC):
                if tc >= 2:
                    sync.wait_ge(SEM["f_ve"], y_at[tc - 2])
                sync.dma_start(out=xres[tc % 2], in_=x2buf_d[tc, :, :]
                               ).then_inc(SEM["fx_a" if tc % 2 == 0 else "fx_b"], 16)

        @block.scalar
        def _(scalar):
            for uc in range(4):
                scalar.wait_ge(SEM["fv_a" if uc % 2 == 0 else "fv_b"], 16 * (uc // 2 + 1))
                scalar.activation(out=junk[:, :], in_=wstage[uc % 2], func=AF.Square,
                                  accum_out=sv2[:, uc:uc + 1]).then_inc(SEM["f_sq"], 1)
            scalar.activation(out=sv_t[:, :], in_=sv2[:, :], func=AF.Ln, bias=eps12[:, :], scale=1.0)
            scalar.drain()
            scalar.activation(out=sv_t[:, :], in_=sv_t[:, :], func=AF.Exp, bias=zerob[:, :],
                              scale=-0.5).then_inc(SEM["f_sq"], 1)
            for tc in range(TC):
                scalar.wait_ge(SEM["f_ve"], fsum_at[tc])
                scalar.activation(out=junk[:, :], in_=(fsb if tc % 2 == 0 else fsbB),
                                  func=AF.Square, accum_out=sumsq[:, :]).then_inc(SEM["f_sc"], 1)
                scalar.wait_ge(SEM["f_ve"], var_at[tc])
                scalar.activation(out=rstd[:, :], in_=var[:, :], func=AF.Ln, bias=eps5[:, :], scale=1.0)
                scalar.drain()
                scalar.activation(out=rstd[:, :], in_=rstd[:, :], func=AF.Exp, bias=zerob[:, :],
                                  scale=-0.5).then_inc(SEM["f_sc"], 1)

        @block.vector
        def _(vector):
            vector.wait_ge(SEM["f_sq"], 5)
            vector.wait_ge(SEM["e_f"], 128)
            for uc in range(4):
                vector.wait_ge(SEM["fv_a" if uc % 2 == 0 else "fv_b"], 16 * ((4 + uc) // 2 + 1))
                vector.scalar_tensor_tensor(vpp[:, uc * 1024:(uc + 1) * 1024],
                                            wstage[uc % 2], sv_t[:, uc:uc + 1], gbc[:, :],
                                            op0=ALU.mult, op1=ALU.mult).then_inc(SEM["f_vpp"], 1)
            def f_apply_y(tc):
                fb = fsb if tc % 2 == 0 else fsbB
                vector.wait_ge(SEM["f_sc"], 2 * tc + 2)
                vector.tensor_scalar(fb, fb, mean[:, :], rstd[:, :],
                                     op0=ALU.subtract, op1=ALU.mult)
                vector.drain()
                vector.wait_ge(SEM["fx_a" if tc % 2 == 0 else "fx_b"], 16 * (tc // 2 + 1))
                if tc >= 2:
                    vector.wait_ge(SEM["fs_a" if tc % 2 == 0 else "fs_b"], 16 * (tc // 2))
                vector.tensor_tensor(ysb[tc % 2], fb, xres[tc % 2], op=ALU.add
                                     ).then_inc(SEM["f_ve"], 1)

            for tc in range(TC):
                ob = OA if tc % 2 == 0 else OB
                fb = fsb if tc % 2 == 0 else fsbB
                vector.wait_ge(SEM["f_dyn"], tc + 1)
                for dh in range(2):
                    ins = vector.tensor_tensor(
                        fb[:, dh * 512:dh * 512 + 512],
                        static[:, tc * 1024 + dh * 512: tc * 1024 + dh * 512 + 512],
                        ob[dh][:, :], op=ALU.add)
                    if dh == 1:
                        ins.then_inc(SEM["f_ve"], 1)
                if tc >= 1:
                    f_apply_y(tc - 1)
                vector.drain()
                vector.reduce_sum(sumx[:, :], fb, axis=AX)
                vector.drain()
                vector.wait_ge(SEM["f_sc"], 2 * tc + 1)
                stats_seq(vector).then_inc(SEM["f_ve"], 1)
            f_apply_y(TC - 1)

        @block.tensor
        def _(tensor):
            for tc in range(TC):
                ob = OA if tc % 2 == 0 else OB
                if tc == 0:
                    tensor.wait_ge(SEM["f_vpp"], 4)
                if tc >= 2:
                    tensor.wait_ge(SEM["f_ve"], fsum_at[tc - 2])
                for uc in range(4):
                    for dh in range(2):
                        ins = tensor.matmul(
                            ob[dh][:, :], wh_r[:, uc * 2048 + tc * 128: uc * 2048 + (tc + 1) * 128],
                            vpp[:, uc * 1024 + dh * 512: uc * 1024 + dh * 512 + 512],
                            start=(uc == 0), stop=(uc == 3), **MK)
                        if uc == 3 and dh == 1:
                            ins.then_inc(SEM["f_dyn"], 1)

        @block.gpsimd
        def _(gpsimd):
            for tc in range(TC):
                gpsimd.wait_ge(SEM["f_ve"], y_at[tc])
                gpsimd.dma_start(out=y_d[_ts(tc), :], in_=ysb[tc % 2]
                                 ).then_inc(SEM["fs_a" if tc % 2 == 0 else "fs_b"], 16)
            gpsimd.wait_ge(SEM["fs_a"], 16 * TC // 2)
            gpsimd.wait_ge(SEM["fs_b"], 16 * TC // 2)

        p4block.__exit__(None, None, None)
    except StopPhase:
        p4block.__exit__(None, None, None)
    return nc


SCALE_EXP = 1.0 / 32.0


@functools.lru_cache(maxsize=2)
def _get_nc(debug=False):
    return build_nc(debug=debug)


def build_upto(n, debug=True):
    return build_nc(debug=debug, upto=n)


def _prep_inputs(inputs, core):
    b = core // 4
    j0 = (core % 4) * JS
    x = np.asarray(inputs["L0_attn_x"], dtype=np.float32)
    c = np.ascontiguousarray
    return {
        "x": c(x[b]),
        "xj": c(x[b, j0:j0 + JS]),
        "wq": c(np.asarray(inputs["L0_attn_W_query"], np.float32)),
        "wk": c(np.asarray(inputs["L0_attn_W_key"], np.float32)),
        "wv": c(np.asarray(inputs["L0_attn_W_value"], np.float32)),
        "w1t": c(np.asarray(inputs["L0_ffn_W1"], np.float32).T),
        "w2t": c(np.asarray(inputs["L0_ffn_W2"], np.float32).T),
        "rwt": c(np.asarray(inputs["L0_ffn_router_W"], np.float32).T),
        "ut": c(np.asarray(inputs["L0_ffn_raw_U"], np.float32).T),
        "u": c(np.asarray(inputs["L0_ffn_raw_U"], np.float32)),
        "v": c(np.asarray(inputs["L0_ffn_raw_V"], np.float32)),
        "lng": c(np.asarray(inputs["L0_ffn_ln_g"], np.float32).reshape(DC, 128, 1)),
        "lnb": c(np.asarray(inputs["L0_ffn_ln_b"], np.float32).reshape(DC, 128, 1)),
        "rb": c(np.asarray(inputs["L0_ffn_router_b"], np.float32).reshape(1, NB)),
        "gamma": c(np.asarray(inputs["L0_ffn_gamma"], np.float32).reshape(1, D)),
        "ident": np.eye(128, dtype=np.float32),
    }


def kernel(**inputs):
    nc = _get_nc(False)
    in_maps = [_prep_inputs(inputs, c) for c in range(8)]
    res = run_bass_kernel_spmd(nc, in_maps, core_ids=list(range(8)))
    out = np.empty((2, SEQ, D), dtype=np.float32)
    for c in range(8):
        b = c // 4
        j0 = (c % 4) * JS
        out[b, j0:j0 + JS] = res.results[c]["y"]
    return out


# version 12
# speedup vs baseline: 1.2733x; 1.0217x over previous
import functools
import numpy as np

import concourse.bass as bass
import concourse.mybir as mybir
from concourse.bass_utils import run_bass_kernel_spmd

F32 = mybir.dt.float32
BF16 = mybir.dt.bfloat16
AF = mybir.ActivationFunctionType
ALU = mybir.AluOpType
AX = mybir.AxisListType.X

SEQ, D, HID, NB = 8192, 1024, 4096, 512
JS = 2048
NSUP, SUP = 8, 1024
B_SHIFT = 30.0
DC = D // 128
TC = JS // 128
XC = SEQ // 128
MK = dict(skip_group_check=True)


class StopPhase(Exception):
    pass


def _ts(i, n=128):
    return slice(i * n, (i + 1) * n)


def build_nc(debug=False, upto=5, p4stage=9):
    nc = bass.Bass("TRN2", target_bir_lowering=False)

    x_d = nc.dram_tensor("x", [SEQ, D], F32, kind="ExternalInput").ap()
    xj_d = nc.dram_tensor("xj", [JS, D], F32, kind="ExternalInput").ap()
    wq_d = nc.dram_tensor("wq", [D, D], F32, kind="ExternalInput").ap()
    wk_d = nc.dram_tensor("wk", [D, D], F32, kind="ExternalInput").ap()
    wv_d = nc.dram_tensor("wv", [D, D], F32, kind="ExternalInput").ap()
    w1t_d = nc.dram_tensor("w1t", [D, HID], F32, kind="ExternalInput").ap()
    w2t_d = nc.dram_tensor("w2t", [HID, D], F32, kind="ExternalInput").ap()
    rwt_d = nc.dram_tensor("rwt", [D, NB], F32, kind="ExternalInput").ap()
    ut_d = nc.dram_tensor("ut", [D, NB], F32, kind="ExternalInput").ap()
    u_d = nc.dram_tensor("u", [NB, D], F32, kind="ExternalInput").ap()
    v_d = nc.dram_tensor("v", [NB, D], F32, kind="ExternalInput").ap()
    lng_d = nc.dram_tensor("lng", [DC, 128, 1], F32, kind="ExternalInput").ap()
    lnb_d = nc.dram_tensor("lnb", [DC, 128, 1], F32, kind="ExternalInput").ap()
    rb_d = nc.dram_tensor("rb", [1, NB], F32, kind="ExternalInput").ap()
    gam_d = nc.dram_tensor("gamma", [1, D], F32, kind="ExternalInput").ap()
    id_d = nc.dram_tensor("ident", [128, 128], F32, kind="ExternalInput").ap()
    y_d = nc.dram_tensor("y", [JS, D], F32, kind="ExternalOutput").ap()

    ikind = "ExternalOutput" if debug else "Internal"
    qbuf_d = nc.dram_tensor("qbuf", [NSUP, DC, 128, SUP], BF16, kind=ikind).ap()
    vbuf_d = nc.dram_tensor("vbuf", [XC, 128, D], BF16, kind=ikind).ap()
    x2buf_d = nc.dram_tensor("x2buf", [TC, 128, D], F32, kind=ikind).ap()
    if debug:
        odbg_d = nc.dram_tensor("odbg", [TC, 128, D], F32, kind="ExternalOutput").ap()
        ldbg_d = nc.dram_tensor("ldbg", [128, TC], F32, kind="ExternalOutput").ap()
        kdbg_d = nc.dram_tensor("kdbg", [DC, 128, JS], BF16, kind="ExternalOutput").ap()
        adbg_d = nc.dram_tensor("adbg", [TC, 128, NB], F32, kind="ExternalOutput").ap()
        wdbg_d = nc.dram_tensor("wdbg", [4, 128, JS], BF16, kind="ExternalOutput").ap()
        sdbg_d = nc.dram_tensor("sdbg", [TC, 128, D], F32, kind="ExternalOutput").ap()
        xsdbg_d = nc.dram_tensor("xsdbg", [128, 8192], BF16, kind="ExternalOutput").ap()
        stdbg_d = nc.dram_tensor("stdbg", [128, 8], F32, kind="ExternalOutput").ap()
        xfdbg_d = nc.dram_tensor("xfdbg", [128, 16384], BF16, kind="ExternalOutput").ap()
        zfdbg_d = nc.dram_tensor("zfdbg", [128, 16384], BF16, kind="ExternalOutput").ap()
        wqdbg_d = nc.dram_tensor("wqdbg", [128, 8192], BF16, kind="ExternalOutput").ap()

    BB = nc.alloc_sbuf_tensor("BB", [128, 49152], BF16).ap()
    FF = nc.alloc_sbuf_tensor("FF", [128, 16384], F32).ap()
    stg = nc.alloc_sbuf_tensor("stg", [128, 6144], F32).ap()
    junk = nc.alloc_sbuf_tensor("junkb", [128, 1024], BF16).ap()

    A = nc.alloc_sbuf_tensor
    ident_b = A("ident_b", [128, 128], BF16).ap()
    ones_b = A("ones_b", [128, 1], BF16).ap()
    ones1f = A("ones1f", [1, 128], F32).ap()
    negB = A("negB", [128, 1], F32).ap()
    zerob = A("zerob", [128, 1], F32).ap()
    eps5 = A("eps5", [128, 1], F32).ap()
    onesf = A("onesf", [128, 1], F32).ap()
    eps12 = A("eps12", [128, 1], F32).ap()
    g_t = A("g_t", [128, DC], F32).ap()
    b_t = A("b_t", [128, DC], F32).ap()
    su2 = A("su2", [128, 4], F32).ap()
    su_t = A("su_t", [128, 4], F32).ap()
    sv2 = A("sv2", [128, 4], F32).ap()
    sv_t = A("sv_t", [128, 4], F32).ap()
    SSt = A("SSt", [128, 16], F32).ap()
    SS2 = A("SS2", [128, 16], F32).ap()
    fac = A("fac", [128, 16], F32).ap()
    l_sb = A("l_sb", [128, 16], F32).ap()
    top8 = A("top8", [128, 8], F32).ap()
    rb_row = A("rb_row", [1, NB], F32).ap()
    brow = A("brow", [1, NB], F32).ap()
    grow = A("grow", [1, D], F32).ap()
    sumx = A("sumx", [128, 1], F32).ap()
    sumsq = A("sumsq", [128, 1], F32).ap()
    mean = A("mean", [128, 1], F32).ap()
    var = A("var", [128, 1], F32).ap()
    rstd = A("rstd", [128, 1], F32).ap()
    msq = A("msq", [128, 1], F32).ap()
    lt = A("lt", [128, 1], F32).ap()
    lt1 = A("lt1", [128, 1], F32).ap()
    vtmp = A("vtmp", [128, 1], F32).ap()
    linv = A("linv", [128, 16], F32).ap()
    sumx2 = A("sumx2", [128, 2], F32).ap()
    sumsq2 = A("sumsq2", [128, 2], F32).ap()
    mean2 = A("mean2", [128, 2], F32).ap()
    vtmp2 = A("vtmp2", [128, 2], F32).ap()
    msq2 = A("msq2", [128, 2], F32).ap()
    var2 = A("var2", [128, 2], F32).ap()
    rstd2 = A("rstd2", [128, 2], F32).ap()
    stdbg = A("stdbg_s", [128, 8], F32).ap()

    wqb = BB[:, 0:8192]
    wkb = BB[:, 8192:16384]
    wvb = BB[:, 16384:24576]
    xsup = [BB[:, 24576:32768], BB[:, 32768:40960]]
    xln = [BB[:, 40960:41984], BB[:, 41984:43008]]
    xln4 = [BB[:, 40960:41984], BB[:, 41984:43008],
            BB[:, 43008:44032], BB[:, 44032:45056]]
    qsb = [BB[:, 45056:46080], BB[:, 46080:47104]]
    xjf = BB[:, 24576:40960]
    kq = [BB[:, 16384:24576], BB[:, 40960:49152]]

    def kfm(e):
        return kq[e // 4][:, (e % 4) * 2048:(e % 4) * 2048 + 2048]

    qstream = BB[:, 0:8192]
    vstream = BB[:, 8192:16384]
    ppair = [BB[:, 24576:25088], BB[:, 25088:25600]]
    xf_fm = BB[:, 0:16384]
    z_fm = BB[:, 16384:32768]
    xf_tm = BB[:, 40960:41984]
    z_tm = BB[:, 41984:43008]
    xf_tm2 = BB[:, 43008:44032]
    z_tm2 = BB[:, 44032:45056]
    rw_b = BB[:, 43008:47104]
    w_tm = BB[:, 32768:40960]
    w_fmr = BB[:, 40960:49152]
    wh_r = BB[:, 16384:24576]
    ut_b = [BB[:, 24576:25600], BB[:, 25600:26624]]
    w1sb2 = [BB[:, 24576:28672], BB[:, 28672:32768]]
    w2sb2 = [BB[:, 32768:36864], BB[:, 36864:40960]]
    h1sb = BB[:, 43008:47104]
    vpp = BB[:, 0:4096]

    O_sb = FF[:, 0:16384]
    wmask = FF[:, 0:8192]
    mge = FF[:, 8192:8704]
    static = FF[:, 0:16384]

    xstg = [stg[:, 0:1024], stg[:, 1024:2048]]
    xstg4 = [stg[:, 0:1024], stg[:, 1024:2048], stg[:, 2048:3072], stg[:, 3072:4096]]
    XSEM = ("x_a", "x_b", "x_c", "x_d")
    xres = [stg[:, 0:1024], stg[:, 1024:2048]]
    x2sb = [stg[:, 2048:3072], stg[:, 3072:4096]]
    xf32 = stg[:, 4096:5120]
    xf32b = stg[:, 5120:6144]
    gbc = stg[:, 0:1024]
    wstage = [stg[:, 1024:2048], stg[:, 2048:3072]]
    bbc = stg[:, 3072:3584]
    clipb = [stg[:, 3584:4096], stg[:, 4096:4608]]
    alphab = [stg[:, 4608:5120], stg[:, 5120:5632]]
    utstg = [stg[:, 3584:3712], stg[:, 3712:3840]]
    wstgE = [stg[:, 1024:3072], stg[:, 3072:5120]]
    fsb = stg[:, 4608:5632]
    fsbB = BB.bitcast(F32)[:, 2048:3072]
    ysb = [stg[:, 2048:3072], stg[:, 3072:4096]]

    P = nc.alloc_psum_tensor
    p0 = P("p0", [128, 512], F32).ap()
    p1 = P("p1", [128, 512], F32).ap()
    p2 = P("p2", [128, 512], F32).ap()
    p3 = P("p3", [128, 512], F32).ap()
    p4 = P("p4", [128, 512], F32).ap()
    p5 = P("p5", [128, 512], F32).ap()
    p6 = P("p6", [128, 512], F32).ap()
    p_t = P("p_t", [128, 1024], BF16).ap()
    p_t2 = p4.bitcast(BF16)
    p_t3 = p5.bitcast(BF16)
    p_t4 = p3.bitcast(BF16)

    def ptb(gi):
        return p_t if gi % 2 == 0 else p_t2

    def stats_seq(vector):
        vector.tensor_scalar(mean[:, :], sumx[:, :], 1.0 / D, None, op0=ALU.mult)
        vector.tensor_scalar(vtmp[:, :], sumsq[:, :], 1.0 / D, None, op0=ALU.mult)
        vector.tensor_tensor(msq[:, :], mean[:, :], mean[:, :], op=ALU.mult)
        vector.drain()
        return vector.tensor_tensor(var[:, :], vtmp[:, :], msq[:, :], op=ALU.subtract)

    sem_names = [
        "s0a", "s0b", "s0i", "x_a", "x_b", "x_c", "x_d", "kx_a", "kx_b", "sp_a", "sp_b",
        "qv_a", "qv_b", "x3_a", "x3_b", "sp3_a", "sp3_b", "ra_a", "ra_b",
        "du_a", "du_b", "dut_a", "dut_b", "ee_a", "ee_b", "fv_a", "fv_b",
        "fx_a", "fx_b", "fs_a", "fs_b",
        "s0_ld", "s0_v",
        "s1_x", "s1_sq", "s1_var", "s1_rstd", "s1_app", "s1_tr", "s1_ev",
        "s1_pmm", "s1_pev", "s1_sp",
        "k_x", "k_sq", "k_var", "k_rstd", "k_app", "k_tr", "k_ev", "k_mm", "k_kev",
        "s2_qv", "s2_smm", "s2_sp", "s2_pv", "s2_fold",
        "s3_x", "s3_s", "s3_v", "s3_tr", "s3_sp",
        "a_ld", "a_te", "a_ve",
        "r_mm", "r_ev", "r_sp", "r_tk",
        "c_w", "c_wt", "c_wf",
        "d_ld", "d_sq", "d_utc", "d_h", "d_wh",
        "e_ld", "e_cast", "e_m1", "e_g", "e_m2", "e_f",
        "f_ld", "f_sq", "f_vpp", "f_dyn", "f_ve", "f_sc", "f_x2", "f_st",
        "dbg",
    ]
    SEM = {n: nc.alloc_semaphore(name=n) for n in sem_names}
    lo = min(s.num for s in SEM.values())
    hi = max(s.num for s in SEM.values())
    nc.gpsimd.sem_clear(range(lo, hi + 1))
    nc.all_engine_barrier()

    with nc.Block("p0") as block:
        @block.sync
        def _(sync):
            sync.dma_start(out=stg[:, 5632:5760], in_=id_d[:, :]).then_inc(SEM["s0i"], 16)
            for wi, wd in enumerate((wq_d, wk_d, wv_d)):
                for c in range(DC):
                    gi = wi * DC + c
                    if gi >= 2:
                        sync.wait_ge(SEM["s0_v"], gi - 1)
                    sync.dma_start(out=xstg[gi % 2], in_=wd[_ts(c), :]
                                   ).then_inc(SEM["s0a" if gi % 2 == 0 else "s0b"], 16)

        @block.scalar
        def _(scalar):
            for fn in (AF.Exp, AF.Ln, AF.Gelu, AF.Tanh, AF.Square):
                scalar.activation(out=junk[0:1, 0:1], in_=junk[0:1, 0:1], func=fn,
                                  bias=zerob[0:1, :], scale=1.0)
                scalar.drain()

        @block.vector
        def _(vector):
            vector.memset(ones_b[:, :], 1.0)
            vector.memset(ones1f[:, :], 1.0)
            vector.memset(negB[:, :], -B_SHIFT)
            vector.memset(zerob[:, :], 0.0)
            vector.memset(eps5[:, :], 1e-5)
            vector.memset(onesf[:, :], 1.0)
            vector.memset(eps12[:, :], 1e-12)
            vector.wait_ge(SEM["s0i"], 16)
            vector.tensor_copy(ident_b[:, :], stg[:, 5632:5760])
            for wi in range(3):
                dst = (wqb, wkb, wvb)[wi]
                for c in range(DC):
                    gi = wi * DC + c
                    vector.wait_ge(SEM["s0a" if gi % 2 == 0 else "s0b"], 16 * (gi // 2 + 1))
                    vector.tensor_copy(dst[:, c * 1024:(c + 1) * 1024],
                                       xstg[gi % 2]).then_inc(SEM["s0_v"], 1)

    def emit_ln_sync(sync, n, src_d, sx, sapp):
        sxa, sxb = sx
        for gi in range(n):
            if gi >= 2:
                sync.wait_ge(SEM[sapp], gi - 1)
            sync.dma_start(out=xstg[gi % 2], in_=src_d[_ts(gi), :]
                           ).then_inc(SEM[sxa if gi % 2 == 0 else sxb], 16)

    def emit_ln_vector(vector, n, sx, ssq, svar, srstd, sapp, str_, sev, dst_of):
        sxa, sxb = sx
        for gi in range(n):
            xc = xstg[gi % 2]
            vector.wait_ge(SEM[sxa if gi % 2 == 0 else sxb], 16 * (gi // 2 + 1))
            vector.reduce_sum(sumx[:, :], xc, axis=AX)
            vector.drain()
            vector.wait_ge(SEM[ssq], gi + 1)
            stats_seq(vector).then_inc(SEM[svar], 1)
            vector.wait_ge(SEM[srstd], gi + 1)
            vector.tensor_scalar(xln[gi % 2], xc, mean[:, :], rstd[:, :],
                                 op0=ALU.subtract, op1=ALU.mult).then_inc(SEM[sapp], 1)
            vector.wait_ge(SEM[str_], gi + 1)
            vector.tensor_copy(dst_of(gi), ptb(gi).rearrange("p (d t) -> p d t", d=DC)
                               ).then_inc(SEM[sev], 1)

    def emit_ln_scalar(scalar, n, sx, ssq, svar, srstd):
        sxa, sxb = sx
        for gi in range(n):
            xc = xstg[gi % 2]
            scalar.wait_ge(SEM[sxa if gi % 2 == 0 else sxb], 16 * (gi // 2 + 1))
            scalar.activation(out=junk[:, :], in_=xc, func=AF.Square,
                              accum_out=sumsq[:, :]).then_inc(SEM[ssq], 1)
            scalar.wait_ge(SEM[svar], gi + 1)
            scalar.activation(out=rstd[:, :], in_=var[:, :], func=AF.Ln, bias=eps5[:, :], scale=1.0)
            scalar.drain()
            scalar.activation(out=rstd[:, :], in_=rstd[:, :], func=AF.Exp, bias=zerob[:, :],
                              scale=-0.5).then_inc(SEM[srstd], 1)

    def emit_tr_chunk(tensor, gi, sapp, sev, str_, src4=False):
        tensor.wait_ge(SEM[sapp], gi + 1)
        if gi >= 2:
            tensor.wait_ge(SEM[sev], gi - 1)
        src = xln4[gi % 4] if src4 else xln[gi % 2]
        for dcx in range(DC):
            ins = tensor.transpose(ptb(gi)[:, dcx * 128:(dcx + 1) * 128],
                                   src[:, dcx * 128:(dcx + 1) * 128],
                                   ident_b[:, :])
            if dcx == DC - 1:
                ins.then_inc(SEM[str_], 1)

    if upto < 1:
        return nc
    with nc.Block("p1") as block:
        @block.sync
        def _(sync):
            for gi in range(XC):
                if gi >= 4:
                    sync.wait_ge(SEM["s1_app"], gi - 3)
                sync.dma_start(out=xstg4[gi % 4], in_=x_d[_ts(gi), :]
                               ).then_inc(SEM[XSEM[gi % 4]], 16)

        @block.scalar
        def _(scalar):
            for gi in range(XC):
                xc = xstg4[gi % 4]
                scalar.wait_ge(SEM[XSEM[gi % 4]], 16 * (gi // 4 + 1))
                scalar.activation(out=junk[:, :], in_=xc, func=AF.Square,
                                  accum_out=sumsq[:, :]).then_inc(SEM["s1_sq"], 1)
                scalar.wait_ge(SEM["s1_var"], gi + 1)
                scalar.activation(out=rstd[:, :], in_=var[:, :], func=AF.Ln, bias=eps5[:, :], scale=1.0)
                scalar.drain()
                scalar.activation(out=rstd[:, :], in_=rstd[:, :], func=AF.Exp, bias=zerob[:, :],
                                  scale=-0.5).then_inc(SEM["s1_rstd"], 1)

        @block.tensor
        def _(tensor):
            pg = 0
            for c in range(8):
                emit_tr_chunk(tensor, c, "s1_app", "s1_ev", "s1_tr", src4=True)
            for s in range(NSUP):
                def maybe_trans(k, s_=s):
                    if s_ + 1 < NSUP and k % 2 == 1:
                        c2 = k // 2
                        emit_tr_chunk(tensor, (s_ + 1) * 8 + c2, "s1_app", "s1_ev", "s1_tr",
                                      src4=True)
                tensor.wait_ge(SEM["s1_ev"], (s + 1) * 8)
                for e in range(DC):
                    pp = (p0, p1) if pg % 2 == 0 else (p2, p3)
                    if pg >= 2:
                        tensor.wait_ge(SEM["s1_pev"], pg - 1)
                    for dcx in range(DC):
                        for th in range(2):
                            ins = tensor.matmul(
                                pp[th][:, :],
                                wqb[:, dcx * 1024 + e * 128: dcx * 1024 + (e + 1) * 128],
                                xsup[s % 2][:, dcx * 1024 + th * 512: dcx * 1024 + th * 512 + 512],
                                start=(dcx == 0), stop=(dcx == DC - 1), **MK)
                            if dcx == DC - 1 and th == 1:
                                ins.then_inc(SEM["s1_pmm"], 1)
                    maybe_trans(pg % 16)
                    pg += 1
                for c in range(8):
                    pp = (p0, p1) if pg % 2 == 0 else (p2, p3)
                    if pg >= 2:
                        tensor.wait_ge(SEM["s1_pev"], pg - 1)
                    for dcx in range(DC):
                        for eh in range(2):
                            ins = tensor.matmul(
                                pp[eh][:, :],
                                xsup[s % 2][:, dcx * 1024 + c * 128: dcx * 1024 + (c + 1) * 128],
                                wvb[:, dcx * 1024 + eh * 512: dcx * 1024 + eh * 512 + 512],
                                start=(dcx == 0), stop=(dcx == DC - 1), **MK)
                            if dcx == DC - 1 and eh == 1:
                                ins.then_inc(SEM["s1_pmm"], 1)
                    maybe_trans(pg % 16)
                    pg += 1

        @block.vector
        def _(vector):
            def ln_sa(s, c):
                gi = s * 8 + c
                xc = xstg4[gi % 4]
                vector.wait_ge(SEM[XSEM[gi % 4]], 16 * (gi // 4 + 1))
                vector.reduce_sum(sumx[:, :], xc, axis=AX)
                vector.drain()
                vector.wait_ge(SEM["s1_sq"], gi + 1)
                stats_seq(vector).then_inc(SEM["s1_var"], 1)
                vector.wait_ge(SEM["s1_rstd"], gi + 1)
                if gi >= 4:
                    vector.wait_ge(SEM["s1_tr"], gi - 3)
                vector.tensor_scalar(xln4[gi % 4], xc, mean[:, :], rstd[:, :],
                                     op0=ALU.subtract, op1=ALU.mult).then_inc(SEM["s1_app"], 1)

            def ln_ev(s, c):
                gi = s * 8 + c
                vector.wait_ge(SEM["s1_tr"], gi + 1)
                if s >= 2 and c == 0:
                    vector.wait_ge(SEM["s1_pmm"], 16 * (s - 1))
                dst = xsup[s % 2].rearrange("p (d t) -> p d t", d=DC)[:, :, c * 128:(c + 1) * 128]
                vector.tensor_copy(dst, ptb(gi).rearrange("p (d t) -> p d t", d=DC)
                                   ).then_inc(SEM["s1_ev"], 1)

            for c in range(8):
                ln_sa(0, c)
                ln_ev(0, c)
            for s in range(NSUP):
                if s + 1 < NSUP:
                    ln_sa(s + 1, 0)
                    ln_sa(s + 1, 1)
                for k in range(16):
                    pg = s * 16 + k
                    pp = (p0, p1) if pg % 2 == 0 else (p2, p3)
                    vector.wait_ge(SEM["s1_pmm"], pg + 1)
                    if pg >= 2:
                        vector.wait_ge(SEM["sp_a" if pg % 2 == 0 else "sp_b"], 16 * (pg // 2))
                    vector.tensor_copy(qsb[pg % 2][:, 0:512], pp[0][:, :])
                    vector.tensor_copy(qsb[pg % 2][:, 512:1024], pp[1][:, :]
                                       ).then_inc(SEM["s1_pev"], 1)
                    if s + 1 < NSUP and k % 2 == 1:
                        c2 = k // 2
                        ln_ev(s + 1, c2)
                        if c2 + 2 <= 7:
                            ln_sa(s + 1, c2 + 2)

        @block.gpsimd
        def _(gpsimd):
            if debug:
                gpsimd.wait_ge(SEM["s1_ev"], 8)
                gpsimd.dma_start(out=xsdbg_d[:, :], in_=xsup[0]).then_inc(SEM["dbg"], 16)
                gpsimd.dma_start(out=stdbg_d[:, :], in_=stdbg[:, :]).then_inc(SEM["dbg"], 16)
                gpsimd.dma_start(out=wqdbg_d[:, :], in_=wqb).then_inc(SEM["dbg"], 16)
            for s in range(NSUP):
                for e in range(DC):
                    pg = s * 16 + e
                    gpsimd.wait_ge(SEM["s1_pev"], pg + 1)
                    gpsimd.dma_start(out=qbuf_d[s, e, :, :], in_=qsb[pg % 2]
                                     ).then_inc(SEM["sp_a" if pg % 2 == 0 else "sp_b"], 16)
                for c in range(8):
                    pg = s * 16 + 8 + c
                    gpsimd.wait_ge(SEM["s1_pev"], pg + 1)
                    gpsimd.dma_start(out=vbuf_d[s * 8 + c, :, :], in_=qsb[pg % 2]
                                     ).then_inc(SEM["sp_a" if pg % 2 == 0 else "sp_b"], 16)
            gpsimd.wait_ge(SEM["sp_a"], 16 * NSUP * 8)
            gpsimd.wait_ge(SEM["sp_b"], 16 * NSUP * 8)

    if upto < 2:
        return nc
    with nc.Block("p1k") as block:
        @block.sync
        def _(sync):
            emit_ln_sync(sync, TC, xj_d, ("kx_a", "kx_b"), "k_app")

        @block.scalar
        def _(scalar):
            emit_ln_scalar(scalar, TC, ("kx_a", "kx_b"), "k_sq", "k_var", "k_rstd")

        @block.tensor
        def _(tensor):
            for gi in range(TC):
                emit_tr_chunk(tensor, gi, "k_app", "k_ev", "k_tr")
            kg = 0
            for tb in range(4):
                tensor.wait_ge(SEM["k_ev"], 4 * (tb + 1))
                for e in range(DC):
                    pp = p0 if kg % 2 == 0 else p1
                    if kg >= 2:
                        tensor.wait_ge(SEM["k_kev"], kg - 1)
                    for dcx in range(DC):
                        ins = tensor.matmul(
                            pp[:, :],
                            wkb[:, dcx * 1024 + e * 128: dcx * 1024 + (e + 1) * 128],
                            xjf[:, dcx * 2048 + tb * 512: dcx * 2048 + tb * 512 + 512],
                            start=(dcx == 0), stop=(dcx == DC - 1), **MK)
                        if dcx == DC - 1:
                            ins.then_inc(SEM["k_mm"], 1)
                    kg += 1

        @block.vector
        def _(vector):
            emit_ln_vector(vector, TC, ("kx_a", "kx_b"), "k_sq", "k_var", "k_rstd", "k_app", "k_tr", "k_ev",
                           lambda gi: xjf.rearrange("p (d t) -> p d t", d=DC)[:, :, gi * 128:(gi + 1) * 128])
            for kg in range(DC * 4):
                tb, e = kg // 8, kg % 8
                pp = p0 if kg % 2 == 0 else p1
                vector.wait_ge(SEM["k_mm"], kg + 1)
                vector.tensor_copy(kfm(e)[:, tb * 512:tb * 512 + 512], pp[:, :]
                                   ).then_inc(SEM["k_kev"], 1)

        if debug:
            @block.gpsimd
            def _(gpsimd):
                gpsimd.wait_ge(SEM["k_kev"], DC * 4)
                for e in range(DC):
                    gpsimd.dma_start(out=kdbg_d[e, :, :], in_=kfm(e)).then_inc(SEM["dbg"], 16)
                gpsimd.wait_ge(SEM["dbg"], 16 * DC)

    if upto < 3:
        return nc
    OA = (p0, p1)
    OB = (p2, p3)
    SBK = (p4, p5)

    with nc.Block("p2") as block:
        @block.sync
        def _(sync):
            for s in range(16):
                if s >= 2:
                    sync.wait_ge(SEM["s2_pv"], 16 * (s - 1))
                qdst = qstream[:, (s % 2) * 4096:(s % 2) * 4096 + 4096]
                vdst = vstream[:, (s % 2) * 4096:(s % 2) * 4096 + 4096]
                qs = "qv_a" if s % 2 == 0 else "qv_b"
                sync.dma_start(out=qdst.rearrange("p (e t) -> p e t", e=DC),
                               in_=qbuf_d[s // 2, :, :, (s % 2) * 512:(s % 2) * 512 + 512]
                               .transpose([1, 0, 2])).then_inc(SEM[qs], 16)
                sync.dma_start(out=vdst.rearrange("p (c e) -> p c e", c=4),
                               in_=vbuf_d[s * 4:(s + 1) * 4].transpose([1, 0, 2])
                               ).then_inc(SEM[qs], 16)

        @block.tensor
        def _(tensor):
            def emit_S(g):
                s = g // 16
                jb = (g // 2) % 8
                t = g % 2
                sb = (s % 2) * 4096
                if t == 0 and jb == 0:
                    tensor.wait_ge(SEM["qv_a" if s % 2 == 0 else "qv_b"], 32 * (s // 2 + 1))
                if g >= 2:
                    tensor.wait_ge(SEM["s2_sp"], g - 1)
                for icl in range(2):
                    ic = t * 2 + icl
                    for e in range(DC):
                        ins = tensor.matmul(
                            SBK[g % 2][:, icl * 256:icl * 256 + 256],
                            qstream[:, sb + e * 512 + ic * 128: sb + e * 512 + (ic + 1) * 128],
                            kfm(e)[:, jb * 256: jb * 256 + 256],
                            start=(e == 0), stop=(e == DC - 1), **MK)
                        if icl == 1 and e == DC - 1:
                            ins.then_inc(SEM["s2_smm"], 1)

            def emit_PV(g):
                G = g // 2
                s = g // 16
                t = g % 2
                sb = (s % 2) * 4096
                tensor.wait_ge(SEM["s2_sp"], g + 1)
                if t == 0:
                    tensor.wait_ge(SEM["s2_fold"], G)
                par = G % 2
                for icl in range(2):
                    ic = t * 2 + icl
                    for jsub in range(2):
                        lhsT = ppair[g % 2][:, icl * 256 + jsub * 128: icl * 256 + (jsub + 1) * 128]
                        ob = OA if jsub == 0 else OB
                        for eh in range(2):
                            tensor.matmul(ob[eh][:, :], lhsT,
                                          vstream[:, sb + ic * 1024 + eh * 512: sb + ic * 1024 + eh * 512 + 512],
                                          start=(ic == 0), stop=(ic == 3), **MK)
                        slot = par * 16 + jsub * 8 + ic
                        ins = tensor.matmul(p6[:, slot:slot + 1], lhsT, ones_b[:, :],
                                            start=True, stop=True, **MK)
                        if icl == 1 and jsub == 1:
                            ins.then_inc(SEM["s2_pv"], 1)

            for s in range(16):
                base = s * 16
                emit_S(base)
                for t in range(15):
                    emit_S(base + t + 1)
                    emit_PV(base + t)
                emit_PV(base + 15)

        @block.scalar
        def _(scalar):
            for g in range(256):
                scalar.wait_ge(SEM["s2_smm"], g + 1)
                if g >= 2:
                    scalar.wait_ge(SEM["s2_pv"], g - 1)
                scalar.activation(out=ppair[g % 2][:, :], in_=SBK[g % 2][:, :], func=AF.Exp,
                                  bias=negB[:, :], scale=SCALE_EXP).then_inc(SEM["s2_sp"], 1)

        @block.vector
        def _(vector):
            for G in range(16 * 8):
                s, jb = G // 8, G % 8
                par = G % 2
                vector.wait_ge(SEM["s2_pv"], 2 * (G + 1))
                for jsub in range(2):
                    jc = jb * 2 + jsub
                    ob = OA if jsub == 0 else OB
                    ltx = lt if jsub == 0 else lt1
                    sl = par * 16 + jsub * 8
                    vector.reduce_sum(ltx[:, :], p6[:, sl:sl + 4], axis=AX)
                    for eh in range(2):
                        dst = O_sb[:, jc * 1024 + eh * 512: jc * 1024 + eh * 512 + 512]
                        if s == 0:
                            vector.tensor_copy(dst, ob[eh][:, :])
                        else:
                            vector.tensor_tensor(dst, dst, ob[eh][:, :], op=ALU.add)
                    if s == 0:
                        ins = vector.tensor_copy(l_sb[:, jc:jc + 1], ltx[:, :])
                    else:
                        ins = vector.tensor_tensor(l_sb[:, jc:jc + 1], l_sb[:, jc:jc + 1],
                                                   ltx[:, :], op=ALU.add)
                    if jsub == 1:
                        ins.then_inc(SEM["s2_fold"], 1)

    if upto < 4:
        return nc
    with nc.Block("p3") as block:
        @block.sync
        def _(sync):
            for tc in range(TC):
                if tc >= 2:
                    p_, j_ = (tc - 2) // 2, (tc - 2) % 2
                    sync.wait_ge(SEM["s3_v"], 12 * p_ + 3 + j_)
                sync.dma_start(out=xres[tc % 2], in_=xj_d[_ts(tc), :]
                               ).then_inc(SEM["x3_a" if tc % 2 == 0 else "x3_b"], 16)

        @block.vector
        def _(vector):
            vector.reciprocal(linv[:, :], l_sb[:, :])
            vector.drain()

            def stats2(base):
                vector.tensor_scalar(mean2[:, :], sumx2[:, :], 1.0 / D, None, op0=ALU.mult)
                vector.tensor_scalar(vtmp2[:, :], sumsq2[:, :], 1.0 / D, None, op0=ALU.mult)
                vector.tensor_tensor(msq2[:, :], mean2[:, :], mean2[:, :], op=ALU.mult)
                vector.drain()
                return vector.tensor_tensor(var2[:, :], vtmp2[:, :], msq2[:, :], op=ALU.subtract)

            for p in range(8):
                ta, tb_ = 2 * p, 2 * p + 1
                oca = O_sb[:, ta * 1024:(ta + 1) * 1024]
                ocb = O_sb[:, tb_ * 1024:(tb_ + 1) * 1024]
                vector.tensor_scalar(oca, oca, linv[:, ta:ta + 1], None, op0=ALU.mult)
                vector.tensor_scalar(ocb, ocb, linv[:, tb_:tb_ + 1], None, op0=ALU.mult
                                     ).then_inc(SEM["s3_v"], 1)
                vector.drain()
                vector.reduce_sum(sumx2[:, 0:1], oca, axis=AX)
                vector.reduce_sum(sumx2[:, 1:2], ocb, axis=AX)
                vector.drain()
                vector.wait_ge(SEM["s3_s"], 9 * p + 2)
                stats2(p).then_inc(SEM["s3_v"], 1)
                vector.wait_ge(SEM["s3_s"], 9 * p + 3)
                if debug:
                    vector.wait_ge(SEM["dbg"], 16 * (DC + 4 + 2 * p + 2))
                for j, (oc, tcx) in enumerate(((oca, ta), (ocb, tb_))):
                    vector.tensor_scalar(oc, oc, mean2[:, j:j + 1], rstd2[:, j:j + 1],
                                         op0=ALU.subtract, op1=ALU.mult)
                    vector.drain()
                    vector.wait_ge(SEM["x3_a" if tcx % 2 == 0 else "x3_b"], 16 * (tcx // 2 + 1))
                    if tcx >= 2:
                        vector.wait_ge(SEM["sp3_a" if tcx % 2 == 0 else "sp3_b"], 16 * (tcx // 2))
                    vector.tensor_tensor(x2sb[tcx % 2], oc, xres[tcx % 2], op=ALU.add
                                         ).then_inc(SEM["s3_v"], 1)
                vector.drain()
                vector.reduce_sum(sumx2[:, 0:1], x2sb[0], axis=AX)
                vector.reduce_sum(sumx2[:, 1:2], x2sb[1], axis=AX)
                vector.drain()
                vector.wait_ge(SEM["s3_s"], 9 * p + 5)
                stats2(p).then_inc(SEM["s3_v"], 1)
                vector.wait_ge(SEM["s3_s"], 9 * p + 6)
                vector.tensor_scalar(xf32, x2sb[0], mean2[:, 0:1], rstd2[:, 0:1],
                                     op0=ALU.subtract, op1=ALU.mult)
                vector.tensor_scalar(xf32b, x2sb[1], mean2[:, 1:2], rstd2[:, 1:2],
                                     op0=ALU.subtract, op1=ALU.mult)
                vector.drain()
                vector.tensor_copy(xf_tm, xf32)
                vector.tensor_copy(xf_tm2, xf32b).then_inc(SEM["s3_v"], 1)
                vector.reduce_sum(sumx2[:, 0:1], xf32, axis=AX)
                vector.reduce_sum(sumx2[:, 1:2], xf32b, axis=AX)
                vector.drain()
                vector.wait_ge(SEM["s3_s"], 9 * p + 8)
                stats2(p).then_inc(SEM["s3_v"], 1)
                vector.wait_ge(SEM["s3_s"], 9 * p + 9)
                vector.tensor_scalar(z_tm, xf32, mean2[:, 0:1], rstd2[:, 0:1],
                                     op0=ALU.subtract, op1=ALU.mult)
                vector.tensor_scalar(z_tm2, xf32b, mean2[:, 1:2], rstd2[:, 1:2],
                                     op0=ALU.subtract, op1=ALU.mult).then_inc(SEM["s3_v"], 1)
                pairs = ((xf_fm, ta, p_t), (xf_fm, tb_, p_t3), (z_fm, ta, p_t2), (z_fm, tb_, p_t4))
                for q, (dstb, tcx, bank) in enumerate(pairs):
                    vector.wait_ge(SEM["s3_tr"], 4 * p + q + 1)
                    vector.tensor_copy(
                        dstb.rearrange("p (d t) -> p d t", d=DC)[:, :, tcx * 128:(tcx + 1) * 128],
                        bank.rearrange("p (d t) -> p d t", d=DC)).then_inc(SEM["s3_v"], 1)

        @block.scalar
        def _(scalar):
            for p in range(8):
                scalar.wait_ge(SEM["s3_v"], 12 * p + 1)
                scalar.activation(out=junk[:, :], in_=O_sb[:, 2 * p * 1024:(2 * p + 1) * 1024],
                                  func=AF.Square, accum_out=sumsq2[:, 0:1]).then_inc(SEM["s3_s"], 1)
                scalar.activation(out=junk[:, :], in_=O_sb[:, (2 * p + 1) * 1024:(2 * p + 2) * 1024],
                                  func=AF.Square, accum_out=sumsq2[:, 1:2]).then_inc(SEM["s3_s"], 1)
                scalar.wait_ge(SEM["s3_v"], 12 * p + 2)
                scalar.activation(out=rstd2[:, :], in_=var2[:, :], func=AF.Ln, bias=eps5[:, :], scale=1.0)
                scalar.drain()
                scalar.activation(out=rstd2[:, :], in_=rstd2[:, :], func=AF.Exp, bias=zerob[:, :],
                                  scale=-0.5).then_inc(SEM["s3_s"], 1)
                scalar.wait_ge(SEM["s3_v"], 12 * p + 3)
                scalar.activation(out=junk[:, :], in_=x2sb[0], func=AF.Square,
                                  accum_out=sumsq2[:, 0:1]).then_inc(SEM["s3_s"], 1)
                scalar.wait_ge(SEM["s3_v"], 12 * p + 4)
                scalar.activation(out=junk[:, :], in_=x2sb[1], func=AF.Square,
                                  accum_out=sumsq2[:, 1:2]).then_inc(SEM["s3_s"], 1)
                scalar.wait_ge(SEM["s3_v"], 12 * p + 5)
                scalar.activation(out=rstd2[:, :], in_=var2[:, :], func=AF.Ln, bias=eps5[:, :], scale=1.0)
                scalar.drain()
                scalar.activation(out=rstd2[:, :], in_=rstd2[:, :], func=AF.Exp, bias=zerob[:, :],
                                  scale=-0.5).then_inc(SEM["s3_s"], 1)
                scalar.wait_ge(SEM["s3_v"], 12 * p + 6)
                scalar.activation(out=junk[:, :], in_=xf32, func=AF.Square,
                                  accum_out=sumsq2[:, 0:1]).then_inc(SEM["s3_s"], 1)
                scalar.activation(out=junk[:, :], in_=xf32b, func=AF.Square,
                                  accum_out=sumsq2[:, 1:2]).then_inc(SEM["s3_s"], 1)
                scalar.wait_ge(SEM["s3_v"], 12 * p + 7)
                scalar.activation(out=rstd2[:, :], in_=var2[:, :], func=AF.Ln, bias=eps5[:, :], scale=1.0)
                scalar.drain()
                scalar.activation(out=rstd2[:, :], in_=rstd2[:, :], func=AF.Exp, bias=zerob[:, :],
                                  scale=-0.5).then_inc(SEM["s3_s"], 1)

        @block.tensor
        def _(tensor):
            for p in range(8):
                if p >= 1:
                    tensor.wait_ge(SEM["s3_v"], 12 * (p - 1) + 12)
                tensor.wait_ge(SEM["s3_v"], 12 * p + 6)
                for src, bank in ((xf_tm, p_t), (xf_tm2, p_t3)):
                    for dcx in range(DC):
                        ins = tensor.transpose(bank[:, dcx * 128:(dcx + 1) * 128],
                                               src[:, dcx * 128:(dcx + 1) * 128], ident_b[:, :])
                        if dcx == DC - 1:
                            ins.then_inc(SEM["s3_tr"], 1)
                tensor.wait_ge(SEM["s3_v"], 12 * p + 8)
                for src, bank in ((z_tm, p_t2), (z_tm2, p_t4)):
                    for dcx in range(DC):
                        ins = tensor.transpose(bank[:, dcx * 128:(dcx + 1) * 128],
                                               src[:, dcx * 128:(dcx + 1) * 128], ident_b[:, :])
                        if dcx == DC - 1:
                            ins.then_inc(SEM["s3_tr"], 1)

        @block.gpsimd
        def _(gpsimd):
            if debug:
                gpsimd.dma_start(out=ldbg_d[:, :], in_=l_sb[:, :]).then_inc(SEM["dbg"], 16)
            for tc in range(TC):
                if debug and tc % 2 == 0:
                    gpsimd.wait_ge(SEM["s3_v"], 12 * (tc // 2) + 1)
                    for t2 in (tc, tc + 1):
                        gpsimd.dma_start(out=odbg_d[t2, :, :],
                                         in_=O_sb[:, t2 * 1024:(t2 + 1) * 1024]
                                         ).then_inc(SEM["dbg"], 16)
                gpsimd.wait_ge(SEM["s3_v"], 12 * (tc // 2) + 3 + (tc % 2))
                gpsimd.dma_start(out=x2buf_d[tc, :, :], in_=x2sb[tc % 2]
                                 ).then_inc(SEM["sp3_a" if tc % 2 == 0 else "sp3_b"], 16)
            gpsimd.wait_ge(SEM["sp3_a"], 16 * TC // 2)
            gpsimd.wait_ge(SEM["sp3_b"], 16 * TC // 2)

    if upto < 5:
        return nc
    try:
        p4block = nc.Block("p4")
        p4block.__enter__()
        block = p4block
        @block.sync
        def _(sync):
            for c in range(DC):
                sync.dma_start(out=g_t[:, c:c + 1], in_=lng_d[c, :, :]).then_inc(SEM["a_ld"], 16)
                sync.dma_start(out=b_t[:, c:c + 1], in_=lnb_d[c, :, :]).then_inc(SEM["a_ld"], 16)
            sync.dma_start(out=rb_row[:, :], in_=rb_d[:, :]).then_inc(SEM["a_ld"], 16)
            sync.dma_start(out=grow[:, :], in_=gam_d[:, :]).then_inc(SEM["a_ld"], 16)
            for c in range(DC):
                if c >= 2:
                    sync.wait_ge(SEM["a_ve"], c - 1)
                    sync.wait_ge(SEM["a_te"], c)
                sync.dma_start(out=wstage[c % 2][:, 0:512], in_=rwt_d[_ts(c), :]
                               ).then_inc(SEM["ra_a" if c % 2 == 0 else "ra_b"], 16)

        @block.tensor
        def _(tensor):
            tensor.wait_ge(SEM["a_ld"], 16 * 18)
            tensor.matmul(p4[:, :], ones1f[:, :], grow[:, 0:512], start=True, stop=True, **MK)
            tensor.matmul(p5[:, :], ones1f[:, :], grow[:, 512:1024], start=True, stop=True, **MK
                          ).then_inc(SEM["a_te"], 1)
            for c in range(DC):
                tensor.wait_ge(SEM["ra_a" if c % 2 == 0 else "ra_b"], 16 * (c // 2 + 1))
                tensor.matmul(p6[0:1, :], b_t[:, c:c + 1], wstage[c % 2][:, 0:512],
                              start=(c == 0), stop=(c == DC - 1), **MK).then_inc(SEM["a_te"], 1)
            tensor.wait_ge(SEM["a_ve"], DC + 2)
            tensor.matmul(p4[:, :], ones1f[:, :], brow[:, :], start=True, stop=True, **MK
                          ).then_inc(SEM["a_te"], 1)

        @block.vector
        def _(vector):
            vector.wait_ge(SEM["a_te"], 1)
            vector.tensor_copy(gbc[:, 0:512], p4[:, :])
            vector.tensor_copy(gbc[:, 512:1024], p5[:, :])
            for c in range(DC):
                vector.wait_ge(SEM["ra_a" if c % 2 == 0 else "ra_b"], 16 * (c // 2 + 1))
                vector.wait_ge(SEM["a_te"], c + 2)
                vector.tensor_scalar(rw_b[:, c * 512:(c + 1) * 512], wstage[c % 2][:, 0:512],
                                     g_t[:, c:c + 1], None, op0=ALU.mult).then_inc(SEM["a_ve"], 1)
            vector.wait_ge(SEM["a_te"], DC + 1)
            vector.tensor_copy(brow[:, :], p6[0:1, :]).then_inc(SEM["a_ve"], 1)
            vector.drain()
            vector.tensor_tensor(brow[:, :], brow[:, :], rb_row[:, :], op=ALU.add
                                 ).then_inc(SEM["a_ve"], 1)
            vector.wait_ge(SEM["a_te"], DC + 2)
            vector.tensor_copy(bbc[:, :], p4[:, :]).then_inc(SEM["a_ve"], 1)

        if p4stage < 1:
            raise StopPhase()
        if debug:
            @block.gpsimd
            def _(gpsimd):
                gpsimd.dma_start(out=xfdbg_d[:, :], in_=xf_fm).then_inc(SEM["dbg"], 16)
                gpsimd.dma_start(out=zfdbg_d[:, :], in_=z_fm).then_inc(SEM["dbg"], 16)

        @block.tensor
        def _(tensor):
            for tc in range(TC):
                pp = p4 if tc % 2 == 0 else p5
                if tc == 0:
                    tensor.wait_ge(SEM["a_ve"], DC + 3)
                if tc >= 2:
                    tensor.wait_ge(SEM["r_ev"], 2 * (tc - 2) + 1)
                for dcx in range(DC):
                    ins = tensor.matmul(
                        pp[:, :],
                        z_fm[:, dcx * 2048 + tc * 128: dcx * 2048 + (tc + 1) * 128],
                        rw_b[:, dcx * 512:(dcx + 1) * 512],
                        start=(dcx == 0), stop=(dcx == DC - 1), **MK)
                    if dcx == DC - 1:
                        ins.then_inc(SEM["r_mm"], 1)

        @block.vector
        def _(vector):
            def topk(tc):
                vector.wait_ge(SEM["r_sp"], tc + 1)
                vector.max(out=top8[:, :], in_=alphab[tc % 2])
                vector.drain()
                vector.reduce_sum(SSt[:, tc:tc + 1], top8[:, :], axis=AX)
                vector.tensor_scalar(mge[:, :], alphab[tc % 2], top8[:, 7:8], None, op0=ALU.is_ge)
                vector.drain()
                vector.tensor_tensor(wmask[:, tc * 512:(tc + 1) * 512], alphab[tc % 2],
                                     mge[:, :], op=ALU.mult).then_inc(SEM["r_tk"], 1)

            for tc in range(TC):
                pp = p4 if tc % 2 == 0 else p5
                vector.wait_ge(SEM["r_mm"], tc + 1)
                if tc >= 2:
                    vector.wait_ge(SEM["r_sp"], tc - 1)
                vector.tensor_tensor(clipb[tc % 2], pp[:, :], bbc[:, :], op=ALU.add
                                     ).then_inc(SEM["r_ev"], 1)
                vector.drain()
                vector.tensor_scalar(clipb[tc % 2], clipb[tc % 2], -10.0, 10.0,
                                     op0=ALU.max, op1=ALU.min).then_inc(SEM["r_ev"], 1)
                if tc >= 1:
                    topk(tc - 1)
            topk(TC - 1)

        @block.scalar
        def _(scalar):
            for tc in range(TC):
                scalar.wait_ge(SEM["r_ev"], 2 * tc + 2)
                if tc >= 2:
                    scalar.wait_ge(SEM["r_tk"], tc - 1)
                    if debug:
                        scalar.wait_ge(SEM["dbg"], 16 * (DC + 1 + TC) + 16 * (tc - 1))
                scalar.activation(out=alphab[tc % 2], in_=clipb[tc % 2], func=AF.Exp,
                                  bias=zerob[:, :], scale=1.0)
                scalar.drain()
                scalar.activation(out=alphab[tc % 2], in_=alphab[tc % 2], func=AF.Ln,
                                  bias=onesf[:, :], scale=1.0).then_inc(SEM["r_sp"], 1)

        if debug:
            @block.gpsimd
            def _(gpsimd):
                for tc in range(TC):
                    gpsimd.wait_ge(SEM["r_sp"], tc + 1)
                    gpsimd.dma_start(out=adbg_d[tc, :, :], in_=alphab[tc % 2]).then_inc(SEM["dbg"], 16)

        if p4stage < 2:
            raise StopPhase()
        @block.scalar
        def _(scalar):
            scalar.wait_ge(SEM["r_tk"], TC)
            scalar.activation(out=fac[:, :], in_=SSt[:, :], func=AF.Tanh, bias=zerob[:, :],
                              scale=1.0).then_inc(SEM["r_sp"], 1)

        @block.vector
        def _(vector):
            vector.wait_ge(SEM["r_sp"], TC + 1)
            vector.tensor_scalar(SS2[:, :], SSt[:, :], 1e-6, None, op0=ALU.add)
            vector.drain()
            vector.reciprocal(SS2[:, :], SS2[:, :])
            vector.drain()
            vector.tensor_tensor(fac[:, :], fac[:, :], SS2[:, :], op=ALU.mult)
            vector.drain()
            for tc in range(TC):
                vector.tensor_scalar(w_tm[:, tc * 512:(tc + 1) * 512],
                                     wmask[:, tc * 512:(tc + 1) * 512],
                                     fac[:, tc:tc + 1], None, op0=ALU.mult).then_inc(SEM["c_w"], 1)

        @block.tensor
        def _(tensor):
            for tc in range(TC):
                tensor.wait_ge(SEM["c_w"], tc + 1)
                if tc >= 2:
                    tensor.wait_ge(SEM["c_wf"], tc - 1)
                bank = ptb(tc)
                for us in range(4):
                    ins = tensor.transpose(bank[:, us * 128:(us + 1) * 128],
                                           w_tm[:, tc * 512 + us * 128: tc * 512 + (us + 1) * 128],
                                           ident_b[:, :])
                    if us == 3:
                        ins.then_inc(SEM["c_wt"], 1)

        @block.vector
        def _(vector):
            for tc in range(TC):
                vector.wait_ge(SEM["c_wt"], tc + 1)
                vector.tensor_copy(
                    w_fmr.rearrange("p (u t) -> p u t", u=4)[:, :, tc * 128:(tc + 1) * 128],
                    ptb(tc)[:, 0:512].rearrange("p (u t) -> p u t", u=4)
                    ).then_inc(SEM["c_wf"], 1)

        if debug:
            @block.gpsimd
            def _(gpsimd):
                gpsimd.wait_ge(SEM["c_wf"], TC)
                for us in range(4):
                    gpsimd.dma_start(out=wdbg_d[us, :, :],
                                     in_=w_fmr[:, us * 2048:(us + 1) * 2048]).then_inc(SEM["dbg"], 16)

        if p4stage < 3:
            raise StopPhase()
        @block.sync
        def _(sync):
            sync.wait_ge(SEM["a_ve"], DC)
            sync.wait_ge(SEM["a_te"], DC + 1)
            for uc in range(4):
                if uc >= 2:
                    sync.wait_ge(SEM["d_sq"], uc - 1)
                sync.dma_start(out=wstage[uc % 2], in_=u_d[_ts(uc), :]
                               ).then_inc(SEM["du_a" if uc % 2 == 0 else "du_b"], 16)
            sync.wait_ge(SEM["r_sp"], TC)
            for uc in range(4):
                for dcx in range(DC):
                    gi = uc * DC + dcx
                    if gi >= 2:
                        sync.wait_ge(SEM["d_utc"], gi - 1)
                    sync.dma_start(out=utstg[gi % 2], in_=ut_d[_ts(dcx), _ts(uc)]
                                   ).then_inc(SEM["dut_a" if gi % 2 == 0 else "dut_b"], 16)

        @block.scalar
        def _(scalar):
            for uc in range(4):
                scalar.wait_ge(SEM["du_a" if uc % 2 == 0 else "du_b"], 16 * (uc // 2 + 1))
                scalar.activation(out=junk[:, :], in_=wstage[uc % 2], func=AF.Square,
                                  accum_out=su2[:, uc:uc + 1]).then_inc(SEM["d_sq"], 1)
            scalar.activation(out=su_t[:, :], in_=su2[:, :], func=AF.Ln, bias=eps12[:, :], scale=1.0)
            scalar.drain()
            scalar.activation(out=su_t[:, :], in_=su_t[:, :], func=AF.Exp, bias=zerob[:, :],
                              scale=-0.5).then_inc(SEM["d_sq"], 1)

        @block.vector
        def _(vector):
            def stt_group(uc_):
                for tb in range(4):
                    hg = uc_ * 4 + tb
                    pp = p4 if hg % 2 == 0 else p5
                    vector.wait_ge(SEM["d_h"], hg + 1)
                    vector.scalar_tensor_tensor(
                        wh_r[:, uc_ * 2048 + tb * 512: uc_ * 2048 + tb * 512 + 512],
                        pp[:, :], su_t[:, uc_:uc_ + 1],
                        w_fmr[:, uc_ * 2048 + tb * 512: uc_ * 2048 + tb * 512 + 512],
                        op0=ALU.mult, op1=ALU.mult).then_inc(SEM["d_wh"], 1)

            vector.wait_ge(SEM["d_sq"], 5)
            for uc in range(4):
                for dcx in range(DC):
                    gi = uc * DC + dcx
                    vector.wait_ge(SEM["dut_a" if gi % 2 == 0 else "dut_b"], 16 * (gi // 2 + 1))
                    if gi >= 4 and dcx == 0:
                        vector.wait_ge(SEM["d_h"], 4 * (uc - 1))
                    vector.tensor_copy(ut_b[uc % 2][:, dcx * 128:(dcx + 1) * 128],
                                       utstg[gi % 2]).then_inc(SEM["d_utc"], 1)
                if uc >= 1:
                    stt_group(uc - 1)
            stt_group(3)

        @block.tensor
        def _(tensor):
            tensor.wait_ge(SEM["d_sq"], 5)
            tensor.wait_ge(SEM["c_wf"], TC)
            hg = 0
            for uc in range(4):
                tensor.wait_ge(SEM["d_utc"], DC * (uc + 1))
                for tb in range(4):
                    pp = p4 if hg % 2 == 0 else p5
                    if hg >= 2:
                        tensor.wait_ge(SEM["d_wh"], hg - 1)
                    for dcx in range(DC):
                        ins = tensor.matmul(
                            pp[:, :], ut_b[uc % 2][:, dcx * 128:(dcx + 1) * 128],
                            xf_fm[:, dcx * 2048 + tb * 512: dcx * 2048 + tb * 512 + 512],
                            start=(dcx == 0), stop=(dcx == DC - 1), **MK)
                        if dcx == DC - 1:
                            ins.then_inc(SEM["d_h"], 1)
                    hg += 1

        if p4stage < 4:
            raise StopPhase()
        @block.sync
        def _(sync):
            sync.wait_ge(SEM["d_sq"], 5)
            for hs in range(8):
                for k in range(4):
                    gi = hs * 4 + k
                    if gi >= 2:
                        sync.wait_ge(SEM["e_cast"], gi - 1)
                    if k < 2:
                        src = w1t_d[_ts(k, 512), hs * 512:(hs + 1) * 512]                             .rearrange("(c p) h -> p c h", p=128)
                    else:
                        src = w2t_d[_ts(hs * 2 + (k - 2), 256), :]                             .rearrange("(c p) h -> p c h", p=128)
                    sync.dma_start(out=wstgE[gi % 2].rearrange("p (c h) -> p c h", h=512 if k < 2 else 1024),
                                   in_=src).then_inc(SEM["ee_a" if gi % 2 == 0 else "ee_b"], 16)

        @block.vector
        def _(vector):
            def fold_group(hs_):
                for k in range(16):
                    g2 = hs_ * 16 + k
                    tb, tc2 = k // 4, k % 4
                    tcg = tb * 4 + tc2
                    ob = OA if g2 % 2 == 0 else OB
                    vector.wait_ge(SEM["e_m2"], g2 + 1)
                    for dh in range(2):
                        dst = static[:, tcg * 1024 + dh * 512: tcg * 1024 + dh * 512 + 512]
                        if hs_ == 0:
                            ins = vector.tensor_copy(dst, ob[dh][:, :])
                        else:
                            ins = vector.tensor_tensor(dst, dst, ob[dh][:, :], op=ALU.add)
                        if dh == 1:
                            ins.then_inc(SEM["e_f"], 1)

            for hs in range(8):
                par = hs % 2
                for k in range(4):
                    gi = hs * 4 + k
                    vector.wait_ge(SEM["ee_a" if gi % 2 == 0 else "ee_b"], 16 * (gi // 2 + 1))
                    if hs >= 2 and k == 0:
                        vector.wait_ge(SEM["e_m2"], 16 * (hs - 1))
                    if k < 2:
                        vector.tensor_copy(w1sb2[par][:, k * 2048:(k + 1) * 2048],
                                           wstgE[gi % 2]).then_inc(SEM["e_cast"], 1)
                    else:
                        vector.tensor_copy(w2sb2[par][:, (k - 2) * 2048:(k - 1) * 2048],
                                           wstgE[gi % 2]).then_inc(SEM["e_cast"], 1)
                if hs >= 1:
                    fold_group(hs - 1)
            fold_group(7)

        @block.tensor
        def _(tensor):
            tensor.wait_ge(SEM["d_wh"], 16)
            for hs in range(8):
                par = hs % 2
                tensor.wait_ge(SEM["e_cast"], 4 * (hs + 1))
                for tb in range(4):
                    gE = hs * 4 + tb
                    for hc in range(4):
                        pp = p4 if hc % 2 == 0 else p5
                        if gE * 4 + hc - 1 > 0:
                            tensor.wait_ge(SEM["e_g"], gE * 4 + hc - 1)
                        for dcx in range(DC):
                            ins = tensor.matmul(
                                pp[:, :], w1sb2[par][:, dcx * 512 + hc * 128: dcx * 512 + (hc + 1) * 128],
                                xf_fm[:, dcx * 2048 + tb * 512: dcx * 2048 + tb * 512 + 512],
                                start=(dcx == 0), stop=(dcx == DC - 1), **MK)
                            if dcx == DC - 1:
                                ins.then_inc(SEM["e_m1"], 1)
                    tensor.wait_ge(SEM["e_g"], 4 * (gE + 1))
                    for tc2 in range(4):
                        g2 = gE * 4 + tc2
                        ob = OA if g2 % 2 == 0 else OB
                        if g2 >= 2:
                            tensor.wait_ge(SEM["e_f"], g2 - 1)
                        for hc in range(4):
                            for dh in range(2):
                                ins = tensor.matmul(
                                    ob[dh][:, :],
                                    h1sb[:, hc * 512 + tc2 * 128: hc * 512 + (tc2 + 1) * 128],
                                    w2sb2[par][:, hc * 1024 + dh * 512: hc * 1024 + dh * 512 + 512],
                                    start=(hc == 0), stop=(hc == 3), **MK)
                                if hc == 3 and dh == 1:
                                    ins.then_inc(SEM["e_m2"], 1)

        @block.scalar
        def _(scalar):
            for gE in range(32):
                for hc in range(4):
                    pp = p4 if hc % 2 == 0 else p5
                    scalar.wait_ge(SEM["e_m1"], gE * 4 + hc + 1)
                    scalar.activation(out=h1sb[:, hc * 512:(hc + 1) * 512], in_=pp[:, :],
                                      func=AF.Gelu, bias=zerob[:, :], scale=1.0
                                      ).then_inc(SEM["e_g"], 1)

        if debug:
            @block.gpsimd
            def _(gpsimd):
                gpsimd.wait_ge(SEM["e_f"], 128)
                for tc in range(TC):
                    gpsimd.dma_start(out=sdbg_d[tc, :, :],
                                     in_=static[:, tc * 1024:(tc + 1) * 1024]).then_inc(SEM["dbg"], 16)

        if p4stage < 5:
            raise StopPhase()
        fsum_at = {}
        var_at = {}
        y_at = {}
        fcnt = 0
        for _tc in range(TC):
            fcnt += 1; fsum_at[_tc] = fcnt
            if _tc >= 1:
                fcnt += 1; y_at[_tc - 1] = fcnt
            fcnt += 1; var_at[_tc] = fcnt
        fcnt += 1; y_at[TC - 1] = fcnt

        @block.sync
        def _(sync):
            sync.wait_ge(SEM["e_cast"], 32)
            for uc in range(4):
                if uc >= 2:
                    sync.wait_ge(SEM["f_sq"], uc - 1)
                sync.dma_start(out=wstage[uc % 2], in_=v_d[_ts(uc), :]
                               ).then_inc(SEM["fv_a" if uc % 2 == 0 else "fv_b"], 16)
            for uc in range(4):
                sync.wait_ge(SEM["f_sq"], min(uc + 2, 5))
                if uc >= 2:
                    sync.wait_ge(SEM["f_vpp"], uc - 1)
                sync.dma_start(out=wstage[uc % 2], in_=v_d[_ts(uc), :]
                               ).then_inc(SEM["fv_a" if uc % 2 == 0 else "fv_b"], 16)
            sync.wait_ge(SEM["f_vpp"], 4)
            for tc in range(TC):
                if tc >= 2:
                    sync.wait_ge(SEM["f_ve"], y_at[tc - 2])
                sync.dma_start(out=xres[tc % 2], in_=x2buf_d[tc, :, :]
                               ).then_inc(SEM["fx_a" if tc % 2 == 0 else "fx_b"], 16)

        @block.scalar
        def _(scalar):
            for uc in range(4):
                scalar.wait_ge(SEM["fv_a" if uc % 2 == 0 else "fv_b"], 16 * (uc // 2 + 1))
                scalar.activation(out=junk[:, :], in_=wstage[uc % 2], func=AF.Square,
                                  accum_out=sv2[:, uc:uc + 1]).then_inc(SEM["f_sq"], 1)
            scalar.activation(out=sv_t[:, :], in_=sv2[:, :], func=AF.Ln, bias=eps12[:, :], scale=1.0)
            scalar.drain()
            scalar.activation(out=sv_t[:, :], in_=sv_t[:, :], func=AF.Exp, bias=zerob[:, :],
                              scale=-0.5).then_inc(SEM["f_sq"], 1)
            for tc in range(TC):
                scalar.wait_ge(SEM["f_ve"], fsum_at[tc])
                scalar.activation(out=junk[:, :], in_=(fsb if tc % 2 == 0 else fsbB),
                                  func=AF.Square, accum_out=sumsq[:, :]).then_inc(SEM["f_sc"], 1)
                scalar.wait_ge(SEM["f_ve"], var_at[tc])
                scalar.activation(out=rstd[:, :], in_=var[:, :], func=AF.Ln, bias=eps5[:, :], scale=1.0)
                scalar.drain()
                scalar.activation(out=rstd[:, :], in_=rstd[:, :], func=AF.Exp, bias=zerob[:, :],
                                  scale=-0.5).then_inc(SEM["f_sc"], 1)

        @block.vector
        def _(vector):
            vector.wait_ge(SEM["f_sq"], 5)
            vector.wait_ge(SEM["e_f"], 128)
            for uc in range(4):
                vector.wait_ge(SEM["fv_a" if uc % 2 == 0 else "fv_b"], 16 * ((4 + uc) // 2 + 1))
                vector.scalar_tensor_tensor(vpp[:, uc * 1024:(uc + 1) * 1024],
                                            wstage[uc % 2], sv_t[:, uc:uc + 1], gbc[:, :],
                                            op0=ALU.mult, op1=ALU.mult).then_inc(SEM["f_vpp"], 1)
            def f_apply_y(tc):
                fb = fsb if tc % 2 == 0 else fsbB
                vector.wait_ge(SEM["f_sc"], 2 * tc + 2)
                vector.tensor_scalar(fb, fb, mean[:, :], rstd[:, :],
                                     op0=ALU.subtract, op1=ALU.mult)
                vector.drain()
                vector.wait_ge(SEM["fx_a" if tc % 2 == 0 else "fx_b"], 16 * (tc // 2 + 1))
                if tc >= 2:
                    vector.wait_ge(SEM["fs_a" if tc % 2 == 0 else "fs_b"], 16 * (tc // 2))
                vector.tensor_tensor(ysb[tc % 2], fb, xres[tc % 2], op=ALU.add
                                     ).then_inc(SEM["f_ve"], 1)

            for tc in range(TC):
                ob = OA if tc % 2 == 0 else OB
                fb = fsb if tc % 2 == 0 else fsbB
                vector.wait_ge(SEM["f_dyn"], tc + 1)
                for dh in range(2):
                    ins = vector.tensor_tensor(
                        fb[:, dh * 512:dh * 512 + 512],
                        static[:, tc * 1024 + dh * 512: tc * 1024 + dh * 512 + 512],
                        ob[dh][:, :], op=ALU.add)
                    if dh == 1:
                        ins.then_inc(SEM["f_ve"], 1)
                if tc >= 1:
                    f_apply_y(tc - 1)
                vector.drain()
                vector.reduce_sum(sumx[:, :], fb, axis=AX)
                vector.drain()
                vector.wait_ge(SEM["f_sc"], 2 * tc + 1)
                stats_seq(vector).then_inc(SEM["f_ve"], 1)
            f_apply_y(TC - 1)

        @block.tensor
        def _(tensor):
            for tc in range(TC):
                ob = OA if tc % 2 == 0 else OB
                if tc == 0:
                    tensor.wait_ge(SEM["f_vpp"], 4)
                if tc >= 2:
                    tensor.wait_ge(SEM["f_ve"], fsum_at[tc - 2])
                for uc in range(4):
                    for dh in range(2):
                        ins = tensor.matmul(
                            ob[dh][:, :], wh_r[:, uc * 2048 + tc * 128: uc * 2048 + (tc + 1) * 128],
                            vpp[:, uc * 1024 + dh * 512: uc * 1024 + dh * 512 + 512],
                            start=(uc == 0), stop=(uc == 3), **MK)
                        if uc == 3 and dh == 1:
                            ins.then_inc(SEM["f_dyn"], 1)

        @block.gpsimd
        def _(gpsimd):
            for tc in range(TC):
                gpsimd.wait_ge(SEM["f_ve"], y_at[tc])
                gpsimd.dma_start(out=y_d[_ts(tc), :], in_=ysb[tc % 2]
                                 ).then_inc(SEM["fs_a" if tc % 2 == 0 else "fs_b"], 16)
            gpsimd.wait_ge(SEM["fs_a"], 16 * TC // 2)
            gpsimd.wait_ge(SEM["fs_b"], 16 * TC // 2)

        p4block.__exit__(None, None, None)
    except StopPhase:
        p4block.__exit__(None, None, None)
    return nc


SCALE_EXP = 1.0 / 32.0


@functools.lru_cache(maxsize=2)
def _get_nc(debug=False):
    return build_nc(debug=debug)


def build_upto(n, debug=True):
    return build_nc(debug=debug, upto=n)


def _prep_inputs(inputs, core):
    b = core // 4
    j0 = (core % 4) * JS
    x = np.asarray(inputs["L0_attn_x"], dtype=np.float32)
    c = np.ascontiguousarray
    return {
        "x": c(x[b]),
        "xj": c(x[b, j0:j0 + JS]),
        "wq": c(np.asarray(inputs["L0_attn_W_query"], np.float32)),
        "wk": c(np.asarray(inputs["L0_attn_W_key"], np.float32)),
        "wv": c(np.asarray(inputs["L0_attn_W_value"], np.float32)),
        "w1t": c(np.asarray(inputs["L0_ffn_W1"], np.float32).T),
        "w2t": c(np.asarray(inputs["L0_ffn_W2"], np.float32).T),
        "rwt": c(np.asarray(inputs["L0_ffn_router_W"], np.float32).T),
        "ut": c(np.asarray(inputs["L0_ffn_raw_U"], np.float32).T),
        "u": c(np.asarray(inputs["L0_ffn_raw_U"], np.float32)),
        "v": c(np.asarray(inputs["L0_ffn_raw_V"], np.float32)),
        "lng": c(np.asarray(inputs["L0_ffn_ln_g"], np.float32).reshape(DC, 128, 1)),
        "lnb": c(np.asarray(inputs["L0_ffn_ln_b"], np.float32).reshape(DC, 128, 1)),
        "rb": c(np.asarray(inputs["L0_ffn_router_b"], np.float32).reshape(1, NB)),
        "gamma": c(np.asarray(inputs["L0_ffn_gamma"], np.float32).reshape(1, D)),
        "ident": np.eye(128, dtype=np.float32),
    }


def kernel(**inputs):
    nc = _get_nc(False)
    in_maps = [_prep_inputs(inputs, c) for c in range(8)]
    res = run_bass_kernel_spmd(nc, in_maps, core_ids=list(range(8)))
    out = np.empty((2, SEQ, D), dtype=np.float32)
    for c in range(8):
        b = c // 4
        j0 = (c % 4) * JS
        out[b, j0:j0 + JS] = res.results[c]["y"]
    return out
